# version 11
# baseline (speedup 1.0000x reference)
import math
from collections import defaultdict
from contextlib import ExitStack

import numpy as np
import ml_dtypes

import concourse.bass as bass
import concourse.mybir as mybir
from concourse.bass_utils import run_bass_kernel_spmd

F32 = mybir.dt.float32
BF16 = mybir.dt.bfloat16
AF = mybir.ActivationFunctionType
ALU = mybir.AluOpType
AX = mybir.AxisListType

B, T, D = 4, 4096, 2048
NCH = D // 128
TPC = T // 2
SET0 = [0, 3, 4, 7, 8, 11, 12, 15]
SET1 = [1, 2, 5, 6, 9, 10, 13, 14]
SLOPES = [2.0 ** (-(g + 1) / 2) for g in range(16)]
QSCALE = 1.0 / math.sqrt(128.0)
NEGBIG = -1.0e30
LNEPS = 1e-5
W_EMIN, W_EMAX = -16, 4
A_EMIN, A_EMAX = -16, 7


def _attn_plan():
    colkey = {}
    plans = []
    for j in range(8):
        a0, a1 = SLOPES[SET0[j]], SLOPES[SET1[j]]
        amax, amin = max(a0, a1), min(a0, a1)
        G = 128 if amax > 0.4 else (256 if amax > 0.2 else 512)
        reach = 115.0 / amin + 128.0
        qgs = []
        for q0 in range(0, T, G):
            ref = q0 + G // 2
            kmin = max(0, int(math.floor((ref - reach) / 128.0)) * 128)
            kts = []
            for k0 in range(kmin, q0 + G, 128):
                key = (j, k0 - ref)
                if key not in colkey:
                    colkey[key] = len(colkey)
                delta = k0 - q0
                mi = delta // 128 if delta >= 0 else None
                kts.append((k0, colkey[key], mi))
            qgs.append((q0, ref, kts))
        plans.append((G, qgs))
    return plans, colkey


PLANS, COLKEY = _attn_plan()
NCOL = len(COLKEY)
assert NCOL <= 512, NCOL


def _alibi_table(hh):
    arr = np.zeros((128, NCOL), np.float32)
    heads = SET0 if hh == 0 else SET1
    pp = np.arange(128, dtype=np.float64)
    for (j, off), idx in COLKEY.items():
        arr[:, idx] = (SLOPES[heads[j]] * (off + pp)).astype(np.float32)
    return np.ascontiguousarray(arr)


def _mask_table():
    m = np.zeros((128, 4, 512), np.float32)
    pp = np.arange(128)[:, None]
    ff = np.arange(512)[None, :]
    for d in range(4):
        m[:, d, :] = np.where(ff >= pp + 128 * d, 0.0, NEGBIG)
    return np.ascontiguousarray(m.reshape(128, 4 * 512))


class Plan:
    def __init__(self):
        self.ops = []
        self.cnt = defaultdict(int)

    def op(self, eng, fn, waits=(), inc=None, n=1):
        w = {}
        for ev in waits:
            if ev is None:
                continue
            s, v = ev
            if w.get(s, -1) < v:
                w[s] = v
        evt = None
        if inc is not None:
            self.cnt[inc] += n
            evt = (inc, self.cnt[inc])
        self.ops.append((eng, w, fn, inc, n))
        return evt

    def dma(self, queue, out_ap, in_ap, waits=()):
        sem = {"sync": "qd", "scalar": "ad", "gpsimd": "gd"}[queue]
        fn = lambda e, o=out_ap, i=in_ap: e.dma_start(out=o, in_=i)
        return self.op(queue, fn, waits, inc=sem, n=16)

    def snap(self):
        return [(s, v) for s, v in self.cnt.items()]


def _emit_scale_ladder(p, amax, scl, rscl, t16, waits, emin, emax):
    e1 = p.op("vector", lambda e, o=amax: e.tensor_scalar(
        o, o, float(2.0 ** emin), None, ALU.max), list(waits), inc="ve")
    es = p.op("vector", lambda e, o=scl: e.memset(o, float(2.0 ** (emin - 2))),
              list(waits), inc="ve")
    for ee in range(emin + 1, emax + 1):
        et = p.op("vector", lambda e, o=t16, i=amax, th=float(2.0 ** ee):
                  e.tensor_scalar(o, i, th, 1.0, ALU.is_ge, ALU.add),
                  [e1, es], inc="ve")
        es = p.op("vector", lambda e, o=scl, i=t16: e.tensor_mul(o, o, i),
                  [et], inc="ve")
    er = p.op("vector", lambda e, o=rscl, i=scl: e.reciprocal(o, i),
              [es], inc="ve")
    es = p.op("vector", lambda e, o=scl: e.tensor_scalar(
        o, o, 0.5, None, ALU.mult), [es, er], inc="ve")
    return es, er


def _emit_round(p, src, out_bf, scl, rscl, scr, frees, waits, pool=False):
    W = src.shape[-1]
    nb = W // 32
    q, aq, sgn, tC, s2 = (scr["q"], scr["aq"], scr["sgn"], scr["tC"],
                          scr["s2"])
    win = src.rearrange("p (g w) -> p g w", w=32)
    rb = rscl.unsqueeze(2).broadcast_to([128, nb, 32])
    qv = q.rearrange("p (g w) -> p g w", w=32)
    eq = p.op("vector", lambda e, o=qv, i=win, r=rb: e.tensor_tensor(
        o, i, r, ALU.mult), list(waits) + [frees.get("q")], inc="ve")
    ea = p.op("scalar", lambda e, o=aq, i=q: e.activation(o, i, AF.Abs),
              [eq, frees.get("aq")], inc="ae")
    eg = p.op("scalar", lambda e, o=sgn, i=q: e.activation(o, i, AF.Sign),
              [eq, frees.get("sgn")], inc="ae")
    eA = p.op("vector", lambda e, o=q, i=aq: e.tensor_scalar(
        o, i, 0.25, None, ALU.is_gt), [ea, eg, eq], inc="ve")
    eA = p.op("vector", lambda e, o=q, i=aq: e.scalar_tensor_tensor(
        o, i, 0.75, o, ALU.is_ge, ALU.add), [eA], inc="ve")
    eA = p.op("vector", lambda e, o=q, i=aq: e.scalar_tensor_tensor(
        o, i, 1.25, o, ALU.is_gt, ALU.add), [eA], inc="ve")
    eA = p.op("vector", lambda e, o=q, i=aq: e.scalar_tensor_tensor(
        o, i, 1.75, o, ALU.is_ge, ALU.add), [eA], inc="ve")
    eU = p.op("vector", lambda e, o=tC, i=aq: e.tensor_scalar(
        o, i, 5.0, 2.0, ALU.is_gt, ALU.mult), [ea, frees.get("tC")], inc="ve")
    eU = p.op("vector", lambda e, o=tC, i=aq: e.scalar_tensor_tensor(
        o, i, 2.5, o, ALU.is_gt, ALU.add), [eU], inc="ve")
    eU = p.op("vector", lambda e, o=tC, i=aq: e.scalar_tensor_tensor(
        o, i, 3.5, o, ALU.is_ge, ALU.add), [eU], inc="ve")
    eA = p.op("vector", lambda e, o=q, i=tC: e.scalar_tensor_tensor(
        o, i, 2.0, o, ALU.mult, ALU.add), [eA, eU], inc="ve")
    sb_ = scl.unsqueeze(2).broadcast_to([128, nb, 32])
    s2v = s2.rearrange("p (g w) -> p g w", w=32)
    sgv = sgn.rearrange("p (g w) -> p g w", w=32)
    e2 = p.op("gpsimd", lambda e, o=s2v, i=sgv, s_=sb_: e.tensor_tensor(
        o, i, s_, ALU.mult), [eg, frees.get("s2")], inc="po")
    eo = p.op("gpsimd", lambda e, o=out_bf, i=q, s_=s2: e.tensor_tensor(
        o, i, s_, ALU.mult), [eA, e2, frees.get("out")], inc="po")
    newfrees = {"q": eo, "aq": eA, "sgn": e2, "tC": eA, "s2": eo}
    return eo, newfrees


def _build(debug=False):
    nc = bass.Bass("TRN2", target_bir_lowering=False, debug=False)

    x = nc.dram_tensor("x", [T, D], F32, kind="ExternalInput").ap()
    xr = nc.dram_tensor("xr", [TPC, D], F32, kind="ExternalInput").ap()
    wq = nc.dram_tensor("wq", [D, 1024], F32, kind="ExternalInput").ap()
    wk = nc.dram_tensor("wk", [D, 1024], F32, kind="ExternalInput").ap()
    wv = nc.dram_tensor("wv", [D, 1024], F32, kind="ExternalInput").ap()
    bqr = nc.dram_tensor("bqr", [128, 8], F32, kind="ExternalInput").ap()
    bkr = nc.dram_tensor("bkr", [128, 8], F32, kind="ExternalInput").ap()
    bvr = nc.dram_tensor("bvr", [1, 1024], F32, kind="ExternalInput").ap()
    wo = nc.dram_tensor("wo", [1024, D], F32, kind="ExternalInput").ap()
    bor = nc.dram_tensor("bor", [1, D], F32, kind="ExternalInput").ap()
    wf = nc.dram_tensor("wf", [256, D], F32, kind="ExternalInput").ap()
    bfr = nc.dram_tensor("bfr", [1, D], F32, kind="ExternalInput").ap()
    eyeb = nc.dram_tensor("eyeb", [128, 128], BF16, kind="ExternalInput").ap()
    onesr = nc.dram_tensor("onesr", [1, 128], F32, kind="ExternalInput").ap()
    onesc = nc.dram_tensor("onesc", [128, 1], BF16, kind="ExternalInput").ap()
    alib = nc.dram_tensor("alib", [128, NCOL], F32, kind="ExternalInput").ap()
    maskt = nc.dram_tensor("maskt", [128, 4 * 512], F32, kind="ExternalInput").ap()
    out = nc.dram_tensor("out", [TPC, D], F32, kind="ExternalOutput").ap()

    Qs = nc.dram_tensor("Qs", [8, 128, T], BF16, kind="Internal")
    Ks = nc.dram_tensor("Ks", [8, 128, T], BF16, kind="Internal")
    Vs = nc.dram_tensor("Vs", [8, T, 128], BF16, kind="Internal")
    Ysin = nc.dram_tensor("Ysin", [1024, T], BF16, kind="Internal")
    POs = nc.dram_tensor("POs", [T, D], BF16, kind="Internal")
    POhA = nc.dram_tensor("POhA", [1024, D], BF16, kind="Internal")
    POhB = nc.dram_tensor("POhB", [1024, D], BF16, kind="Internal")
    Wfc = nc.dram_tensor("Wfc", [NCH * 128, 256], BF16, kind="Internal")
    Wfg = nc.dram_tensor("Wfg", [8 * NCH * 128, 256], BF16, kind="Internal",
                         addr_space="Shared")
    Y1s = nc.dram_tensor("Y1s", [TPC, D], F32, kind="Internal")

    p = Plan()
    ctx = ExitStack()

    def SB(scope, name, shape, dt):
        return scope.enter_context(nc.sbuf_tensor(name, shape, dt))

    eyeb_sb = SB(ctx, "eyeb_sb", [128, 128], BF16)
    onesr_sb = SB(ctx, "onesr_sb", [1, 128], F32)
    onesc_sb = SB(ctx, "onesc_sb", [128, 1], BF16)
    alib_sb = SB(ctx, "alib_sb", [128, NCOL], F32)
    bq_sb = SB(ctx, "bq_sb", [128, 8], F32)
    bk_sb = SB(ctx, "bk_sb", [128, 8], F32)
    robuf = SB(ctx, "robuf", [1, D], F32)
    bobf_bc = SB(ctx, "bobf_bc", [128, D], F32)
    amaxA = SB(ctx, "amaxA", [128, NCH, 64], F32)
    sclA = SB(ctx, "sclA", [128, NCH * 64], F32)
    rsclA = SB(ctx, "rsclA", [128, NCH * 64], F32)

    pb = [ctx.enter_context(nc.psum_tensor(f"pb{i}", [128, 512], F32))
          for i in range(8)]

    def pbbf(i):
        return pb[i][:, :].bitcast(BF16)[:, 0:512]

    ev_eye = p.dma("sync", eyeb_sb[:, :], eyeb)
    ev_onesr = p.dma("sync", onesr_sb[:, :], onesr)
    ev_onesc = p.dma("sync", onesc_sb[:, :], onesc)
    ev_alib = p.dma("sync", alib_sb[:, :], alib)
    ev_bq = p.dma("sync", bq_sb[:, :], bqr)
    ev_bk = p.dma("sync", bk_sb[:, :], bkr)
    ev_bv = p.dma("sync", robuf[0:1, 0:1024], bvr)

    s1 = ExitStack()
    wq_sb = SB(s1, "wq_sb", [128, NCH, 1024], BF16)
    wk_sb = SB(s1, "wk_sb", [128, NCH, 1024], BF16)
    wv_sb = SB(s1, "wv_sb", [128, NCH, 1024], BF16)
    xb_sb = [SB(s1, f"xb{i}", [128, 4, D], BF16) for i in range(2)]
    xfm_sb = SB(s1, "xfm", [128, NCH, 512], BF16)
    qkst = [SB(s1, f"qkst{i}", [128, 512], BF16) for i in range(2)]
    vst = [SB(s1, f"vst{i}", [128, 512], BF16) for i in range(2)]
    wt_sb = [SB(s1, f"wt{i}", [128, 512], F32) for i in range(2)]
    amaxW = SB(s1, "amaxW", [128, 8, 16], F32)
    sclW = SB(s1, "sclW", [128, 128], F32)
    rsclW = SB(s1, "rsclW", [128, 128], F32)
    t16W = SB(s1, "t16W", [128, 128], F32)
    wscr = dict(
        q=SB(s1, "w_q", [128, 512], F32)[:, :],
        aq=SB(s1, "w_aq", [128, 512], F32)[:, :],
        sgn=SB(s1, "w_sgn", [128, 512], BF16)[:, :],
        tC=SB(s1, "w_tC", [128, 512], F32)[:, :],
        s2=SB(s1, "w_s2", [128, 512], BF16)[:, :],
    )
    wq_tm = SB(s1, "wq_tm", [128, 512], BF16)
    wftst = [SB(s1, f"wftst{i}", [128, 4, 128], BF16) for i in range(2)]

    wstate = {"wt": [None, None], "wt_used": [None, None], "ladder": None,
              "frees": {}, "evft": None, "evft_store": [None, None],
              "amax_done": [], "stores": []}

    def w_unit_src(u):
        r, cc = u % 2, u // 2
        return wf[r * 128:(r + 1) * 128, cc * 512:(cc + 1) * 512]

    def w_pass1_pair(g):
        for u in (2 * g, 2 * g + 1):
            sl = u % 2
            wstate["wt"][sl] = p.dma(
                "scalar", wt_sb[sl][:, :], w_unit_src(u),
                [wstate["wt"][sl], wstate["wt_used"][sl]])
            wstate["amax_done"].append(p.op(
                "vector",
                lambda e, o=amaxW[:, u, :],
                i=wt_sb[sl][:, :].rearrange("p (g w) -> p g w", w=32):
                e.reduce_max(o, i, axis=AX.X, apply_absolute_value=True),
                [wstate["wt"][sl]], inc="ve"))
            wstate["wt_used"][sl] = wstate["amax_done"][-1]

    def w_pass2b(u):
        sl = u % 2
        wstate["wt"][sl] = p.dma(
            "scalar", wt_sb[sl][:, :], w_unit_src(u),
            [wstate["wt"][sl], wstate["wt_used"][sl]])
        evq, wstate["frees"] = _emit_round(
            p, wt_sb[sl][:, :], wq_tm[:, :],
            sclW[:, u * 16:(u + 1) * 16], rsclW[:, u * 16:(u + 1) * 16],
            wscr, wstate["frees"],
            [wstate["wt"][sl]] + list(wstate["ladder"]))
        wstate["wt_used"][sl] = wstate["frees"]["q"]
        r, cc = u % 2, u // 2
        tb = u % 2
        evt = None
        for kk in range(4):
            evt = p.op(
                "tensor",
                lambda e, o=pbbf(7)[:, kk * 128:(kk + 1) * 128],
                i=wq_tm[:, kk * 128:(kk + 1) * 128],
                idn=eyeb_sb[:, :]: e.transpose(o, i, idn),
                [evq, ev_eye, wstate["evft"]],
                inc="pe" if kk == 3 else None)
        wstate["evft"] = p.op(
            "scalar",
            lambda e, o=wftst[tb][:, :, :].rearrange("p a b -> p (a b)"),
            i=pbbf(7)[:, :]: e.activation(o, i, AF.Copy),
            [evt, wstate["evft_store"][tb]], inc="ae")
        wstate["evft_store"][tb] = p.dma(
            "scalar",
            Wfc.ap().rearrange("(c p) n -> c p n", p=128)
            [cc * 4:(cc + 1) * 4, :, r * 128:(r + 1) * 128]
            .rearrange("c k n -> k c n"),
            wftst[tb][:, :, :],
            [wstate["evft"]])
        wstate["stores"].append(wstate["evft_store"][tb])
        wstate["frees"]["out"] = evt


    ev_wq = p.dma("gpsimd", wq_sb[:, :, :],
                  wq.rearrange("(c p) f -> p c f", p=128))
    ev_wk = p.dma("gpsimd", wk_sb[:, :, :],
                  wk.rearrange("(c p) f -> p c f", p=128))
    ev_wv = p.dma("gpsimd", wv_sb[:, :, :],
                  wv.rearrange("(c p) f -> p c f", p=128))

    ev_xb = [None, None]
    ev_xb_used = [None, None]
    ev_tr_evac = [None, None]
    ev_qk_evac = [None, None]
    ev_v_evac = [None, None]
    ev_qkst_store = [None, None]
    ev_vst_store = [None, None]
    ev_qs_store = {}
    ev_ks_store = {}

    for grp in range(8):
        t0 = grp * 512
        par = grp % 2
        ev_xb[par] = p.dma(
            "gpsimd", xb_sb[par][:, :, :],
            x[t0:t0 + 512, :].rearrange("(n p) f -> p n f", p=128),
            [ev_xb_used[par]])
        evtr = None
        for c in range(NCH):
            tb = c % 2
            for tt in range(4):
                evtr = p.op(
                    "tensor",
                    lambda e, o=pbbf(tb)[:, tt * 128:(tt + 1) * 128],
                    i=xb_sb[par][:, tt, c * 128:(c + 1) * 128],
                    idn=eyeb_sb[:, :]: e.transpose(o, i, idn),
                    [ev_xb[par], ev_eye, ev_tr_evac[tb]],
                    inc="pe" if tt == 3 else None)
            ev_tr_evac[tb] = p.op(
                "scalar", lambda e, o=xfm_sb[:, c, :], i=pbbf(tb)[:, :]:
                e.activation(o, i, AF.Copy), [evtr], inc="ae")
        ev_xb_used[par] = evtr
        ev_xfm = ev_tr_evac[(NCH - 1) % 2]
        ev_xfm2 = ev_tr_evac[NCH % 2]

        for kind in ("k", "q"):
            wsb = wq_sb if kind == "q" else wk_sb
            ev_w = ev_wq if kind == "q" else ev_wk
            bias_sb = bq_sb if kind == "q" else bk_sb
            dst = Qs if kind == "q" else Ks
            for fc in range(8):
                bank = 2 + (fc % 2)
                evm = None
                for kc in range(NCH):
                    evm = p.op(
                        "tensor",
                        lambda e, o=pb[bank][:, :],
                        l=wsb[:, kc, fc * 128:(fc + 1) * 128],
                        r=xfm_sb[:, kc, :], st=(kc == 0), sp=(kc == NCH - 1):
                        e.matmul(o, lhsT=l, rhs=r, start=st, stop=sp),
                        [ev_w, ev_xfm, ev_xfm2, ev_qk_evac[fc % 2]],
                        inc="pe" if kc == NCH - 1 else None)
                sp_ = fc % 2
                ev_qk_evac[sp_] = p.op(
                    "scalar",
                    lambda e, o=qkst[sp_][:, :], i=pb[bank][:, :],
                    bi=bias_sb[:, fc:fc + 1]:
                    e.activation(o, i, AF.Identity, bias=bi),
                    [evm, ev_qkst_store[sp_]], inc="ae")
                ev_qkst_store[sp_] = p.dma(
                    "sync", dst.ap()[fc, :, t0:t0 + 512], qkst[sp_][:, :],
                    [ev_qk_evac[sp_]])
                (ev_qs_store if kind == "q" else ev_ks_store)[fc] = \
                    ev_qkst_store[sp_]

        for ts4 in range(4):
            for fg in range(2):
                sp_ = (ts4 * 2 + fg) % 2
                bank = 4 + sp_
                for kc in range(NCH):
                    p.op("tensor",
                         lambda e, o=pb[bank][:, :],
                         l=xfm_sb[:, kc, ts4 * 128:(ts4 + 1) * 128],
                         r=wv_sb[:, kc, fg * 512:(fg + 1) * 512],
                         st=(kc == 0):
                         e.matmul(o, lhsT=l, rhs=r, start=st, stop=False),
                         [ev_wv, ev_xfm, ev_xfm2, ev_v_evac[sp_]],
                         inc=None)
                evm = p.op(
                    "tensor",
                    lambda e, o=pb[bank][:, :], l=onesr_sb[:, :],
                    r=robuf[0:1, fg * 512 + 512 * 0:fg * 512 + 512]:
                    e.matmul(o, lhsT=l, rhs=r, start=False, stop=True),
                    [ev_bv, ev_onesr], inc="pe")
                ev_robuf_free = evm
                ev_v_evac[sp_] = p.op(
                    "scalar",
                    lambda e, o=vst[sp_][:, :], i=pb[bank][:, :]:
                    e.activation(o, i, AF.Copy),
                    [evm, ev_vst_store[sp_]], inc="ae")
                ev_vst_store[sp_] = p.dma(
                    "sync",
                    Vs.ap()[fg * 4:(fg + 1) * 4,
                            t0 + ts4 * 128:t0 + (ts4 + 1) * 128, :]
                    .rearrange("j t d -> t j d"),
                    vst[sp_][:, :].rearrange("p (j d) -> p j d", d=128),
                    [ev_v_evac[sp_]])

        if grp < 4:
            w_pass1_pair(grp)
            if grp == 3:
                wstate["ladder"] = _emit_scale_ladder(
                    p, amaxW[:, :, :].rearrange("p a b -> p (a b)"),
                    sclW[:, :], rsclW[:, :], t16W[:, :],
                    wstate["amax_done"], W_EMIN, W_EMAX)
        else:
            for u in (2 * (grp - 4), 2 * (grp - 4) + 1):
                w_pass2b(u)

    s1_end = p.snap()
    s1.close()

    s2keep = ExitStack()
    wo_sb = SB(s2keep, "wo_sb", [128, 8, D], BF16)
    s2 = ExitStack()
    maskt_sb = SB(s2, "maskt_sb", [128, 4, 512], F32)
    khead = [SB(s2, f"khead{i}", [128, T], BF16) for i in range(2)]
    vhead = [SB(s2, f"vhead{i}", [128, 32, 128], BF16) for i in range(2)]
    qhead = [SB(s2, f"qhead{i}", [128, T], BF16) for i in range(2)]
    expb = [SB(s2, f"expb{i}", [128, 512], BF16) for i in range(6)]
    dn_sb = [SB(s2, f"dn_sb{i}", [1, 512], F32) for i in range(2)]
    rec_sb = [SB(s2, f"rec_sb{i}", [1, 512], F32) for i in range(2)]
    bcm_sb = [SB(s2, f"bcm_sb{i}", [128, 512], F32) for i in range(2)]
    yst = [SB(s2, f"yst{i}", [128, 512], BF16) for i in range(2)]
    ev_mask = p.dma("sync", maskt_sb[:, :, :].rearrange("p a b -> p (a b)"),
                    maskt, s1_end)
    ev_wo = None
    for sl in range(2):
        waits = list(s1_end) if sl == 0 else []
        ev_wo = p.dma(
            "gpsimd", wo_sb[:, sl * 4:(sl + 1) * 4, :],
            wo[sl * 512:(sl + 1) * 512, :].rearrange("(c p) f -> p c f", p=128),
            waits + [ev_wo])
    ev_wo_done = ev_wo

    ev_kh = [None, None]
    ev_vh = [None, None]
    ev_qh = [None, None]
    last_pe_head = [None, None]
    exp_free = [None] * 6
    S_free = {0: None, 1: None, 5: None}
    pv_free = [None, None]
    dn_free = [None, None]
    rec_free = [None, None]
    dnsb_free = [None, None]
    bcm_free = [None, None]
    yst_free = [None, None]
    ygl = 0
    gti = [0]
    bcst = {"bank": None, "yp": 0}
    SBANKS = (0, 1, 5)

    for j in range(8):
        hp = j % 2
        w0 = s1_end if j < 2 else []
        ev_qh[hp] = p.dma("sync", qhead[hp][:, :], Qs.ap()[j, :, :],
                          w0 + [ev_qs_store[j], last_pe_head[hp], ev_qh[hp]])
        ev_kh[hp] = p.dma("sync", khead[hp][:, :], Ks.ap()[j, :, :],
                          w0 + [ev_ks_store[j], last_pe_head[hp], ev_kh[hp]])
        ev_vh[hp] = p.dma("sync", vhead[hp][:, :, :],
                          Vs.ap()[j, :, :].rearrange("(a p) d -> p a d", p=128),
                          w0 + [ev_vst_store[0], ev_vst_store[1],
                                last_pe_head[hp], ev_vh[hp]])
        G, qgs = PLANS[j]
        pend = []

        def pop_one():
            fn_, i_, xp_, ev_, epi_ = pend.pop(0)
            pv_ = fn_(i_, xp_, ev_)
            exp_free[xp_] = pv_
            if epi_ is not None:
                epi_(pv_)

        for (q0, ref, kts) in qgs:
            qp = ygl % 2
            pvbank = 2 + qp
            dnrow = 32 * qp
            nt = len(kts)

            def emit_dnpv(i, xp, evexp, qp=qp, dnrow=dnrow, pvbank=pvbank,
                          G=G, kts=kts, nt=nt, hp=hp):
                first, last = (i == 0), (i == nt - 1)
                p.op("tensor",
                     lambda e, o=pb[4][dnrow:dnrow + 1, 0:G], l=onesc_sb[:, :],
                     r=expb[xp][:, 0:G], st=first, sp=last:
                     e.matmul(o, lhsT=l, rhs=r, start=st, stop=sp),
                     [evexp, ev_onesc] + ([dn_free[qp]] if first else []),
                     inc=None)
                kloc = kts[i][0] // 128
                return p.op(
                    "tensor",
                    lambda e, o=pb[pvbank][:, 0:G],
                    l=vhead[hp][:, kloc, :], r=expb[xp][:, 0:G],
                    st=first, sp=last:
                    e.matmul(o, lhsT=l, rhs=r, start=st, stop=sp),
                    [evexp, ev_vh[hp]] + ([pv_free[qp]] if first else []),
                    inc="pe")

            def epilogue(ev_pvlast, qp=qp, dnrow=dnrow, pvbank=pvbank,
                         G=G, q0=q0, j=j, hp=hp):
                ev_dnev = p.op("scalar",
                               lambda e, o=dn_sb[qp][0:1, 0:G],
                               i=pb[4][dnrow:dnrow + 1, 0:G]:
                               e.activation(o, i, AF.Copy),
                               [ev_pvlast, dnsb_free[qp]], inc="ae")
                dn_free[qp] = ev_dnev
                ev_rec = p.op("vector",
                              lambda e, o=rec_sb[qp][0:1, 0:G],
                              i=dn_sb[qp][0:1, 0:G]:
                              e.reciprocal(o, i), [ev_dnev, rec_free[qp]],
                              inc="ve")
                dnsb_free[qp] = ev_rec
                ev_bcmm = p.op("tensor",
                               lambda e, o=pb[6][:, 0:G], l=onesr_sb[:, :],
                               r=rec_sb[qp][0:1, 0:G]:
                               e.matmul(o, lhsT=l, rhs=r, start=True,
                                        stop=True),
                               [ev_rec, ev_onesr, bcst["bank"]], inc="pe")
                rec_free[qp] = ev_bcmm
                ev_bcev = p.op("scalar",
                               lambda e, o=bcm_sb[qp][:, 0:G], i=pb[6][:, 0:G]:
                               e.activation(o, i, AF.Copy),
                               [ev_bcmm, bcm_free[qp]], inc="ae")
                bcst["bank"] = ev_bcev
                yp = bcst["yp"]
                bcst["yp"] = 1 - yp
                ev_ym = p.op("vector",
                             lambda e, o=yst[yp][:, 0:G], i=pb[pvbank][:, 0:G],
                             b_=bcm_sb[qp][:, 0:G]:
                             e.tensor_tensor(o, i, b_, ALU.mult),
                             [ev_bcev, ev_pvlast, yst_free[yp]], inc="ve")
                pv_free[qp] = ev_ym
                bcm_free[qp] = ev_ym
                yst_free[yp] = p.dma(
                    "scalar",
                    Ysin.ap()[j * 128:(j + 1) * 128, q0:q0 + G],
                    yst[yp][:, 0:G], [ev_ym])
                last_pe_head[hp] = ev_ym

            for i, (k0, colidx, mi) in enumerate(kts):
                sp_ = SBANKS[gti[0] % 3]
                xp = gti[0] % 6
                gti[0] += 1
                evS = p.op(
                    "tensor",
                    lambda e, o=pb[sp_][:, 0:G],
                    l=khead[hp][:, k0:k0 + 128],
                    r=qhead[hp][:, q0:q0 + G]:
                    e.matmul(o, lhsT=l, rhs=r, start=True, stop=True),
                    [ev_kh[hp], ev_qh[hp], S_free[sp_]], inc="pe")
                if mi is not None:
                    evS = p.op(
                        "vector",
                        lambda e, o=pb[sp_][:, 0:G], m=maskt_sb[:, mi, 0:G]:
                        e.tensor_tensor(o, o, m, ALU.add),
                        [evS, ev_mask], inc="ve")
                evexp = p.op(
                    "scalar",
                    lambda e, o=expb[xp][:, 0:G], i=pb[sp_][:, 0:G],
                    bi=alib_sb[:, colidx:colidx + 1]:
                    e.activation(o, i, AF.Exp, bias=bi, scale=QSCALE),
                    [evS, ev_alib, exp_free[xp]], inc="ae")
                S_free[sp_] = evexp
                pend.append((emit_dnpv, i, xp, evexp,
                             epilogue if i == nt - 1 else None))
                if len(pend) > 4:
                    pop_one()
            ygl += 1
        while pend:
            pop_one()
        if j == 1:
            ev_wcc = p.op("gpsimd",
                          lambda e: e.collective_compute(
                              "AllGather", ALU.bypass,
                              replica_groups=[list(range(8))],
                              ins=[Wfc.ap().opt()], outs=[Wfg.ap().opt()]),
                          wstate["stores"], inc="cc")

    s2_all = p.snap()

    s25 = ExitStack()
    ylhs = [SB(s25, f"ylhs{i}", [128, 8, 128], BF16) for i in range(4)]
    post = [SB(s25, f"post{i}", [128, D], BF16) for i in range(2)]

    ev_yl = [None] * 4
    ev_yl_used = [None] * 4
    ev_post_store = [None, None]
    bank_free3 = [None, None, None, None]
    TTORD = list(range(0, 8)) + list(range(16, 24)) +         list(range(8, 16)) + list(range(24, 32))
    GORD = [0, 2, 1, 3]
    ev_cc_a = None
    for it, tt in enumerate(TTORD):
        tp = it % 4
        pp_ = it % 2
        tg = tt * 128
        rowoff = GORD.index(tt // 8) * 1024 + (tt % 8) * 128
        ev_yl[tp] = p.dma(
            "sync" if it % 2 == 0 else "scalar", ylhs[tp][:, :, :],
            Ysin.ap()[:, tg:tg + 128].rearrange("(c p) t -> p c t", p=128),
            (s2_all if it < 4 else []) + [ev_yl_used[tp]])
        evmm = [None] * 4
        for dc in range(8):
            for ng in range(4):
                evmm[ng] = p.op(
                    "tensor",
                    lambda e, o=pb[ng][:, :], l=ylhs[tp][:, dc, :],
                    r=wo_sb[:, dc, ng * 512:(ng + 1) * 512],
                    st=(dc == 0), sp=(dc == 7):
                    e.matmul(o, lhsT=l, rhs=r, start=st, stop=sp),
                    (s2_all if it < 2 else []) +
                    [ev_yl[tp], ev_wo_done, bank_free3[ng]],
                    inc="pe" if dc == 7 else None)
        ev_yl_used[tp] = evmm[3]
        evev = [None] * 4
        for ng in range(4):
            evev[ng] = p.op(
                "scalar",
                lambda e, o=post[pp_][:, ng * 512:(ng + 1) * 512],
                i=pb[ng][:, :]: e.activation(o, i, AF.Copy),
                [evmm[ng], ev_post_store[pp_]], inc="ae")
            bank_free3[ng] = evev[ng]
        ev_post_store[pp_] = p.dma(
            "sync", POs.ap()[rowoff:rowoff + 128, :], post[pp_][:, :],
            [evev[0], evev[1], evev[2], evev[3]])
        if it == 15:
            half_a = p.snap()
            ev_cc_a = p.op("gpsimd",
                           lambda e: e.collective_compute(
                               "ReduceScatter", ALU.add,
                               replica_groups=[[0, 1], [2, 3], [4, 5], [6, 7]],
                               ins=[POs.ap()[0:2048, :].opt()],
                               outs=[POhA.ap().opt()]),
                           half_a, inc="cc")

    s25_all = p.snap()
    s25.close()
    ev_cc_b = p.op("gpsimd",
                   lambda e: e.collective_compute(
                       "ReduceScatter", ALU.add,
                       replica_groups=[[0, 1], [2, 3], [4, 5], [6, 7]],
                       ins=[POs.ap()[2048:4096, :].opt()],
                       outs=[POhB.ap().opt()]),
                   s25_all, inc="cc")
    s2.close()

    s3 = ExitStack()
    potile = [SB(s3, f"pot{i}", [128, D], BF16) for i in range(2)]
    xres = [SB(s3, f"xres{i}", [128, D], F32) for i in range(2)]
    xpbo_sb = SB(s3, "xpbo", [128, D], F32)
    r_sb = SB(s3, "r_sb", [128, D], F32)
    c_sb = SB(s3, "c_sb", [128, D], F32)
    m1_sb = SB(s3, "m1_sb", [128, 1], F32)
    m_sb = SB(s3, "m_sb", [128, 1], F32)
    v1_sb = SB(s3, "v1_sb", [128, 1], F32)
    rstd_sb = SB(s3, "rstd_sb", [128, 1], F32)
    y1st = [SB(s3, f"y1st{i}", [128, D], F32) for i in range(2)]

    ev_bo = p.dma("sync", robuf[0:1, :], bor, [ev_robuf_free] + s2_all)
    ev_bobc = None
    for sl in range(4):
        evm = p.op("tensor", lambda e, o=pb[4][:, :], l=onesr_sb[:, :],
                   r=robuf[0:1, sl * 512:(sl + 1) * 512]:
                   e.matmul(o, lhsT=l, rhs=r, start=True, stop=True),
                   s25_all + [ev_bo, ev_bobc], inc="pe")
        ev_bobc = p.op("scalar",
                       lambda e, o=bobf_bc[:, sl * 512:(sl + 1) * 512],
                       i=pb[4][:, :]: e.activation(o, i, AF.Copy),
                       [evm], inc="ae")
    ev_robuf_free = ev_bobc

    ev_po = [None, None]
    ev_po_used = [None, None]
    ev_xr = [None, None]
    ev_xr_used = [None, None]
    ev_xpbo_used = None
    ev_r_used = None
    ev_c_used = None
    ev_y1_store = [None, None]
    ev_amax3 = [None, None]

    for tt in range(16):
        tp = tt % 2
        tloc = tt * 128
        posrc = (POhA.ap()[tloc:tloc + 128, :] if tt < 8
                 else POhB.ap()[tloc - 1024:tloc - 1024 + 128, :])
        ev_po[tp] = p.dma("sync", potile[tp][:, :], posrc,
                          [ev_cc_a if tt < 8 else ev_cc_b, ev_po_used[tp]])
        ev_xr[tp] = p.dma("scalar", xres[tp][:, :], xr[tloc:tloc + 128, :],
                          (s25_all if tt < 2 else []) + [ev_xr_used[tp]])
        ev_xpbo = p.op("gpsimd",
                       lambda e, o=xpbo_sb[:, :], i=xres[tp][:, :],
                       b_=bobf_bc[:, :]: e.tensor_tensor(o, i, b_, ALU.add),
                       [ev_xr[tp], ev_bobc, ev_xpbo_used], inc="po")
        ev_xr_used[tp] = ev_xpbo
        evr = p.op("vector",
                   lambda e, o=r_sb[:, :], i=potile[tp][:, :],
                   b_=xpbo_sb[:, :]: e.tensor_tensor(o, i, b_, ALU.add),
                   [ev_po[tp], ev_xpbo, ev_r_used], inc="ve")
        ev_po_used[tp] = evr
        ev_xpbo_used = evr
        evm1 = p.op("vector", lambda e, o=m1_sb[:, :], i=r_sb[:, :]:
                    e.reduce_sum(o, i, axis=AX.X), [evr], inc="ve")
        evm = p.op("vector", lambda e, o=m_sb[:, :], i=m1_sb[:, :]:
                   e.tensor_scalar(o, i, -1.0 / D, None, ALU.mult),
                   [evm1], inc="ve")
        evc = p.op("scalar", lambda e, o=c_sb[:, :], i=r_sb[:, :],
                   m=m_sb[:, 0:1]:
                   e.activation(o, i, AF.Identity, bias=m),
                   [evm, ev_c_used], inc="ae")
        ev_r_used = evc
        evsq = p.op("scalar", lambda e, o=r_sb[:, :], i=c_sb[:, :],
                    a_=v1_sb[:, 0:1]:
                    e.activation(o, i, AF.Square, accum_out=a_),
                    [evc], inc="ae")
        ev_r_used = evsq
        evve = p.op("vector", lambda e, o=v1_sb[:, :]:
                    e.tensor_scalar(o, o, 1.0 / D, LNEPS, ALU.mult, ALU.add),
                    [evsq], inc="ve")
        evln = p.op("scalar", lambda e, o=rstd_sb[:, :], i=v1_sb[:, :]:
                    e.activation(o, i, AF.Ln), [evve], inc="ae")
        evex = p.op("scalar", lambda e, o=rstd_sb[:, :], i=rstd_sb[:, :]:
                    e.activation(o, i, AF.Exp, scale=-0.5), [evln], inc="ae")
        evy1 = p.op("scalar", lambda e, o=y1st[tp][:, :], i=c_sb[:, :],
                    m=rstd_sb[:, 0:1]:
                    e.activation(o, i, AF.Identity, scale=m),
                    [evex, ev_y1_store[tp], ev_amax3[tp]], inc="ae")
        ev_c_used = evy1
        ev_amax3[tp] = p.op(
            "vector",
            lambda e, o=amaxA[:, tt, :],
            i=y1st[tp][:, :].rearrange("p (g w) -> p g w", w=32):
            e.reduce_max(o, i, axis=AX.X, apply_absolute_value=True),
            [evy1], inc="ve")
        ev_y1_store[tp] = p.dma("sync", Y1s.ap()[tloc:tloc + 128, :],
                                y1st[tp][:, :], [evy1])

    s3_end = p.snap()
    s3.close()
    s2keep.close()

    s4 = ExitStack()
    wft_sb = SB(s4, "wft_sb", [128, NCH, D], BF16)
    y1l = [SB(s4, f"y1l{i}", [128, D], F32) for i in range(2)]
    t16A = SB(s4, "t16A", [128, NCH * 64], F32)
    ascr = dict(
        q=SB(s4, "a_q", [128, D], F32)[:, :],
        aq=SB(s4, "a_aq", [128, D], F32)[:, :],
        sgn=SB(s4, "a_sgn", [128, D], BF16)[:, :],
        tC=SB(s4, "a_tC", [128, D], F32)[:, :],
        s2=SB(s4, "a_s2", [128, D], BF16)[:, :],
    )
    aq_tm = [SB(s4, f"aq_tm{i}", [128, D], BF16) for i in range(2)]
    afm_sb = [SB(s4, f"afm{i}", [128, NCH, 128], BF16) for i in range(2)]
    r4_sb = SB(s4, "r4_sb", [128, D], F32)
    ffnout_sb = SB(s4, "ffnout", [128, D], F32)
    c4_sb = SB(s4, "c4_sb", [128, D], F32)
    m14_sb = SB(s4, "m14_sb", [128, 1], F32)
    m4_sb = SB(s4, "m4_sb", [128, 1], F32)
    v14_sb = SB(s4, "v14_sb", [128, 1], F32)
    rstd4_sb = SB(s4, "rstd4_sb", [128, 1], F32)
    outst = [SB(s4, f"outst{i}", [128, D], F32) for i in range(2)]

    ev_ladA = _emit_scale_ladder(
        p, amaxA[:, :, :].rearrange("p a b -> p (a b)"),
        sclA[:, :], rsclA[:, :], t16A[:, :],
        s3_end, A_EMIN, A_EMAX)

    ev_y1l = [None, None]
    for tt in range(2):
        ev_y1l[tt] = p.dma("scalar", y1l[tt][:, :],
                           Y1s.ap()[tt * 128:tt * 128 + 128, :],
                           s3_end + [ev_y1_store[tt % 2]])

    ev_bf = p.dma("sync", robuf[0:1, :], bfr, [ev_robuf_free] + s3_end)

    ev_wft_r = [None] * 8
    prevq = {"sync": None, "scalar": None}
    for r_ in range(8):
        qn = "sync" if r_ % 2 == 0 else "scalar"
        ev_wft_r[r_] = p.dma(
            qn, wft_sb[:, :, r_ * 256:(r_ + 1) * 256],
            Wfg.ap()[r_ * 2048:(r_ + 1) * 2048, :]
            .rearrange("(c p) n -> p c n", p=128),
            (s3_end if r_ < 2 else []) + [ev_wcc, prevq[qn]])
        prevq[qn] = ev_wft_r[r_]

    afree = {}
    ev_y1l_used = [None, None]
    ev_aqtm_used = [None, None]
    ev_afm_evac = [None, None]
    ev_afm_used = [None, None]
    bank_free4 = [None, None, None, None]
    ev_out_store = [None, None]
    st4 = {"r4_used": None, "c4_used": None, "evmm": None, "ffn_used": None}

    def s4_tail(tt):
        tp = tt % 2
        tloc = tt * 128
        evcp = None
        for ng in range(4):
            evcp = p.op("scalar",
                        lambda e, o=ffnout_sb[:, ng * 512:(ng + 1) * 512],
                        i=pb[ng][:, :]: e.activation(o, i, AF.Copy),
                        [st4["evmm"][ng], st4["ffn_used"]], inc="ae")
            bank_free4[ng] = evcp
        evr = p.op("gpsimd",
                   lambda e, o=r4_sb[:, :], i0=ffnout_sb[:, :],
                   i1=y1l[tp][:, :]: e.tensor_tensor(o, i0, i1, ALU.add),
                   [evcp, st4["r4_used"]], inc="po")
        st4["ffn_used"] = evr
        ev_y1l_used[tp] = evr
        st4["r4_used"] = evr
        evmean = p.op("scalar",
                      lambda e, o=c4_sb[:, :], i=r4_sb[:, :],
                      a_=m14_sb[:, 0:1]:
                      e.activation(o, i, AF.Copy, accum_out=a_),
                      [evr, st4["c4_used"]], inc="ae")
        evm = p.op("vector", lambda e, o=m4_sb[:, :], i=m14_sb[:, :]:
                   e.tensor_scalar(o, i, -1.0 / D, None, ALU.mult),
                   [evmean], inc="ve")
        evc = p.op("scalar", lambda e, o=c4_sb[:, :], i=r4_sb[:, :],
                   m=m4_sb[:, 0:1]:
                   e.activation(o, i, AF.Identity, bias=m),
                   [evm], inc="ae")
        evsq = p.op("scalar", lambda e, o=r4_sb[:, :], i=c4_sb[:, :],
                    a_=v14_sb[:, 0:1]:
                    e.activation(o, i, AF.Square, accum_out=a_),
                    [evc], inc="ae")
        st4["r4_used"] = evsq
        evve4 = p.op("vector", lambda e, o=v14_sb[:, :]:
                     e.tensor_scalar(o, o, 1.0 / D, LNEPS, ALU.mult, ALU.add),
                     [evsq], inc="ve")
        evln = p.op("scalar", lambda e, o=rstd4_sb[:, :], i=v14_sb[:, :]:
                    e.activation(o, i, AF.Ln), [evve4], inc="ae")
        evex = p.op("scalar", lambda e, o=rstd4_sb[:, :], i=rstd4_sb[:, :]:
                    e.activation(o, i, AF.Exp, scale=-0.5), [evln], inc="ae")
        evout = p.op("scalar", lambda e, o=outst[tp][:, :], i=c4_sb[:, :],
                     m=rstd4_sb[:, 0:1]:
                     e.activation(o, i, AF.Identity, scale=m),
                     [evex, ev_out_store[tp]], inc="ae")
        st4["c4_used"] = evout
        ev_out_store[tp] = p.dma("sync", out[tloc:tloc + 128, :],
                                 outst[tp][:, :], [evout])

    for tt in range(16):
        tp = tt % 2
        evq, afree = _emit_round(
            p, y1l[tp][:, :], aq_tm[tp][:, :],
            sclA[:, tt * 64:(tt + 1) * 64], rsclA[:, tt * 64:(tt + 1) * 64],
            ascr, dict(afree, out=ev_aqtm_used[tp]),
            [ev_y1l[tp]] + list(ev_ladA), pool=True)
        if tt >= 1:
            s4_tail(tt - 1)
            if tt + 1 < 16:
                ev_y1l[(tt + 1) % 2] = p.dma(
                    "scalar", y1l[(tt + 1) % 2][:, :],
                    Y1s.ap()[(tt + 1) * 128:(tt + 1) * 128 + 128, :],
                    [ev_y1_store[(tt + 1) % 2], ev_y1l_used[(tt + 1) % 2]])
        evt = None
        for kq in range(4):
            tb = kq % 2
            for kk in range(4):
                kc = kq * 4 + kk
                evt = p.op(
                    "tensor",
                    lambda e, o=pbbf(4 + tb)[:, kk * 128:(kk + 1) * 128],
                    i=aq_tm[tp][:, kc * 128:(kc + 1) * 128],
                    idn=eyeb_sb[:, :]: e.transpose(o, i, idn),
                    [evq, ev_eye, ev_afm_evac[tb]],
                    inc="pe" if kk == 3 else None)
            ev_afm_evac[tb] = p.op(
                "scalar",
                lambda e, o=afm_sb[tp][:, kq * 4:(kq + 1) * 4, :]
                .rearrange("p a b -> p (a b)"),
                i=pbbf(4 + tb)[:, :]: e.activation(o, i, AF.Copy),
                [evt, ev_afm_used[tp] if kq < 2 else None], inc="ae")
        ev_aqtm_used[tp] = evt
        afree["out"] = evt
        ev_afm = [ev_afm_evac[0], ev_afm_evac[1]]
        for kc in range(NCH):
            for ng in range(4):
                w0 = ([ev_wft_r[2 * ng], ev_wft_r[2 * ng + 1],
                       bank_free4[ng]] if kc == 0 else [])
                p.op("tensor",
                     lambda e, o=pb[ng][:, :], l=afm_sb[tp][:, kc, :],
                     r=wft_sb[:, kc, ng * 512:(ng + 1) * 512],
                     st=(kc == 0):
                     e.matmul(o, lhsT=l, rhs=r, start=st, stop=False),
                     ev_afm + w0, inc=None)
        evmm = [None] * 4
        for ng in range(4):
            evmm[ng] = p.op(
                "tensor",
                lambda e, o=pb[ng][:, :], l=onesr_sb[:, :],
                r=robuf[0:1, ng * 512:(ng + 1) * 512]:
                e.matmul(o, lhsT=l, rhs=r, start=False, stop=True),
                [ev_bf], inc="pe")
        ev_afm_used[tp] = evmm[3]
        st4["evmm"] = evmm
    s4_tail(15)

    dbg = []
    if debug:
        for name, th, shape, dt in (
                ("dbg_qs", Qs, [8, 128, T], BF16),
                ("dbg_ks", Ks, [8, 128, T], BF16),
                ("dbg_vs", Vs, [8, T, 128], BF16),
                ("dbg_ysin", Ysin, [1024, T], BF16),
                ("dbg_poha", POhA, [1024, D], BF16),
                ("dbg_pohb", POhB, [1024, D], BF16),
                ("dbg_pos", POs, [T, D], BF16),
                ("dbg_wfg", Wfg, [8 * NCH * 128, 256], BF16),
                ("dbg_y1", Y1s, [TPC, D], F32)):
            dap = nc.dram_tensor(name, shape, dt, kind="ExternalOutput").ap()
            p.dma("sync", dap, th.ap(), p.snap())
            dbg.append(name)

    fin = p.snap()
    p.op("sync", lambda e: e.nop(), fin)

    with ExitStack() as ectx:
        sems = {name: ectx.enter_context(nc.semaphore(name))
                for name in ("qd", "ad", "gd", "pe", "ve", "ae", "po", "cc")}
        block = ectx.enter_context(nc.Block())

        def emitter(engname):
            def run(eng):
                lastw = {}
                for (en, w, fn, inc, n) in p.ops:
                    if en != engname:
                        continue
                    for s, v in w.items():
                        if lastw.get(s, -1) >= v:
                            continue
                        eng.wait_ge(sems[s], v)
                        lastw[s] = v
                    inst = fn(eng)
                    if inc is not None:
                        inst.then_inc(sems[inc], n)
            return run

        block.sync(emitter("sync"))
        block.gpsimd(emitter("gpsimd"))
        block.tensor(emitter("tensor"))
        block.vector(emitter("vector"))
        block.scalar(emitter("scalar"))

    s4.close()
    ctx.close()
    return nc, dbg


def _host_inputs(inputs, c):
    b, hh = c >> 1, c & 1
    heads = SET0 if hh == 0 else SET1
    cols = np.concatenate([np.arange(h * 128, (h + 1) * 128) for h in heads])
    f32 = np.float32
    xb = np.ascontiguousarray(np.asarray(inputs["L0_attn_x"][b], dtype=f32))
    d = {
        "x": xb,
        "xr": np.ascontiguousarray(xb[hh * TPC:(hh + 1) * TPC, :]),
        "wq": np.ascontiguousarray(np.asarray(inputs["L0_attn_Wq"], f32)[:, cols]),
        "wk": np.ascontiguousarray(np.asarray(inputs["L0_attn_Wk"], f32)[:, cols]),
        "wv": np.ascontiguousarray(np.asarray(inputs["L0_attn_Wv"], f32)[:, cols]),
        "bqr": np.ascontiguousarray(
            np.asarray(inputs["L0_attn_bq"], f32)[cols].reshape(8, 128).T),
        "bkr": np.ascontiguousarray(
            np.asarray(inputs["L0_attn_bk"], f32)[cols].reshape(8, 128).T),
        "bvr": np.ascontiguousarray(
            np.asarray(inputs["L0_attn_bv"], f32)[cols].reshape(1, 1024)),
        "wo": np.ascontiguousarray(np.asarray(inputs["L0_attn_Wo"], f32)[cols, :]),
        "bor": np.ascontiguousarray(
            np.asarray(inputs["L0_attn_bo"], f32).reshape(1, D)),
        "wf": np.ascontiguousarray(
            np.asarray(inputs["L0_ffn_weight"], f32)[c * 256:(c + 1) * 256, :]),
        "bfr": np.ascontiguousarray(
            np.asarray(inputs["L0_ffn_bias"], f32).reshape(1, D)),
        "eyeb": np.eye(128, dtype=ml_dtypes.bfloat16),
        "onesr": np.ones((1, 128), f32),
        "onesc": np.ones((128, 1), ml_dtypes.bfloat16),
        "alib": _alibi_table(hh),
        "maskt": _mask_table(),
    }
    return d


def kernel(**inputs):
    nc, _ = _build(debug=False)
    in_maps = [_host_inputs(inputs, c) for c in range(8)]
    res = run_bass_kernel_spmd(nc, in_maps, core_ids=list(range(8)))
    out = np.empty((B, T, D), np.float32)
    for c in range(8):
        b, hh = c >> 1, c & 1
        out[b, hh * TPC:(hh + 1) * TPC, :] = res.results[c]["out"]
    return out


# version 12
# speedup vs baseline: 1.0205x; 1.0205x over previous
import math
from collections import defaultdict
from contextlib import ExitStack

import numpy as np
import ml_dtypes

import concourse.bass as bass
import concourse.mybir as mybir
from concourse.bass_utils import run_bass_kernel_spmd

F32 = mybir.dt.float32
BF16 = mybir.dt.bfloat16
AF = mybir.ActivationFunctionType
ALU = mybir.AluOpType
AX = mybir.AxisListType

B, T, D = 4, 4096, 2048
NCH = D // 128
TPC = T // 2
SET0 = [0, 3, 4, 7, 8, 11, 12, 15]
SET1 = [1, 2, 5, 6, 9, 10, 13, 14]
SLOPES = [2.0 ** (-(g + 1) / 2) for g in range(16)]
QSCALE = 1.0 / math.sqrt(128.0)
NEGBIG = -1.0e30
LNEPS = 1e-5
W_EMIN, W_EMAX = -16, 4
A_EMIN, A_EMAX = -16, 7


def _attn_plan():
    colkey = {}
    plans = []
    for j in range(8):
        a0, a1 = SLOPES[SET0[j]], SLOPES[SET1[j]]
        amax, amin = max(a0, a1), min(a0, a1)
        G = 128 if amax > 0.4 else (256 if amax > 0.2 else 512)
        reach = 115.0 / amin + 128.0
        qgs = []
        for q0 in range(0, T, G):
            ref = q0 + G // 2
            kmin = max(0, int(math.floor((ref - reach) / 128.0)) * 128)
            kts = []
            for k0 in range(kmin, q0 + G, 128):
                key = (j, k0 - ref)
                if key not in colkey:
                    colkey[key] = len(colkey)
                delta = k0 - q0
                mi = delta // 128 if delta >= 0 else None
                kts.append((k0, colkey[key], mi))
            qgs.append((q0, ref, kts))
        plans.append((G, qgs))
    return plans, colkey


PLANS, COLKEY = _attn_plan()
NCOL = len(COLKEY)
assert NCOL <= 512, NCOL


def _alibi_table(hh):
    arr = np.zeros((128, NCOL), np.float32)
    heads = SET0 if hh == 0 else SET1
    pp = np.arange(128, dtype=np.float64)
    for (j, off), idx in COLKEY.items():
        arr[:, idx] = (SLOPES[heads[j]] * (off + pp)).astype(np.float32)
    return np.ascontiguousarray(arr)


def _mask_table():
    m = np.zeros((128, 4, 512), np.float32)
    pp = np.arange(128)[:, None]
    ff = np.arange(512)[None, :]
    for d in range(4):
        m[:, d, :] = np.where(ff >= pp + 128 * d, 0.0, NEGBIG)
    return np.ascontiguousarray(m.reshape(128, 4 * 512))


class Plan:
    def __init__(self):
        self.ops = []
        self.cnt = defaultdict(int)

    def op(self, eng, fn, waits=(), inc=None, n=1):
        w = {}
        for ev in waits:
            if ev is None:
                continue
            s, v = ev
            if w.get(s, -1) < v:
                w[s] = v
        evt = None
        if inc is not None:
            self.cnt[inc] += n
            evt = (inc, self.cnt[inc])
        self.ops.append((eng, w, fn, inc, n))
        return evt

    def dma(self, queue, out_ap, in_ap, waits=()):
        sem = {"sync": "qd", "scalar": "ad", "gpsimd": "gd"}[queue]
        fn = lambda e, o=out_ap, i=in_ap: e.dma_start(out=o, in_=i)
        return self.op(queue, fn, waits, inc=sem, n=16)

    def snap(self):
        return [(s, v) for s, v in self.cnt.items()]


def _emit_scale_ladder(p, amax, scl, rscl, t16, waits, emin, emax):
    e1 = p.op("vector", lambda e, o=amax: e.tensor_scalar(
        o, o, float(2.0 ** emin), None, ALU.max), list(waits), inc="ve")
    es = p.op("vector", lambda e, o=scl: e.memset(o, float(2.0 ** (emin - 2))),
              list(waits), inc="ve")
    for ee in range(emin + 1, emax + 1):
        et = p.op("vector", lambda e, o=t16, i=amax, th=float(2.0 ** ee):
                  e.tensor_scalar(o, i, th, 1.0, ALU.is_ge, ALU.add),
                  [e1, es], inc="ve")
        es = p.op("vector", lambda e, o=scl, i=t16: e.tensor_mul(o, o, i),
                  [et], inc="ve")
    er = p.op("vector", lambda e, o=rscl, i=scl: e.reciprocal(o, i),
              [es], inc="ve")
    es = p.op("vector", lambda e, o=scl: e.tensor_scalar(
        o, o, 0.5, None, ALU.mult), [es, er], inc="ve")
    return es, er


def _emit_round(p, src, out_bf, scl, rscl, scr, frees, waits, pool=False):
    W = src.shape[-1]
    nb = W // 32
    q, aq, sgn, tC, s2 = (scr["q"], scr["aq"], scr["sgn"], scr["tC"],
                          scr["s2"])
    win = src.rearrange("p (g w) -> p g w", w=32)
    rb = rscl.unsqueeze(2).broadcast_to([128, nb, 32])
    qv = q.rearrange("p (g w) -> p g w", w=32)
    eq = p.op("vector", lambda e, o=qv, i=win, r=rb: e.tensor_tensor(
        o, i, r, ALU.mult), list(waits) + [frees.get("q")], inc="ve")
    ea = p.op("scalar", lambda e, o=aq, i=q: e.activation(o, i, AF.Abs),
              [eq, frees.get("aq")], inc="ae")
    eg = p.op("scalar", lambda e, o=sgn, i=q: e.activation(o, i, AF.Sign),
              [eq, frees.get("sgn")], inc="ae")
    eA = p.op("vector", lambda e, o=q, i=aq: e.tensor_scalar(
        o, i, 0.25, None, ALU.is_gt), [ea, eg, eq], inc="ve")
    eA = p.op("vector", lambda e, o=q, i=aq: e.scalar_tensor_tensor(
        o, i, 0.75, o, ALU.is_ge, ALU.add), [eA], inc="ve")
    eA = p.op("vector", lambda e, o=q, i=aq: e.scalar_tensor_tensor(
        o, i, 1.25, o, ALU.is_gt, ALU.add), [eA], inc="ve")
    eA = p.op("vector", lambda e, o=q, i=aq: e.scalar_tensor_tensor(
        o, i, 1.75, o, ALU.is_ge, ALU.add), [eA], inc="ve")
    eU = p.op("vector", lambda e, o=tC, i=aq: e.tensor_scalar(
        o, i, 5.0, 2.0, ALU.is_gt, ALU.mult), [ea, frees.get("tC")], inc="ve")
    eU = p.op("vector", lambda e, o=tC, i=aq: e.scalar_tensor_tensor(
        o, i, 2.5, o, ALU.is_gt, ALU.add), [eU], inc="ve")
    eU = p.op("vector", lambda e, o=tC, i=aq: e.scalar_tensor_tensor(
        o, i, 3.5, o, ALU.is_ge, ALU.add), [eU], inc="ve")
    eA = p.op("vector", lambda e, o=q, i=tC: e.scalar_tensor_tensor(
        o, i, 2.0, o, ALU.mult, ALU.add), [eA, eU], inc="ve")
    sb_ = scl.unsqueeze(2).broadcast_to([128, nb, 32])
    s2v = s2.rearrange("p (g w) -> p g w", w=32)
    sgv = sgn.rearrange("p (g w) -> p g w", w=32)
    e2 = p.op("gpsimd", lambda e, o=s2v, i=sgv, s_=sb_: e.tensor_tensor(
        o, i, s_, ALU.mult), [eg, frees.get("s2")], inc="po")
    eo = p.op("gpsimd", lambda e, o=out_bf, i=q, s_=s2: e.tensor_tensor(
        o, i, s_, ALU.mult), [eA, e2, frees.get("out")], inc="po")
    newfrees = {"q": eo, "aq": eA, "sgn": e2, "tC": eA, "s2": eo}
    return eo, newfrees


def _build(debug=False):
    nc = bass.Bass("TRN2", target_bir_lowering=False, debug=False)

    x = nc.dram_tensor("x", [T, D], F32, kind="ExternalInput").ap()
    xr = nc.dram_tensor("xr", [TPC, D], F32, kind="ExternalInput").ap()
    wq = nc.dram_tensor("wq", [D, 1024], F32, kind="ExternalInput").ap()
    wk = nc.dram_tensor("wk", [D, 1024], F32, kind="ExternalInput").ap()
    wv = nc.dram_tensor("wv", [D, 1024], F32, kind="ExternalInput").ap()
    bqr = nc.dram_tensor("bqr", [128, 8], F32, kind="ExternalInput").ap()
    bkr = nc.dram_tensor("bkr", [128, 8], F32, kind="ExternalInput").ap()
    bvr = nc.dram_tensor("bvr", [1, 1024], F32, kind="ExternalInput").ap()
    wo = nc.dram_tensor("wo", [1024, D], F32, kind="ExternalInput").ap()
    bor = nc.dram_tensor("bor", [1, D], F32, kind="ExternalInput").ap()
    wf = nc.dram_tensor("wf", [256, D], F32, kind="ExternalInput").ap()
    bfr = nc.dram_tensor("bfr", [1, D], F32, kind="ExternalInput").ap()
    eyeb = nc.dram_tensor("eyeb", [128, 128], BF16, kind="ExternalInput").ap()
    onesr = nc.dram_tensor("onesr", [1, 128], F32, kind="ExternalInput").ap()
    onesc = nc.dram_tensor("onesc", [128, 1], BF16, kind="ExternalInput").ap()
    alib = nc.dram_tensor("alib", [128, NCOL], F32, kind="ExternalInput").ap()
    maskt = nc.dram_tensor("maskt", [128, 4 * 512], F32, kind="ExternalInput").ap()
    out = nc.dram_tensor("out", [TPC, D], F32, kind="ExternalOutput").ap()

    Qs = nc.dram_tensor("Qs", [8, 128, T], BF16, kind="Internal")
    Ks = nc.dram_tensor("Ks", [8, 128, T], BF16, kind="Internal")
    Vs = nc.dram_tensor("Vs", [8, T, 128], BF16, kind="Internal")
    Ysin = nc.dram_tensor("Ysin", [1024, T], BF16, kind="Internal")
    POs = nc.dram_tensor("POs", [T, D], BF16, kind="Internal")
    POhA = nc.dram_tensor("POhA", [1024, D], BF16, kind="Internal")
    POhB = nc.dram_tensor("POhB", [1024, D], BF16, kind="Internal")
    Wfc = nc.dram_tensor("Wfc", [NCH * 128, 256], BF16, kind="Internal")
    Wfg = nc.dram_tensor("Wfg", [8 * NCH * 128, 256], BF16, kind="Internal",
                         addr_space="Shared")
    Y1s = nc.dram_tensor("Y1s", [TPC, D], F32, kind="Internal")

    p = Plan()
    ctx = ExitStack()

    def SB(scope, name, shape, dt):
        return scope.enter_context(nc.sbuf_tensor(name, shape, dt))

    eyeb_sb = SB(ctx, "eyeb_sb", [128, 128], BF16)
    onesr_sb = SB(ctx, "onesr_sb", [1, 128], F32)
    onesc_sb = SB(ctx, "onesc_sb", [128, 1], BF16)
    alib_sb = SB(ctx, "alib_sb", [128, NCOL], F32)
    bq_sb = SB(ctx, "bq_sb", [128, 8], F32)
    bk_sb = SB(ctx, "bk_sb", [128, 8], F32)
    robuf = SB(ctx, "robuf", [1, D], F32)
    bobf_bc = SB(ctx, "bobf_bc", [128, D], F32)
    amaxA = SB(ctx, "amaxA", [128, NCH, 64], F32)
    sclA = SB(ctx, "sclA", [128, NCH * 64], F32)
    rsclA = SB(ctx, "rsclA", [128, NCH * 64], F32)

    pb = [ctx.enter_context(nc.psum_tensor(f"pb{i}", [128, 512], F32))
          for i in range(8)]

    def pbbf(i):
        return pb[i][:, :].bitcast(BF16)[:, 0:512]

    ev_eye = p.dma("sync", eyeb_sb[:, :], eyeb)
    ev_onesr = p.dma("sync", onesr_sb[:, :], onesr)
    ev_onesc = p.dma("sync", onesc_sb[:, :], onesc)
    ev_alib = p.dma("sync", alib_sb[:, :], alib)
    ev_bq = p.dma("sync", bq_sb[:, :], bqr)
    ev_bk = p.dma("sync", bk_sb[:, :], bkr)
    ev_bv = p.dma("sync", robuf[0:1, 0:1024], bvr)

    s1 = ExitStack()
    wq_sb = SB(s1, "wq_sb", [128, NCH, 1024], BF16)
    wk_sb = SB(s1, "wk_sb", [128, NCH, 1024], BF16)
    wv_sb = SB(s1, "wv_sb", [128, NCH, 1024], BF16)
    xb_sb = [SB(s1, f"xb{i}", [128, 4, D], BF16) for i in range(2)]
    xfm_sb = SB(s1, "xfm", [128, NCH, 512], BF16)
    qkst = [SB(s1, f"qkst{i}", [128, 512], BF16) for i in range(2)]
    vst = [SB(s1, f"vst{i}", [128, 512], BF16) for i in range(2)]
    wt_sb = [SB(s1, f"wt{i}", [128, 512], F32) for i in range(2)]
    amaxW = SB(s1, "amaxW", [128, 8, 16], F32)
    sclW = SB(s1, "sclW", [128, 128], F32)
    rsclW = SB(s1, "rsclW", [128, 128], F32)
    t16W = SB(s1, "t16W", [128, 128], F32)
    wscr = dict(
        q=SB(s1, "w_q", [128, 512], F32)[:, :],
        aq=SB(s1, "w_aq", [128, 512], F32)[:, :],
        sgn=SB(s1, "w_sgn", [128, 512], BF16)[:, :],
        tC=SB(s1, "w_tC", [128, 512], F32)[:, :],
        s2=SB(s1, "w_s2", [128, 512], BF16)[:, :],
    )
    wq_tm = SB(s1, "wq_tm", [128, 512], BF16)
    wftst = [SB(s1, f"wftst{i}", [128, 4, 128], BF16) for i in range(2)]

    wstate = {"wt": [None, None], "wt_used": [None, None], "ladder": None,
              "frees": {}, "evft": None, "evft_store": [None, None],
              "amax_done": [], "stores": []}

    def w_unit_src(u):
        r, cc = u % 2, u // 2
        return wf[r * 128:(r + 1) * 128, cc * 512:(cc + 1) * 512]

    def w_pass1_pair(g):
        for u in (2 * g, 2 * g + 1):
            sl = u % 2
            wstate["wt"][sl] = p.dma(
                "scalar", wt_sb[sl][:, :], w_unit_src(u),
                [wstate["wt"][sl], wstate["wt_used"][sl]])
            wstate["amax_done"].append(p.op(
                "vector",
                lambda e, o=amaxW[:, u, :],
                i=wt_sb[sl][:, :].rearrange("p (g w) -> p g w", w=32):
                e.reduce_max(o, i, axis=AX.X, apply_absolute_value=True),
                [wstate["wt"][sl]], inc="ve"))
            wstate["wt_used"][sl] = wstate["amax_done"][-1]

    def w_pass2b(u):
        sl = u % 2
        wstate["wt"][sl] = p.dma(
            "scalar", wt_sb[sl][:, :], w_unit_src(u),
            [wstate["wt"][sl], wstate["wt_used"][sl]])
        evq, wstate["frees"] = _emit_round(
            p, wt_sb[sl][:, :], wq_tm[:, :],
            sclW[:, u * 16:(u + 1) * 16], rsclW[:, u * 16:(u + 1) * 16],
            wscr, wstate["frees"],
            [wstate["wt"][sl]] + list(wstate["ladder"]))
        wstate["wt_used"][sl] = wstate["frees"]["q"]
        r, cc = u % 2, u // 2
        tb = u % 2
        evt = None
        for kk in range(4):
            evt = p.op(
                "tensor",
                lambda e, o=pbbf(7)[:, kk * 128:(kk + 1) * 128],
                i=wq_tm[:, kk * 128:(kk + 1) * 128],
                idn=eyeb_sb[:, :]: e.transpose(o, i, idn),
                [evq, ev_eye, wstate["evft"]],
                inc="pe" if kk == 3 else None)
        wstate["evft"] = p.op(
            "scalar",
            lambda e, o=wftst[tb][:, :, :].rearrange("p a b -> p (a b)"),
            i=pbbf(7)[:, :]: e.activation(o, i, AF.Copy),
            [evt, wstate["evft_store"][tb]], inc="ae")
        wstate["evft_store"][tb] = p.dma(
            "scalar",
            Wfc.ap().rearrange("(c p) n -> c p n", p=128)
            [cc * 4:(cc + 1) * 4, :, r * 128:(r + 1) * 128]
            .rearrange("c k n -> k c n"),
            wftst[tb][:, :, :],
            [wstate["evft"]])
        wstate["stores"].append(wstate["evft_store"][tb])
        wstate["frees"]["out"] = evt


    ev_wq = p.dma("gpsimd", wq_sb[:, :, :],
                  wq.rearrange("(c p) f -> p c f", p=128))
    ev_wk = p.dma("gpsimd", wk_sb[:, :, :],
                  wk.rearrange("(c p) f -> p c f", p=128))
    ev_wv = p.dma("gpsimd", wv_sb[:, :, :],
                  wv.rearrange("(c p) f -> p c f", p=128))

    ev_xb = [None, None]
    ev_xb_used = [None, None]
    ev_tr_evac = [None, None]
    ev_qk_evac = [None, None]
    ev_v_evac = [None, None]
    ev_qkst_store = [None, None]
    ev_vst_store = [None, None]
    ev_qs_store = {}
    ev_ks_store = {}

    for grp in range(8):
        t0 = grp * 512
        par = grp % 2
        ev_xb[par] = p.dma(
            "gpsimd", xb_sb[par][:, :, :],
            x[t0:t0 + 512, :].rearrange("(n p) f -> p n f", p=128),
            [ev_xb_used[par]])
        evtr = None
        for c in range(NCH):
            tb = c % 2
            for tt in range(4):
                evtr = p.op(
                    "tensor",
                    lambda e, o=pbbf(tb)[:, tt * 128:(tt + 1) * 128],
                    i=xb_sb[par][:, tt, c * 128:(c + 1) * 128],
                    idn=eyeb_sb[:, :]: e.transpose(o, i, idn),
                    [ev_xb[par], ev_eye, ev_tr_evac[tb]],
                    inc="pe" if tt == 3 else None)
            ev_tr_evac[tb] = p.op(
                "scalar", lambda e, o=xfm_sb[:, c, :], i=pbbf(tb)[:, :]:
                e.activation(o, i, AF.Copy), [evtr], inc="ae")
        ev_xb_used[par] = evtr
        ev_xfm = ev_tr_evac[(NCH - 1) % 2]
        ev_xfm2 = ev_tr_evac[NCH % 2]

        for kind in ("k", "q"):
            wsb = wq_sb if kind == "q" else wk_sb
            ev_w = ev_wq if kind == "q" else ev_wk
            bias_sb = bq_sb if kind == "q" else bk_sb
            dst = Qs if kind == "q" else Ks
            for fc in range(8):
                bank = 2 + (fc % 2)
                evm = None
                for kc in range(NCH):
                    evm = p.op(
                        "tensor",
                        lambda e, o=pb[bank][:, :],
                        l=wsb[:, kc, fc * 128:(fc + 1) * 128],
                        r=xfm_sb[:, kc, :], st=(kc == 0), sp=(kc == NCH - 1):
                        e.matmul(o, lhsT=l, rhs=r, start=st, stop=sp),
                        [ev_w, ev_xfm, ev_xfm2, ev_qk_evac[fc % 2]],
                        inc="pe" if kc == NCH - 1 else None)
                sp_ = fc % 2
                ev_qk_evac[sp_] = p.op(
                    "scalar",
                    lambda e, o=qkst[sp_][:, :], i=pb[bank][:, :],
                    bi=bias_sb[:, fc:fc + 1]:
                    e.activation(o, i, AF.Identity, bias=bi),
                    [evm, ev_qkst_store[sp_]], inc="ae")
                ev_qkst_store[sp_] = p.dma(
                    "sync", dst.ap()[fc, :, t0:t0 + 512], qkst[sp_][:, :],
                    [ev_qk_evac[sp_]])
                (ev_qs_store if kind == "q" else ev_ks_store)[fc] = \
                    ev_qkst_store[sp_]

        for ts4 in range(4):
            for fg in range(2):
                sp_ = (ts4 * 2 + fg) % 2
                bank = 4 + sp_
                for kc in range(NCH):
                    p.op("tensor",
                         lambda e, o=pb[bank][:, :],
                         l=xfm_sb[:, kc, ts4 * 128:(ts4 + 1) * 128],
                         r=wv_sb[:, kc, fg * 512:(fg + 1) * 512],
                         st=(kc == 0):
                         e.matmul(o, lhsT=l, rhs=r, start=st, stop=False),
                         [ev_wv, ev_xfm, ev_xfm2, ev_v_evac[sp_]],
                         inc=None)
                evm = p.op(
                    "tensor",
                    lambda e, o=pb[bank][:, :], l=onesr_sb[:, :],
                    r=robuf[0:1, fg * 512 + 512 * 0:fg * 512 + 512]:
                    e.matmul(o, lhsT=l, rhs=r, start=False, stop=True),
                    [ev_bv, ev_onesr], inc="pe")
                ev_robuf_free = evm
                ev_v_evac[sp_] = p.op(
                    "scalar",
                    lambda e, o=vst[sp_][:, :], i=pb[bank][:, :]:
                    e.activation(o, i, AF.Copy),
                    [evm, ev_vst_store[sp_]], inc="ae")
                ev_vst_store[sp_] = p.dma(
                    "sync",
                    Vs.ap()[fg * 4:(fg + 1) * 4,
                            t0 + ts4 * 128:t0 + (ts4 + 1) * 128, :]
                    .rearrange("j t d -> t j d"),
                    vst[sp_][:, :].rearrange("p (j d) -> p j d", d=128),
                    [ev_v_evac[sp_]])

        if grp < 4:
            w_pass1_pair(grp)
            if grp == 3:
                wstate["ladder"] = _emit_scale_ladder(
                    p, amaxW[:, :, :].rearrange("p a b -> p (a b)"),
                    sclW[:, :], rsclW[:, :], t16W[:, :],
                    wstate["amax_done"], W_EMIN, W_EMAX)
        else:
            for u in (2 * (grp - 4), 2 * (grp - 4) + 1):
                w_pass2b(u)

    s1_end = p.snap()
    s1.close()

    s2keep = ExitStack()
    wo_sb = SB(s2keep, "wo_sb", [128, 8, D], BF16)
    s2 = ExitStack()
    maskt_sb = SB(s2, "maskt_sb", [128, 4, 512], F32)
    khead = [SB(s2, f"khead{i}", [128, T], BF16) for i in range(2)]
    vhead = [SB(s2, f"vhead{i}", [128, 32, 128], BF16) for i in range(2)]
    qhead = [SB(s2, f"qhead{i}", [128, T], BF16) for i in range(2)]
    expb = [SB(s2, f"expb{i}", [128, 512], BF16) for i in range(6)]
    dn_sb = [SB(s2, f"dn_sb{i}", [1, 512], F32) for i in range(2)]
    rec_sb = [SB(s2, f"rec_sb{i}", [1, 512], F32) for i in range(2)]
    bcm_sb = [SB(s2, f"bcm_sb{i}", [128, 512], F32) for i in range(2)]
    yst = [SB(s2, f"yst{i}", [128, 512], BF16) for i in range(2)]
    ev_mask = p.dma("sync", maskt_sb[:, :, :].rearrange("p a b -> p (a b)"),
                    maskt, s1_end)
    ev_wo = None
    for sl in range(2):
        waits = list(s1_end) if sl == 0 else []
        ev_wo = p.dma(
            "gpsimd", wo_sb[:, sl * 4:(sl + 1) * 4, :],
            wo[sl * 512:(sl + 1) * 512, :].rearrange("(c p) f -> p c f", p=128),
            waits + [ev_wo])
    ev_wo_done = ev_wo

    ev_kh = [None, None]
    ev_vh = [None, None]
    ev_qh = [None, None]
    last_pe_head = [None, None]
    exp_free = [None] * 6
    S_free = {0: None, 1: None, 5: None}
    pv_free = [None, None]
    dn_free = [None, None]
    rec_free = [None, None]
    dnsb_free = [None, None]
    bcm_free = [None, None]
    yst_free = [None, None]
    ygl = 0
    gti = [0]
    bcst = {"bank": None, "yp": 0}
    SBANKS = (0, 1, 5)

    for j in range(8):
        hp = j % 2
        w0 = s1_end if j < 2 else []
        ev_qh[hp] = p.dma("sync", qhead[hp][:, :], Qs.ap()[j, :, :],
                          w0 + [ev_qs_store[j], last_pe_head[hp], ev_qh[hp]])
        ev_kh[hp] = p.dma("sync", khead[hp][:, :], Ks.ap()[j, :, :],
                          w0 + [ev_ks_store[j], last_pe_head[hp], ev_kh[hp]])
        ev_vh[hp] = p.dma("sync", vhead[hp][:, :, :],
                          Vs.ap()[j, :, :].rearrange("(a p) d -> p a d", p=128),
                          w0 + [ev_vst_store[0], ev_vst_store[1],
                                last_pe_head[hp], ev_vh[hp]])
        G, qgs = PLANS[j]
        pend = []

        def pop_one():
            fn_, i_, xp_, ev_, epi_ = pend.pop(0)
            pv_ = fn_(i_, xp_, ev_)
            exp_free[xp_] = pv_
            if epi_ is not None:
                epi_(pv_)

        for (q0, ref, kts) in qgs:
            qp = ygl % 2
            pvbank = 2 + qp
            dnrow = 32 * qp
            nt = len(kts)

            def emit_dnpv(i, xp, evexp, qp=qp, dnrow=dnrow, pvbank=pvbank,
                          G=G, kts=kts, nt=nt, hp=hp):
                first, last = (i == 0), (i == nt - 1)
                p.op("tensor",
                     lambda e, o=pb[4][dnrow:dnrow + 1, 0:G], l=onesc_sb[:, :],
                     r=expb[xp][:, 0:G], st=first, sp=last:
                     e.matmul(o, lhsT=l, rhs=r, start=st, stop=sp),
                     [evexp, ev_onesc] + ([dn_free[qp]] if first else []),
                     inc=None)
                kloc = kts[i][0] // 128
                return p.op(
                    "tensor",
                    lambda e, o=pb[pvbank][:, 0:G],
                    l=vhead[hp][:, kloc, :], r=expb[xp][:, 0:G],
                    st=first, sp=last:
                    e.matmul(o, lhsT=l, rhs=r, start=st, stop=sp),
                    [evexp, ev_vh[hp]] + ([pv_free[qp]] if first else []),
                    inc="pe")

            def epilogue(ev_pvlast, qp=qp, dnrow=dnrow, pvbank=pvbank,
                         G=G, q0=q0, j=j, hp=hp):
                ev_dnev = p.op("scalar",
                               lambda e, o=dn_sb[qp][0:1, 0:G],
                               i=pb[4][dnrow:dnrow + 1, 0:G]:
                               e.activation(o, i, AF.Copy),
                               [ev_pvlast, dnsb_free[qp]], inc="ae")
                dn_free[qp] = ev_dnev
                ev_rec = p.op("vector",
                              lambda e, o=rec_sb[qp][0:1, 0:G],
                              i=dn_sb[qp][0:1, 0:G]:
                              e.reciprocal(o, i), [ev_dnev, rec_free[qp]],
                              inc="ve")
                dnsb_free[qp] = ev_rec
                ev_bcmm = p.op("tensor",
                               lambda e, o=pb[6][:, 0:G], l=onesr_sb[:, :],
                               r=rec_sb[qp][0:1, 0:G]:
                               e.matmul(o, lhsT=l, rhs=r, start=True,
                                        stop=True),
                               [ev_rec, ev_onesr, bcst["bank"]], inc="pe")
                rec_free[qp] = ev_bcmm
                ev_bcev = p.op("scalar",
                               lambda e, o=bcm_sb[qp][:, 0:G], i=pb[6][:, 0:G]:
                               e.activation(o, i, AF.Copy),
                               [ev_bcmm, bcm_free[qp]], inc="ae")
                bcst["bank"] = ev_bcev
                yp = bcst["yp"]
                bcst["yp"] = 1 - yp
                ev_ym = p.op("vector",
                             lambda e, o=yst[yp][:, 0:G], i=pb[pvbank][:, 0:G],
                             b_=bcm_sb[qp][:, 0:G]:
                             e.tensor_tensor(o, i, b_, ALU.mult),
                             [ev_bcev, ev_pvlast, yst_free[yp]], inc="ve")
                pv_free[qp] = ev_ym
                bcm_free[qp] = ev_ym
                yst_free[yp] = p.dma(
                    "scalar",
                    Ysin.ap()[j * 128:(j + 1) * 128, q0:q0 + G],
                    yst[yp][:, 0:G], [ev_ym])
                last_pe_head[hp] = ev_ym

            for i, (k0, colidx, mi) in enumerate(kts):
                sp_ = SBANKS[gti[0] % 3]
                xp = gti[0] % 6
                gti[0] += 1
                evS = p.op(
                    "tensor",
                    lambda e, o=pb[sp_][:, 0:G],
                    l=khead[hp][:, k0:k0 + 128],
                    r=qhead[hp][:, q0:q0 + G]:
                    e.matmul(o, lhsT=l, rhs=r, start=True, stop=True),
                    [ev_kh[hp], ev_qh[hp], S_free[sp_]], inc="pe")
                if mi is not None:
                    evS = p.op(
                        "vector",
                        lambda e, o=pb[sp_][:, 0:G], m=maskt_sb[:, mi, 0:G]:
                        e.tensor_tensor(o, o, m, ALU.add),
                        [evS, ev_mask], inc="ve")
                evexp = p.op(
                    "scalar",
                    lambda e, o=expb[xp][:, 0:G], i=pb[sp_][:, 0:G],
                    bi=alib_sb[:, colidx:colidx + 1]:
                    e.activation(o, i, AF.Exp, bias=bi, scale=QSCALE),
                    [evS, ev_alib, exp_free[xp]], inc="ae")
                S_free[sp_] = evexp
                pend.append((emit_dnpv, i, xp, evexp,
                             epilogue if i == nt - 1 else None))
                if len(pend) > 4:
                    pop_one()
            ygl += 1
        while pend:
            pop_one()
        if j == 1:
            ev_wcc = p.op("gpsimd",
                          lambda e: e.collective_compute(
                              "AllGather", ALU.bypass,
                              replica_groups=[list(range(8))],
                              ins=[Wfc.ap().opt()], outs=[Wfg.ap().opt()]),
                          wstate["stores"], inc="cc")

    s2_all = p.snap()

    s25 = ExitStack()
    ylhs = [SB(s25, f"ylhs{i}", [128, 8, 128], BF16) for i in range(4)]
    post = [SB(s25, f"post{i}", [128, D], BF16) for i in range(2)]

    ev_yl = [None] * 4
    ev_yl_used = [None] * 4
    ev_post_store = [None, None]
    bank_free3 = [None, None, None, None]
    TTORD = list(range(0, 8)) + list(range(16, 24)) +         list(range(8, 16)) + list(range(24, 32))
    GORD = [0, 2, 1, 3]
    ev_cc_a = None
    for it, tt in enumerate(TTORD):
        tp = it % 4
        pp_ = it % 2
        tg = tt * 128
        rowoff = GORD.index(tt // 8) * 1024 + (tt % 8) * 128
        ev_yl[tp] = p.dma(
            "sync" if it % 2 == 0 else "scalar", ylhs[tp][:, :, :],
            Ysin.ap()[:, tg:tg + 128].rearrange("(c p) t -> p c t", p=128),
            (s2_all if it < 4 else []) + [ev_yl_used[tp]])
        evmm = [None] * 4
        for dc in range(8):
            for ng in range(4):
                evmm[ng] = p.op(
                    "tensor",
                    lambda e, o=pb[ng][:, :], l=ylhs[tp][:, dc, :],
                    r=wo_sb[:, dc, ng * 512:(ng + 1) * 512],
                    st=(dc == 0), sp=(dc == 7):
                    e.matmul(o, lhsT=l, rhs=r, start=st, stop=sp),
                    (s2_all if it < 2 else []) +
                    [ev_yl[tp], ev_wo_done, bank_free3[ng]],
                    inc="pe" if dc == 7 else None)
        ev_yl_used[tp] = evmm[3]
        evev = [None] * 4
        for ng in range(4):
            evev[ng] = p.op(
                "scalar",
                lambda e, o=post[pp_][:, ng * 512:(ng + 1) * 512],
                i=pb[ng][:, :]: e.activation(o, i, AF.Copy),
                [evmm[ng], ev_post_store[pp_]], inc="ae")
            bank_free3[ng] = evev[ng]
        ev_post_store[pp_] = p.dma(
            "sync", POs.ap()[rowoff:rowoff + 128, :], post[pp_][:, :],
            [evev[0], evev[1], evev[2], evev[3]])
        if it == 15:
            half_a = p.snap()
            ev_cc_a = p.op("gpsimd",
                           lambda e: e.collective_compute(
                               "ReduceScatter", ALU.add,
                               replica_groups=[[0, 1], [2, 3], [4, 5], [6, 7]],
                               ins=[POs.ap()[0:2048, :].opt()],
                               outs=[POhA.ap().opt()]),
                           half_a, inc="cc")

    s25_all = p.snap()
    s25.close()
    ev_cc_b = p.op("gpsimd",
                   lambda e: e.collective_compute(
                       "ReduceScatter", ALU.add,
                       replica_groups=[[0, 1], [2, 3], [4, 5], [6, 7]],
                       ins=[POs.ap()[2048:4096, :].opt()],
                       outs=[POhB.ap().opt()]),
                   s25_all, inc="cc")
    s2.close()

    s3 = ExitStack()
    potile = [SB(s3, f"pot{i}", [128, D], BF16) for i in range(2)]
    xres = [SB(s3, f"xres{i}", [128, D], F32) for i in range(2)]
    xpbo_sb = SB(s3, "xpbo", [128, D], F32)
    r_sb = SB(s3, "r_sb", [128, D], F32)
    c_sb = SB(s3, "c_sb", [128, D], F32)
    m1_sb = SB(s3, "m1_sb", [128, 1], F32)
    m_sb = SB(s3, "m_sb", [128, 1], F32)
    v1_sb = SB(s3, "v1_sb", [128, 1], F32)
    rstd_sb = SB(s3, "rstd_sb", [128, 1], F32)
    y1st = [SB(s3, f"y1st{i}", [128, D], F32) for i in range(2)]

    ev_bo = p.dma("sync", robuf[0:1, :], bor, [ev_robuf_free] + s2_all)
    ev_bobc = None
    for sl in range(4):
        evm = p.op("tensor", lambda e, o=pb[4][:, :], l=onesr_sb[:, :],
                   r=robuf[0:1, sl * 512:(sl + 1) * 512]:
                   e.matmul(o, lhsT=l, rhs=r, start=True, stop=True),
                   s25_all + [ev_bo, ev_bobc], inc="pe")
        ev_bobc = p.op("scalar",
                       lambda e, o=bobf_bc[:, sl * 512:(sl + 1) * 512],
                       i=pb[4][:, :]: e.activation(o, i, AF.Copy),
                       [evm], inc="ae")
    ev_robuf_free = ev_bobc

    ev_po = [None, None]
    ev_po_used = [None, None]
    ev_xr = [None, None]
    ev_xr_used = [None, None]
    ev_xpbo_used = None
    ev_r_used = None
    ev_c_used = None
    ev_y1_store = [None, None]
    ev_amax3 = [None, None]

    for tt in range(16):
        tp = tt % 2
        tloc = tt * 128
        posrc = (POhA.ap()[tloc:tloc + 128, :] if tt < 8
                 else POhB.ap()[tloc - 1024:tloc - 1024 + 128, :])
        ev_po[tp] = p.dma("sync", potile[tp][:, :], posrc,
                          [ev_cc_a if tt < 8 else ev_cc_b, ev_po_used[tp]])
        ev_xr[tp] = p.dma("scalar", xres[tp][:, :], xr[tloc:tloc + 128, :],
                          (s25_all if tt < 2 else []) + [ev_xr_used[tp]])
        ev_xpbo = p.op("gpsimd",
                       lambda e, o=xpbo_sb[:, :], i=xres[tp][:, :],
                       b_=bobf_bc[:, :]: e.tensor_tensor(o, i, b_, ALU.add),
                       [ev_xr[tp], ev_bobc, ev_xpbo_used], inc="po")
        ev_xr_used[tp] = ev_xpbo
        evr = p.op("vector",
                   lambda e, o=r_sb[:, :], i=potile[tp][:, :],
                   b_=xpbo_sb[:, :]: e.tensor_tensor(o, i, b_, ALU.add),
                   [ev_po[tp], ev_xpbo, ev_r_used], inc="ve")
        ev_po_used[tp] = evr
        ev_xpbo_used = evr
        evm1 = p.op("vector", lambda e, o=m1_sb[:, :], i=r_sb[:, :]:
                    e.reduce_sum(o, i, axis=AX.X), [evr], inc="ve")
        evm = p.op("vector", lambda e, o=m_sb[:, :], i=m1_sb[:, :]:
                   e.tensor_scalar(o, i, -1.0 / D, None, ALU.mult),
                   [evm1], inc="ve")
        evc = p.op("scalar", lambda e, o=c_sb[:, :], i=r_sb[:, :],
                   m=m_sb[:, 0:1]:
                   e.activation(o, i, AF.Identity, bias=m),
                   [evm, ev_c_used], inc="ae")
        ev_r_used = evc
        evsq = p.op("scalar", lambda e, o=r_sb[:, :], i=c_sb[:, :],
                    a_=v1_sb[:, 0:1]:
                    e.activation(o, i, AF.Square, accum_out=a_),
                    [evc], inc="ae")
        ev_r_used = evsq
        evve = p.op("vector", lambda e, o=v1_sb[:, :]:
                    e.tensor_scalar(o, o, 1.0 / D, LNEPS, ALU.mult, ALU.add),
                    [evsq], inc="ve")
        evln = p.op("scalar", lambda e, o=rstd_sb[:, :], i=v1_sb[:, :]:
                    e.activation(o, i, AF.Ln), [evve], inc="ae")
        evex = p.op("scalar", lambda e, o=rstd_sb[:, :], i=rstd_sb[:, :]:
                    e.activation(o, i, AF.Exp, scale=-0.5), [evln], inc="ae")
        evy1 = p.op("scalar", lambda e, o=y1st[tp][:, :], i=c_sb[:, :],
                    m=rstd_sb[:, 0:1]:
                    e.activation(o, i, AF.Identity, scale=m),
                    [evex, ev_y1_store[tp], ev_amax3[tp]], inc="ae")
        ev_c_used = evy1
        ev_amax3[tp] = p.op(
            "vector",
            lambda e, o=amaxA[:, tt, :],
            i=y1st[tp][:, :].rearrange("p (g w) -> p g w", w=32):
            e.reduce_max(o, i, axis=AX.X, apply_absolute_value=True),
            [evy1], inc="ve")
        ev_y1_store[tp] = p.dma("sync", Y1s.ap()[tloc:tloc + 128, :],
                                y1st[tp][:, :], [evy1])

    s3_end = p.snap()
    s3.close()
    s2keep.close()

    s4 = ExitStack()
    wft_sb = SB(s4, "wft_sb", [128, NCH, D], BF16)
    y1l = [SB(s4, f"y1l{i}", [128, D], F32) for i in range(3)]
    t16A = SB(s4, "t16A", [128, NCH * 64], F32)
    ascr = dict(
        q=SB(s4, "a_q", [128, D], F32)[:, :],
        aq=SB(s4, "a_aq", [128, D], F32)[:, :],
        sgn=SB(s4, "a_sgn", [128, D], BF16)[:, :],
        tC=SB(s4, "a_tC", [128, D], F32)[:, :],
        s2=SB(s4, "a_s2", [128, D], BF16)[:, :],
    )
    aq_tm = [SB(s4, f"aq_tm{i}", [128, D], BF16) for i in range(2)]
    afm_sb = [SB(s4, f"afm{i}", [128, NCH, 128], BF16) for i in range(2)]
    r4_sb = SB(s4, "r4_sb", [128, D], F32)
    ffnout_sb = SB(s4, "ffnout", [128, D], F32)
    c4_sb = SB(s4, "c4_sb", [128, D], F32)
    m14_sb = SB(s4, "m14_sb", [128, 1], F32)
    m4_sb = SB(s4, "m4_sb", [128, 1], F32)
    v14_sb = SB(s4, "v14_sb", [128, 1], F32)
    rstd4_sb = SB(s4, "rstd4_sb", [128, 1], F32)
    outst = SB(s4, "outst", [128, D], F32)

    ev_ladA = _emit_scale_ladder(
        p, amaxA[:, :, :].rearrange("p a b -> p (a b)"),
        sclA[:, :], rsclA[:, :], t16A[:, :],
        s3_end, A_EMIN, A_EMAX)

    ev_y1l = [None, None, None]
    for tt in range(3):
        ev_y1l[tt] = p.dma("scalar", y1l[tt][:, :],
                           Y1s.ap()[tt * 128:tt * 128 + 128, :],
                           s3_end + [ev_y1_store[tt % 2]])

    ev_bf = p.dma("sync", robuf[0:1, :], bfr, [ev_robuf_free] + s3_end)

    ev_wft_r = [None] * 8
    prevq = {"sync": None, "scalar": None}
    for r_ in range(8):
        qn = "sync" if r_ % 2 == 0 else "scalar"
        ev_wft_r[r_] = p.dma(
            qn, wft_sb[:, :, r_ * 256:(r_ + 1) * 256],
            Wfg.ap()[r_ * 2048:(r_ + 1) * 2048, :]
            .rearrange("(c p) n -> p c n", p=128),
            (s3_end if r_ < 2 else []) + [ev_wcc, prevq[qn]])
        prevq[qn] = ev_wft_r[r_]

    afree = {}
    ev_y1l_used = [None, None, None]
    ev_aqtm_used = [None, None]
    ev_afm_evac = [None, None]
    ev_afm_used = [None, None]
    bank_free4 = [None, None, None, None]
    ev_out_store = [None]
    st4 = {"r4_used": None, "c4_used": None, "evmm": None, "ffn_used": None}

    def tail_a(tt):
        evcp = None
        for ng in range(4):
            evcp = p.op("scalar",
                        lambda e, o=ffnout_sb[:, ng * 512:(ng + 1) * 512],
                        i=pb[ng][:, :]: e.activation(o, i, AF.Copy),
                        [st4["evmm"][ng], st4["ffn_used"]], inc="ae")
            bank_free4[ng] = evcp
        st4["evcp"] = evcp

    def tail_b(tt):
        tp = tt % 3
        op_ = tt % 2
        tloc = tt * 128
        evr = p.op("gpsimd",
                   lambda e, o=r4_sb[:, :], i0=ffnout_sb[:, :],
                   i1=y1l[tp][:, :]: e.tensor_tensor(o, i0, i1, ALU.add),
                   [st4["evcp"], st4["r4_used"]], inc="po")
        st4["ffn_used"] = evr
        ev_y1l_used[tp] = evr
        st4["r4_used"] = evr
        evmean = p.op("scalar",
                      lambda e, o=c4_sb[:, :], i=r4_sb[:, :],
                      a_=m14_sb[:, 0:1]:
                      e.activation(o, i, AF.Copy, accum_out=a_),
                      [evr, st4["c4_used"]], inc="ae")
        evm = p.op("vector", lambda e, o=m4_sb[:, :], i=m14_sb[:, :]:
                   e.tensor_scalar(o, i, -1.0 / D, None, ALU.mult),
                   [evmean], inc="ve")
        evc = p.op("scalar", lambda e, o=c4_sb[:, :], i=r4_sb[:, :],
                   m=m4_sb[:, 0:1]:
                   e.activation(o, i, AF.Identity, bias=m),
                   [evm], inc="ae")
        evsq = p.op("scalar", lambda e, o=r4_sb[:, :], i=c4_sb[:, :],
                    a_=v14_sb[:, 0:1]:
                    e.activation(o, i, AF.Square, accum_out=a_),
                    [evc], inc="ae")
        st4["r4_used"] = evsq
        evve4 = p.op("vector", lambda e, o=v14_sb[:, :]:
                     e.tensor_scalar(o, o, 1.0 / D, LNEPS, ALU.mult, ALU.add),
                     [evsq], inc="ve")
        evln = p.op("scalar", lambda e, o=rstd4_sb[:, :], i=v14_sb[:, :]:
                    e.activation(o, i, AF.Ln), [evve4], inc="ae")
        evex = p.op("scalar", lambda e, o=rstd4_sb[:, :], i=rstd4_sb[:, :]:
                    e.activation(o, i, AF.Exp, scale=-0.5), [evln], inc="ae")
        evout = p.op("scalar", lambda e, o=outst[:, :], i=c4_sb[:, :],
                     m=rstd4_sb[:, 0:1]:
                     e.activation(o, i, AF.Identity, scale=m),
                     [evex, ev_out_store[0]], inc="ae")
        st4["c4_used"] = evout
        ev_out_store[0] = p.dma("sync", out[tloc:tloc + 128, :],
                                outst[:, :], [evout])

    for tt in range(16):
        tp = tt % 3
        ap_ = tt % 2
        evq, afree = _emit_round(
            p, y1l[tp][:, :], aq_tm[ap_][:, :],
            sclA[:, tt * 64:(tt + 1) * 64], rsclA[:, tt * 64:(tt + 1) * 64],
            ascr, dict(afree, out=ev_aqtm_used[ap_]),
            [ev_y1l[tp]] + list(ev_ladA), pool=True)
        if tt >= 1:
            tail_a(tt - 1)
        evt = None
        for kq in range(4):
            tb = kq % 2
            for kk in range(4):
                kc = kq * 4 + kk
                evt = p.op(
                    "tensor",
                    lambda e, o=pbbf(4 + tb)[:, kk * 128:(kk + 1) * 128],
                    i=aq_tm[ap_][:, kc * 128:(kc + 1) * 128],
                    idn=eyeb_sb[:, :]: e.transpose(o, i, idn),
                    [evq, ev_eye, ev_afm_evac[tb]],
                    inc="pe" if kk == 3 else None)
            ev_afm_evac[tb] = p.op(
                "scalar",
                lambda e, o=afm_sb[ap_][:, kq * 4:(kq + 1) * 4, :]
                .rearrange("p a b -> p (a b)"),
                i=pbbf(4 + tb)[:, :]: e.activation(o, i, AF.Copy),
                [evt, ev_afm_used[ap_] if kq < 2 else None], inc="ae")
        ev_aqtm_used[ap_] = evt
        afree["out"] = evt
        ev_afm = [ev_afm_evac[0], ev_afm_evac[1]]
        for kc in range(NCH):
            for ng in range(4):
                w0 = ([ev_wft_r[2 * ng], ev_wft_r[2 * ng + 1],
                       bank_free4[ng]] if kc == 0 else [])
                p.op("tensor",
                     lambda e, o=pb[ng][:, :], l=afm_sb[ap_][:, kc, :],
                     r=wft_sb[:, kc, ng * 512:(ng + 1) * 512],
                     st=(kc == 0):
                     e.matmul(o, lhsT=l, rhs=r, start=st, stop=False),
                     ev_afm + w0, inc=None)
        evmm = [None] * 4
        for ng in range(4):
            evmm[ng] = p.op(
                "tensor",
                lambda e, o=pb[ng][:, :], l=onesr_sb[:, :],
                r=robuf[0:1, ng * 512:(ng + 1) * 512]:
                e.matmul(o, lhsT=l, rhs=r, start=False, stop=True),
                [ev_bf], inc="pe")
        ev_afm_used[ap_] = evmm[3]
        st4["evmm"] = evmm
        if tt >= 1:
            tail_b(tt - 1)
            if tt + 2 < 16:
                ev_y1l[(tt + 2) % 3] = p.dma(
                    "scalar", y1l[(tt + 2) % 3][:, :],
                    Y1s.ap()[(tt + 2) * 128:(tt + 2) * 128 + 128, :],
                    [ev_y1_store[tt % 2], ev_y1l_used[(tt + 2) % 3]])
    tail_a(15)
    tail_b(15)

    dbg = []
    if debug:
        for name, th, shape, dt in (
                ("dbg_qs", Qs, [8, 128, T], BF16),
                ("dbg_ks", Ks, [8, 128, T], BF16),
                ("dbg_vs", Vs, [8, T, 128], BF16),
                ("dbg_ysin", Ysin, [1024, T], BF16),
                ("dbg_poha", POhA, [1024, D], BF16),
                ("dbg_pohb", POhB, [1024, D], BF16),
                ("dbg_pos", POs, [T, D], BF16),
                ("dbg_wfg", Wfg, [8 * NCH * 128, 256], BF16),
                ("dbg_y1", Y1s, [TPC, D], F32)):
            dap = nc.dram_tensor(name, shape, dt, kind="ExternalOutput").ap()
            p.dma("sync", dap, th.ap(), p.snap())
            dbg.append(name)

    fin = p.snap()
    p.op("sync", lambda e: e.nop(), fin)

    with ExitStack() as ectx:
        sems = {name: ectx.enter_context(nc.semaphore(name))
                for name in ("qd", "ad", "gd", "pe", "ve", "ae", "po", "cc")}
        block = ectx.enter_context(nc.Block())

        def emitter(engname):
            def run(eng):
                lastw = {}
                for (en, w, fn, inc, n) in p.ops:
                    if en != engname:
                        continue
                    for s, v in w.items():
                        if lastw.get(s, -1) >= v:
                            continue
                        eng.wait_ge(sems[s], v)
                        lastw[s] = v
                    inst = fn(eng)
                    if inc is not None:
                        inst.then_inc(sems[inc], n)
            return run

        block.sync(emitter("sync"))
        block.gpsimd(emitter("gpsimd"))
        block.tensor(emitter("tensor"))
        block.vector(emitter("vector"))
        block.scalar(emitter("scalar"))

    s4.close()
    ctx.close()
    return nc, dbg


def _host_inputs(inputs, c):
    b, hh = c >> 1, c & 1
    heads = SET0 if hh == 0 else SET1
    cols = np.concatenate([np.arange(h * 128, (h + 1) * 128) for h in heads])
    f32 = np.float32
    xb = np.ascontiguousarray(np.asarray(inputs["L0_attn_x"][b], dtype=f32))
    d = {
        "x": xb,
        "xr": np.ascontiguousarray(xb[hh * TPC:(hh + 1) * TPC, :]),
        "wq": np.ascontiguousarray(np.asarray(inputs["L0_attn_Wq"], f32)[:, cols]),
        "wk": np.ascontiguousarray(np.asarray(inputs["L0_attn_Wk"], f32)[:, cols]),
        "wv": np.ascontiguousarray(np.asarray(inputs["L0_attn_Wv"], f32)[:, cols]),
        "bqr": np.ascontiguousarray(
            np.asarray(inputs["L0_attn_bq"], f32)[cols].reshape(8, 128).T),
        "bkr": np.ascontiguousarray(
            np.asarray(inputs["L0_attn_bk"], f32)[cols].reshape(8, 128).T),
        "bvr": np.ascontiguousarray(
            np.asarray(inputs["L0_attn_bv"], f32)[cols].reshape(1, 1024)),
        "wo": np.ascontiguousarray(np.asarray(inputs["L0_attn_Wo"], f32)[cols, :]),
        "bor": np.ascontiguousarray(
            np.asarray(inputs["L0_attn_bo"], f32).reshape(1, D)),
        "wf": np.ascontiguousarray(
            np.asarray(inputs["L0_ffn_weight"], f32)[c * 256:(c + 1) * 256, :]),
        "bfr": np.ascontiguousarray(
            np.asarray(inputs["L0_ffn_bias"], f32).reshape(1, D)),
        "eyeb": np.eye(128, dtype=ml_dtypes.bfloat16),
        "onesr": np.ones((1, 128), f32),
        "onesc": np.ones((128, 1), ml_dtypes.bfloat16),
        "alib": _alibi_table(hh),
        "maskt": _mask_table(),
    }
    return d


def kernel(**inputs):
    nc, _ = _build(debug=False)
    in_maps = [_host_inputs(inputs, c) for c in range(8)]
    res = run_bass_kernel_spmd(nc, in_maps, core_ids=list(range(8)))
    out = np.empty((B, T, D), np.float32)
    for c in range(8):
        b, hh = c >> 1, c & 1
        out[b, hh * TPC:(hh + 1) * TPC, :] = res.results[c]["out"]
    return out


# version 13
# speedup vs baseline: 1.0266x; 1.0060x over previous
import math
from collections import defaultdict
from contextlib import ExitStack

import numpy as np
import ml_dtypes

import concourse.bass as bass
import concourse.mybir as mybir
from concourse.bass_utils import run_bass_kernel_spmd

F32 = mybir.dt.float32
BF16 = mybir.dt.bfloat16
AF = mybir.ActivationFunctionType
ALU = mybir.AluOpType
AX = mybir.AxisListType

B, T, D = 4, 4096, 2048
NCH = D // 128
TPC = T // 2
SET0 = [0, 3, 4, 7, 8, 11, 12, 15]
SET1 = [1, 2, 5, 6, 9, 10, 13, 14]
SLOPES = [2.0 ** (-(g + 1) / 2) for g in range(16)]
QSCALE = 1.0 / math.sqrt(128.0)
NEGBIG = -1.0e30
LNEPS = 1e-5
W_EMIN, W_EMAX = -16, 4
A_EMIN, A_EMAX = -16, 7


def _attn_plan():
    colkey = {}
    plans = []
    for j in range(8):
        a0, a1 = SLOPES[SET0[j]], SLOPES[SET1[j]]
        amax, amin = max(a0, a1), min(a0, a1)
        G = 128 if amax > 0.4 else (256 if amax > 0.2 else 512)
        reach = 115.0 / amin + 128.0
        qgs = []
        for q0 in range(0, T, G):
            ref = q0 + G // 2
            kmin = max(0, int(math.floor((ref - reach) / 128.0)) * 128)
            kts = []
            for k0 in range(kmin, q0 + G, 128):
                key = (j, k0 - ref)
                if key not in colkey:
                    colkey[key] = len(colkey)
                delta = k0 - q0
                mi = delta // 128 if delta >= 0 else None
                kts.append((k0, colkey[key], mi))
            qgs.append((q0, ref, kts))
        plans.append((G, qgs))
    return plans, colkey


PLANS, COLKEY = _attn_plan()
NCOL = len(COLKEY)
assert NCOL <= 512, NCOL


def _alibi_table(hh):
    arr = np.zeros((128, NCOL), np.float32)
    heads = SET0 if hh == 0 else SET1
    pp = np.arange(128, dtype=np.float64)
    for (j, off), idx in COLKEY.items():
        arr[:, idx] = (SLOPES[heads[j]] * (off + pp)).astype(np.float32)
    return np.ascontiguousarray(arr)


def _mask_table():
    m = np.zeros((128, 4, 512), np.float32)
    pp = np.arange(128)[:, None]
    ff = np.arange(512)[None, :]
    for d in range(4):
        m[:, d, :] = np.where(ff >= pp + 128 * d, 0.0, NEGBIG)
    return np.ascontiguousarray(m.reshape(128, 4 * 512))


class Plan:
    def __init__(self):
        self.ops = []
        self.cnt = defaultdict(int)

    def op(self, eng, fn, waits=(), inc=None, n=1):
        w = {}
        for ev in waits:
            if ev is None:
                continue
            s, v = ev
            if w.get(s, -1) < v:
                w[s] = v
        evt = None
        if inc is not None:
            self.cnt[inc] += n
            evt = (inc, self.cnt[inc])
        self.ops.append((eng, w, fn, inc, n))
        return evt

    def dma(self, queue, out_ap, in_ap, waits=()):
        sem = {"sync": "qd", "scalar": "ad", "gpsimd": "gd"}[queue]
        fn = lambda e, o=out_ap, i=in_ap: e.dma_start(out=o, in_=i)
        return self.op(queue, fn, waits, inc=sem, n=16)

    def snap(self):
        return [(s, v) for s, v in self.cnt.items()]


def _emit_scale_ladder(p, amax, scl, rscl, t16, waits, emin, emax):
    e1 = p.op("vector", lambda e, o=amax: e.tensor_scalar(
        o, o, float(2.0 ** emin), None, ALU.max), list(waits), inc="ve")
    es = p.op("vector", lambda e, o=scl: e.memset(o, float(2.0 ** (emin - 2))),
              list(waits), inc="ve")
    for ee in range(emin + 1, emax + 1):
        et = p.op("vector", lambda e, o=t16, i=amax, th=float(2.0 ** ee):
                  e.tensor_scalar(o, i, th, 1.0, ALU.is_ge, ALU.add),
                  [e1, es], inc="ve")
        es = p.op("vector", lambda e, o=scl, i=t16: e.tensor_mul(o, o, i),
                  [et], inc="ve")
    er = p.op("vector", lambda e, o=rscl, i=scl: e.reciprocal(o, i),
              [es], inc="ve")
    es = p.op("vector", lambda e, o=scl: e.tensor_scalar(
        o, o, 0.5, None, ALU.mult), [es, er], inc="ve")
    return es, er


def _emit_round(p, src, out_bf, scl, rscl, scr, frees, waits, pool=False):
    W = src.shape[-1]
    nb = W // 32
    q, aq, sgn, tC, s2 = (scr["q"], scr["aq"], scr["sgn"], scr["tC"],
                          scr["s2"])
    win = src.rearrange("p (g w) -> p g w", w=32)
    rb = rscl.unsqueeze(2).broadcast_to([128, nb, 32])
    qv = q.rearrange("p (g w) -> p g w", w=32)
    eq = p.op("vector", lambda e, o=qv, i=win, r=rb: e.tensor_tensor(
        o, i, r, ALU.mult), list(waits) + [frees.get("q")], inc="ve")
    ea = p.op("scalar", lambda e, o=aq, i=q: e.activation(o, i, AF.Abs),
              [eq, frees.get("aq")], inc="ae")
    eg = p.op("scalar", lambda e, o=sgn, i=q: e.activation(o, i, AF.Sign),
              [eq, frees.get("sgn")], inc="ae")
    eA = p.op("vector", lambda e, o=q, i=aq: e.tensor_scalar(
        o, i, 0.25, None, ALU.is_gt), [ea, eg, eq], inc="ve")
    eA = p.op("vector", lambda e, o=q, i=aq: e.scalar_tensor_tensor(
        o, i, 0.75, o, ALU.is_ge, ALU.add), [eA], inc="ve")
    eA = p.op("vector", lambda e, o=q, i=aq: e.scalar_tensor_tensor(
        o, i, 1.25, o, ALU.is_gt, ALU.add), [eA], inc="ve")
    eA = p.op("vector", lambda e, o=q, i=aq: e.scalar_tensor_tensor(
        o, i, 1.75, o, ALU.is_ge, ALU.add), [eA], inc="ve")
    eU = p.op("vector", lambda e, o=tC, i=aq: e.tensor_scalar(
        o, i, 5.0, 2.0, ALU.is_gt, ALU.mult), [ea, frees.get("tC")], inc="ve")
    eU = p.op("vector", lambda e, o=tC, i=aq: e.scalar_tensor_tensor(
        o, i, 2.5, o, ALU.is_gt, ALU.add), [eU], inc="ve")
    eU = p.op("vector", lambda e, o=tC, i=aq: e.scalar_tensor_tensor(
        o, i, 3.5, o, ALU.is_ge, ALU.add), [eU], inc="ve")
    eA = p.op("vector", lambda e, o=q, i=tC: e.scalar_tensor_tensor(
        o, i, 2.0, o, ALU.mult, ALU.add), [eA, eU], inc="ve")
    sb_ = scl.unsqueeze(2).broadcast_to([128, nb, 32])
    s2v = s2.rearrange("p (g w) -> p g w", w=32)
    sgv = sgn.rearrange("p (g w) -> p g w", w=32)
    e2 = p.op("gpsimd", lambda e, o=s2v, i=sgv, s_=sb_: e.tensor_tensor(
        o, i, s_, ALU.mult), [eg, frees.get("s2")], inc="po")
    eo = p.op("gpsimd", lambda e, o=out_bf, i=q, s_=s2: e.tensor_tensor(
        o, i, s_, ALU.mult), [eA, e2, frees.get("out")], inc="po")
    newfrees = {"q": eo, "aq": eA, "sgn": e2, "tC": eA, "s2": eo}
    return eo, newfrees


def _build(debug=False):
    nc = bass.Bass("TRN2", target_bir_lowering=False, debug=False)

    x = nc.dram_tensor("x", [T, D], F32, kind="ExternalInput").ap()
    xr = nc.dram_tensor("xr", [TPC, D], F32, kind="ExternalInput").ap()
    wq = nc.dram_tensor("wq", [D, 1024], F32, kind="ExternalInput").ap()
    wk = nc.dram_tensor("wk", [D, 1024], F32, kind="ExternalInput").ap()
    wv = nc.dram_tensor("wv", [D, 1024], F32, kind="ExternalInput").ap()
    bqr = nc.dram_tensor("bqr", [128, 8], F32, kind="ExternalInput").ap()
    bkr = nc.dram_tensor("bkr", [128, 8], F32, kind="ExternalInput").ap()
    bvr = nc.dram_tensor("bvr", [1, 1024], F32, kind="ExternalInput").ap()
    wo = nc.dram_tensor("wo", [1024, D], F32, kind="ExternalInput").ap()
    bor = nc.dram_tensor("bor", [1, D], F32, kind="ExternalInput").ap()
    wf = nc.dram_tensor("wf", [256, D], F32, kind="ExternalInput").ap()
    bfr = nc.dram_tensor("bfr", [1, D], F32, kind="ExternalInput").ap()
    eyeb = nc.dram_tensor("eyeb", [128, 128], BF16, kind="ExternalInput").ap()
    onesr = nc.dram_tensor("onesr", [1, 128], F32, kind="ExternalInput").ap()
    onesc = nc.dram_tensor("onesc", [128, 1], BF16, kind="ExternalInput").ap()
    alib = nc.dram_tensor("alib", [128, NCOL], F32, kind="ExternalInput").ap()
    maskt = nc.dram_tensor("maskt", [128, 4 * 512], F32, kind="ExternalInput").ap()
    out = nc.dram_tensor("out", [TPC, D], F32, kind="ExternalOutput").ap()

    Qs = nc.dram_tensor("Qs", [8, 128, T], BF16, kind="Internal")
    Ks = nc.dram_tensor("Ks", [8, 128, T], BF16, kind="Internal")
    Vs = nc.dram_tensor("Vs", [8, T, 128], BF16, kind="Internal")
    Ysin = nc.dram_tensor("Ysin", [1024, T], BF16, kind="Internal")
    POs = nc.dram_tensor("POs", [T, D], BF16, kind="Internal")
    POhA = nc.dram_tensor("POhA", [1024, D], BF16, kind="Internal")
    POhB = nc.dram_tensor("POhB", [1024, D], BF16, kind="Internal")
    Wfc = nc.dram_tensor("Wfc", [NCH * 128, 256], BF16, kind="Internal")
    Wfg = nc.dram_tensor("Wfg", [8 * NCH * 128, 256], BF16, kind="Internal",
                         addr_space="Shared")
    Y1s = nc.dram_tensor("Y1s", [TPC, D], F32, kind="Internal")

    p = Plan()
    ctx = ExitStack()

    def SB(scope, name, shape, dt):
        return scope.enter_context(nc.sbuf_tensor(name, shape, dt))

    eyeb_sb = SB(ctx, "eyeb_sb", [128, 128], BF16)
    onesr_sb = SB(ctx, "onesr_sb", [1, 128], F32)
    onesc_sb = SB(ctx, "onesc_sb", [128, 1], BF16)
    alib_sb = SB(ctx, "alib_sb", [128, NCOL], F32)
    bq_sb = SB(ctx, "bq_sb", [128, 8], F32)
    bk_sb = SB(ctx, "bk_sb", [128, 8], F32)
    robuf = SB(ctx, "robuf", [1, D], F32)
    bobf_bc = SB(ctx, "bobf_bc", [128, D], F32)
    amaxA = SB(ctx, "amaxA", [128, NCH, 64], F32)
    sclA = SB(ctx, "sclA", [128, NCH * 64], F32)
    rsclA = SB(ctx, "rsclA", [128, NCH * 64], F32)

    pb = [ctx.enter_context(nc.psum_tensor(f"pb{i}", [128, 512], F32))
          for i in range(8)]

    def pbbf(i):
        return pb[i][:, :].bitcast(BF16)[:, 0:512]

    ev_eye = p.dma("sync", eyeb_sb[:, :], eyeb)
    ev_onesr = p.dma("sync", onesr_sb[:, :], onesr)
    ev_onesc = p.dma("sync", onesc_sb[:, :], onesc)
    ev_alib = p.dma("sync", alib_sb[:, :], alib)
    ev_bq = p.dma("sync", bq_sb[:, :], bqr)
    ev_bk = p.dma("sync", bk_sb[:, :], bkr)
    ev_bv = p.dma("sync", robuf[0:1, 0:1024], bvr)

    s1 = ExitStack()
    wq_sb = SB(s1, "wq_sb", [128, NCH, 1024], BF16)
    wk_sb = SB(s1, "wk_sb", [128, NCH, 1024], BF16)
    wv_sb = SB(s1, "wv_sb", [128, NCH, 1024], BF16)
    xb_sb = [SB(s1, f"xb{i}", [128, 4, D], BF16) for i in range(2)]
    xfm_sb = SB(s1, "xfm", [128, NCH, 512], BF16)
    qkst = [SB(s1, f"qkst{i}", [128, 512], BF16) for i in range(2)]
    vst = [SB(s1, f"vst{i}", [128, 512], BF16) for i in range(2)]
    wt_sb = [SB(s1, f"wt{i}", [128, 512], F32) for i in range(2)]
    amaxW = SB(s1, "amaxW", [128, 8, 16], F32)
    sclW = SB(s1, "sclW", [128, 128], F32)
    rsclW = SB(s1, "rsclW", [128, 128], F32)
    t16W = SB(s1, "t16W", [128, 128], F32)
    wscr = dict(
        q=SB(s1, "w_q", [128, 512], F32)[:, :],
        aq=SB(s1, "w_aq", [128, 512], F32)[:, :],
        sgn=SB(s1, "w_sgn", [128, 512], BF16)[:, :],
        tC=SB(s1, "w_tC", [128, 512], F32)[:, :],
        s2=SB(s1, "w_s2", [128, 512], BF16)[:, :],
    )
    wq_tm = [SB(s1, f"wq_tm{i}", [128, 512], BF16) for i in range(2)]
    wftst = [SB(s1, f"wftst{i}", [128, 4, 128], BF16) for i in range(2)]

    wstate = {"wt": [None, None], "wt_used": [None, None], "ladder": None,
              "frees": {}, "evft": None, "evft_store": [None, None],
              "amax_done": [], "stores": [], "evq": {},
              "out_free": [None, None]}

    def w_unit_src(u):
        r, cc = u % 2, u // 2
        return wf[r * 128:(r + 1) * 128, cc * 512:(cc + 1) * 512]

    def w_pass1_pair(g):
        for u in (2 * g, 2 * g + 1):
            sl = u % 2
            wstate["wt"][sl] = p.dma(
                "scalar", wt_sb[sl][:, :], w_unit_src(u),
                [wstate["wt"][sl], wstate["wt_used"][sl]])
            wstate["amax_done"].append(p.op(
                "vector",
                lambda e, o=amaxW[:, u, :],
                i=wt_sb[sl][:, :].rearrange("p (g w) -> p g w", w=32):
                e.reduce_max(o, i, axis=AX.X, apply_absolute_value=True),
                [wstate["wt"][sl]], inc="ve"))
            wstate["wt_used"][sl] = wstate["amax_done"][-1]

    def w_round(u):
        sl = u % 2
        wstate["wt"][sl] = p.dma(
            "scalar", wt_sb[sl][:, :], w_unit_src(u),
            [wstate["wt"][sl], wstate["wt_used"][sl]])
        evq, wstate["frees"] = _emit_round(
            p, wt_sb[sl][:, :], wq_tm[sl][:, :],
            sclW[:, u * 16:(u + 1) * 16], rsclW[:, u * 16:(u + 1) * 16],
            wscr, dict(wstate["frees"], out=wstate["out_free"][sl]),
            [wstate["wt"][sl]] + list(wstate["ladder"]))
        wstate["wt_used"][sl] = wstate["frees"]["q"]
        wstate["evq"][u] = evq

    def w_store(u):
        sl = u % 2
        evq = wstate["evq"][u]
        r, cc = u % 2, u // 2
        tb = u % 2
        evt = None
        for kk in range(4):
            evt = p.op(
                "tensor",
                lambda e, o=pbbf(7)[:, kk * 128:(kk + 1) * 128],
                i=wq_tm[sl][:, kk * 128:(kk + 1) * 128],
                idn=eyeb_sb[:, :]: e.transpose(o, i, idn),
                [evq, ev_eye, wstate["evft"]],
                inc="pe" if kk == 3 else None)
        wstate["evft"] = p.op(
            "scalar",
            lambda e, o=wftst[tb][:, :, :].rearrange("p a b -> p (a b)"),
            i=pbbf(7)[:, :]: e.activation(o, i, AF.Copy),
            [evt, wstate["evft_store"][tb]], inc="ae")
        wstate["evft_store"][tb] = p.dma(
            "scalar",
            Wfc.ap().rearrange("(c p) n -> c p n", p=128)
            [cc * 4:(cc + 1) * 4, :, r * 128:(r + 1) * 128]
            .rearrange("c k n -> k c n"),
            wftst[tb][:, :, :],
            [wstate["evft"]])
        wstate["stores"].append(wstate["evft_store"][tb])
        wstate["out_free"][sl] = evt


    ev_wq = p.dma("gpsimd", wq_sb[:, :, :],
                  wq.rearrange("(c p) f -> p c f", p=128))
    ev_wk = p.dma("gpsimd", wk_sb[:, :, :],
                  wk.rearrange("(c p) f -> p c f", p=128))
    ev_wv = p.dma("gpsimd", wv_sb[:, :, :],
                  wv.rearrange("(c p) f -> p c f", p=128))

    ev_xb = [None, None]
    ev_xb_used = [None, None]
    ev_tr_evac = [None, None]
    ev_qk_evac = [None, None]
    ev_v_evac = [None, None]
    ev_qkst_store = [None, None]
    ev_vst_store = [None, None]
    ev_qs_store = {}
    ev_ks_store = {}

    for grp in range(8):
        t0 = grp * 512
        par = grp % 2
        ev_xb[par] = p.dma(
            "gpsimd", xb_sb[par][:, :, :],
            x[t0:t0 + 512, :].rearrange("(n p) f -> p n f", p=128),
            [ev_xb_used[par]])
        evtr = None
        for c in range(NCH):
            tb = c % 2
            for tt in range(4):
                evtr = p.op(
                    "tensor",
                    lambda e, o=pbbf(tb)[:, tt * 128:(tt + 1) * 128],
                    i=xb_sb[par][:, tt, c * 128:(c + 1) * 128],
                    idn=eyeb_sb[:, :]: e.transpose(o, i, idn),
                    [ev_xb[par], ev_eye, ev_tr_evac[tb]],
                    inc="pe" if tt == 3 else None)
            ev_tr_evac[tb] = p.op(
                "scalar", lambda e, o=xfm_sb[:, c, :], i=pbbf(tb)[:, :]:
                e.activation(o, i, AF.Copy), [evtr], inc="ae")
        ev_xb_used[par] = evtr
        ev_xfm = ev_tr_evac[(NCH - 1) % 2]
        ev_xfm2 = ev_tr_evac[NCH % 2]

        for kind in ("k", "q"):
            wsb = wq_sb if kind == "q" else wk_sb
            ev_w = ev_wq if kind == "q" else ev_wk
            bias_sb = bq_sb if kind == "q" else bk_sb
            dst = Qs if kind == "q" else Ks
            for fc in range(8):
                bank = 2 + (fc % 2)
                evm = None
                for kc in range(NCH):
                    evm = p.op(
                        "tensor",
                        lambda e, o=pb[bank][:, :],
                        l=wsb[:, kc, fc * 128:(fc + 1) * 128],
                        r=xfm_sb[:, kc, :], st=(kc == 0), sp=(kc == NCH - 1):
                        e.matmul(o, lhsT=l, rhs=r, start=st, stop=sp),
                        [ev_w, ev_xfm, ev_xfm2, ev_qk_evac[fc % 2]],
                        inc="pe" if kc == NCH - 1 else None)
                sp_ = fc % 2
                ev_qk_evac[sp_] = p.op(
                    "scalar",
                    lambda e, o=qkst[sp_][:, :], i=pb[bank][:, :],
                    bi=bias_sb[:, fc:fc + 1]:
                    e.activation(o, i, AF.Identity, bias=bi),
                    [evm, ev_qkst_store[sp_]], inc="ae")
                ev_qkst_store[sp_] = p.dma(
                    "sync", dst.ap()[fc, :, t0:t0 + 512], qkst[sp_][:, :],
                    [ev_qk_evac[sp_]])
                (ev_qs_store if kind == "q" else ev_ks_store)[fc] = \
                    ev_qkst_store[sp_]

        for ts4 in range(4):
            for fg in range(2):
                sp_ = (ts4 * 2 + fg) % 2
                bank = 4 + sp_
                for kc in range(NCH):
                    p.op("tensor",
                         lambda e, o=pb[bank][:, :],
                         l=xfm_sb[:, kc, ts4 * 128:(ts4 + 1) * 128],
                         r=wv_sb[:, kc, fg * 512:(fg + 1) * 512],
                         st=(kc == 0):
                         e.matmul(o, lhsT=l, rhs=r, start=st, stop=False),
                         [ev_wv, ev_xfm, ev_xfm2, ev_v_evac[sp_]],
                         inc=None)
                evm = p.op(
                    "tensor",
                    lambda e, o=pb[bank][:, :], l=onesr_sb[:, :],
                    r=robuf[0:1, fg * 512 + 512 * 0:fg * 512 + 512]:
                    e.matmul(o, lhsT=l, rhs=r, start=False, stop=True),
                    [ev_bv, ev_onesr], inc="pe")
                ev_robuf_free = evm
                ev_v_evac[sp_] = p.op(
                    "scalar",
                    lambda e, o=vst[sp_][:, :], i=pb[bank][:, :]:
                    e.activation(o, i, AF.Copy),
                    [evm, ev_vst_store[sp_]], inc="ae")
                ev_vst_store[sp_] = p.dma(
                    "sync",
                    Vs.ap()[fg * 4:(fg + 1) * 4,
                            t0 + ts4 * 128:t0 + (ts4 + 1) * 128, :]
                    .rearrange("j t d -> t j d"),
                    vst[sp_][:, :].rearrange("p (j d) -> p j d", d=128),
                    [ev_v_evac[sp_]])

        if grp < 4:
            w_pass1_pair(grp)
            if grp == 3:
                wstate["ladder"] = _emit_scale_ladder(
                    p, amaxW[:, :, :].rearrange("p a b -> p (a b)"),
                    sclW[:, :], rsclW[:, :], t16W[:, :],
                    wstate["amax_done"], W_EMIN, W_EMAX)
        else:
            for u in (2 * (grp - 5), 2 * (grp - 5) + 1):
                if u >= 0:
                    w_store(u)
            for u in (2 * (grp - 4), 2 * (grp - 4) + 1):
                w_round(u)
    for u in (6, 7):
        w_store(u)

    s1_end = p.snap()
    s1.close()

    s2keep = ExitStack()
    wo_sb = SB(s2keep, "wo_sb", [128, 8, D], BF16)
    s2 = ExitStack()
    maskt_sb = SB(s2, "maskt_sb", [128, 4, 512], F32)
    khead = [SB(s2, f"khead{i}", [128, T], BF16) for i in range(2)]
    vhead = [SB(s2, f"vhead{i}", [128, 32, 128], BF16) for i in range(2)]
    qhead = [SB(s2, f"qhead{i}", [128, T], BF16) for i in range(2)]
    expb = [SB(s2, f"expb{i}", [128, 512], BF16) for i in range(6)]
    dn_sb = [SB(s2, f"dn_sb{i}", [1, 512], F32) for i in range(2)]
    rec_sb = [SB(s2, f"rec_sb{i}", [1, 512], F32) for i in range(2)]
    bcm_sb = [SB(s2, f"bcm_sb{i}", [128, 512], F32) for i in range(2)]
    yst = [SB(s2, f"yst{i}", [128, 512], BF16) for i in range(2)]
    ev_mask = p.dma("sync", maskt_sb[:, :, :].rearrange("p a b -> p (a b)"),
                    maskt, s1_end)
    ev_wo = None
    for sl in range(2):
        waits = list(s1_end) if sl == 0 else []
        ev_wo = p.dma(
            "gpsimd", wo_sb[:, sl * 4:(sl + 1) * 4, :],
            wo[sl * 512:(sl + 1) * 512, :].rearrange("(c p) f -> p c f", p=128),
            waits + [ev_wo])
    ev_wo_done = ev_wo

    ev_kh = [None, None]
    ev_vh = [None, None]
    ev_qh = [None, None]
    last_pe_head = [None, None]
    exp_free = [None] * 6
    S_free = {0: None, 1: None, 5: None}
    pv_free = [None, None]
    dn_free = [None, None]
    rec_free = [None, None]
    dnsb_free = [None, None]
    bcm_free = [None, None]
    yst_free = [None, None]
    ygl = 0
    gti = [0]
    bcst = {"bank": None, "yp": 0}
    SBANKS = (0, 1, 5)

    for j in range(8):
        hp = j % 2
        w0 = s1_end if j < 2 else []
        ev_qh[hp] = p.dma("sync", qhead[hp][:, :], Qs.ap()[j, :, :],
                          w0 + [ev_qs_store[j], last_pe_head[hp], ev_qh[hp]])
        ev_kh[hp] = p.dma("sync", khead[hp][:, :], Ks.ap()[j, :, :],
                          w0 + [ev_ks_store[j], last_pe_head[hp], ev_kh[hp]])
        ev_vh[hp] = p.dma("sync", vhead[hp][:, :, :],
                          Vs.ap()[j, :, :].rearrange("(a p) d -> p a d", p=128),
                          w0 + [ev_vst_store[0], ev_vst_store[1],
                                last_pe_head[hp], ev_vh[hp]])
        G, qgs = PLANS[j]
        pend = []

        def pop_one():
            fn_, i_, xp_, ev_, epi_ = pend.pop(0)
            pv_ = fn_(i_, xp_, ev_)
            exp_free[xp_] = pv_
            if epi_ is not None:
                epi_(pv_)

        for (q0, ref, kts) in qgs:
            qp = ygl % 2
            pvbank = 2 + qp
            dnrow = 32 * qp
            nt = len(kts)

            def emit_dnpv(i, xp, evexp, qp=qp, dnrow=dnrow, pvbank=pvbank,
                          G=G, kts=kts, nt=nt, hp=hp):
                first, last = (i == 0), (i == nt - 1)
                p.op("tensor",
                     lambda e, o=pb[4][dnrow:dnrow + 1, 0:G], l=onesc_sb[:, :],
                     r=expb[xp][:, 0:G], st=first, sp=last:
                     e.matmul(o, lhsT=l, rhs=r, start=st, stop=sp),
                     [evexp, ev_onesc] + ([dn_free[qp]] if first else []),
                     inc=None)
                kloc = kts[i][0] // 128
                return p.op(
                    "tensor",
                    lambda e, o=pb[pvbank][:, 0:G],
                    l=vhead[hp][:, kloc, :], r=expb[xp][:, 0:G],
                    st=first, sp=last:
                    e.matmul(o, lhsT=l, rhs=r, start=st, stop=sp),
                    [evexp, ev_vh[hp]] + ([pv_free[qp]] if first else []),
                    inc="pe")

            def epilogue(ev_pvlast, qp=qp, dnrow=dnrow, pvbank=pvbank,
                         G=G, q0=q0, j=j, hp=hp):
                ev_dnev = p.op("scalar",
                               lambda e, o=dn_sb[qp][0:1, 0:G],
                               i=pb[4][dnrow:dnrow + 1, 0:G]:
                               e.activation(o, i, AF.Copy),
                               [ev_pvlast, dnsb_free[qp]], inc="ae")
                dn_free[qp] = ev_dnev
                ev_rec = p.op("vector",
                              lambda e, o=rec_sb[qp][0:1, 0:G],
                              i=dn_sb[qp][0:1, 0:G]:
                              e.reciprocal(o, i), [ev_dnev, rec_free[qp]],
                              inc="ve")
                dnsb_free[qp] = ev_rec
                ev_bcmm = p.op("tensor",
                               lambda e, o=pb[6][:, 0:G], l=onesr_sb[:, :],
                               r=rec_sb[qp][0:1, 0:G]:
                               e.matmul(o, lhsT=l, rhs=r, start=True,
                                        stop=True),
                               [ev_rec, ev_onesr, bcst["bank"]], inc="pe")
                rec_free[qp] = ev_bcmm
                ev_bcev = p.op("scalar",
                               lambda e, o=bcm_sb[qp][:, 0:G], i=pb[6][:, 0:G]:
                               e.activation(o, i, AF.Copy),
                               [ev_bcmm, bcm_free[qp]], inc="ae")
                bcst["bank"] = ev_bcev
                yp = bcst["yp"]
                bcst["yp"] = 1 - yp
                ev_ym = p.op("vector",
                             lambda e, o=yst[yp][:, 0:G], i=pb[pvbank][:, 0:G],
                             b_=bcm_sb[qp][:, 0:G]:
                             e.tensor_tensor(o, i, b_, ALU.mult),
                             [ev_bcev, ev_pvlast, yst_free[yp]], inc="ve")
                pv_free[qp] = ev_ym
                bcm_free[qp] = ev_ym
                yst_free[yp] = p.dma(
                    "scalar",
                    Ysin.ap()[j * 128:(j + 1) * 128, q0:q0 + G],
                    yst[yp][:, 0:G], [ev_ym])
                last_pe_head[hp] = ev_ym

            for i, (k0, colidx, mi) in enumerate(kts):
                sp_ = SBANKS[gti[0] % 3]
                xp = gti[0] % 6
                gti[0] += 1
                evS = p.op(
                    "tensor",
                    lambda e, o=pb[sp_][:, 0:G],
                    l=khead[hp][:, k0:k0 + 128],
                    r=qhead[hp][:, q0:q0 + G]:
                    e.matmul(o, lhsT=l, rhs=r, start=True, stop=True),
                    [ev_kh[hp], ev_qh[hp], S_free[sp_]], inc="pe")
                if mi is not None:
                    evS = p.op(
                        "vector",
                        lambda e, o=pb[sp_][:, 0:G], m=maskt_sb[:, mi, 0:G]:
                        e.tensor_tensor(o, o, m, ALU.add),
                        [evS, ev_mask], inc="ve")
                evexp = p.op(
                    "scalar",
                    lambda e, o=expb[xp][:, 0:G], i=pb[sp_][:, 0:G],
                    bi=alib_sb[:, colidx:colidx + 1]:
                    e.activation(o, i, AF.Exp, bias=bi, scale=QSCALE),
                    [evS, ev_alib, exp_free[xp]], inc="ae")
                S_free[sp_] = evexp
                pend.append((emit_dnpv, i, xp, evexp,
                             epilogue if i == nt - 1 else None))
                if len(pend) > 4:
                    pop_one()
            ygl += 1
        while pend:
            pop_one()
        if j == 1:
            ev_wcc = p.op("gpsimd",
                          lambda e: e.collective_compute(
                              "AllGather", ALU.bypass,
                              replica_groups=[list(range(8))],
                              ins=[Wfc.ap().opt()], outs=[Wfg.ap().opt()]),
                          wstate["stores"], inc="cc")

    s2_all = p.snap()

    s25 = ExitStack()
    ylhs = [SB(s25, f"ylhs{i}", [128, 8, 128], BF16) for i in range(4)]
    post = [SB(s25, f"post{i}", [128, D], BF16) for i in range(2)]

    ev_yl = [None] * 4
    ev_yl_used = [None] * 4
    ev_post_store = [None, None]
    bank_free3 = [None, None, None, None]
    TTORD = list(range(0, 8)) + list(range(16, 24)) +         list(range(8, 16)) + list(range(24, 32))
    GORD = [0, 2, 1, 3]
    ev_cc_a = None
    for it, tt in enumerate(TTORD):
        tp = it % 4
        pp_ = it % 2
        tg = tt * 128
        rowoff = GORD.index(tt // 8) * 1024 + (tt % 8) * 128
        ev_yl[tp] = p.dma(
            "sync" if it % 2 == 0 else "scalar", ylhs[tp][:, :, :],
            Ysin.ap()[:, tg:tg + 128].rearrange("(c p) t -> p c t", p=128),
            (s2_all if it < 4 else []) + [ev_yl_used[tp]])
        evmm = [None] * 4
        for dc in range(8):
            for ng in range(4):
                evmm[ng] = p.op(
                    "tensor",
                    lambda e, o=pb[ng][:, :], l=ylhs[tp][:, dc, :],
                    r=wo_sb[:, dc, ng * 512:(ng + 1) * 512],
                    st=(dc == 0), sp=(dc == 7):
                    e.matmul(o, lhsT=l, rhs=r, start=st, stop=sp),
                    (s2_all if it < 2 else []) +
                    [ev_yl[tp], ev_wo_done, bank_free3[ng]],
                    inc="pe" if dc == 7 else None)
        ev_yl_used[tp] = evmm[3]
        evev = [None] * 4
        for ng in range(4):
            evev[ng] = p.op(
                "scalar",
                lambda e, o=post[pp_][:, ng * 512:(ng + 1) * 512],
                i=pb[ng][:, :]: e.activation(o, i, AF.Copy),
                [evmm[ng], ev_post_store[pp_]], inc="ae")
            bank_free3[ng] = evev[ng]
        ev_post_store[pp_] = p.dma(
            "sync", POs.ap()[rowoff:rowoff + 128, :], post[pp_][:, :],
            [evev[0], evev[1], evev[2], evev[3]])
        if it == 15:
            half_a = p.snap()
            ev_cc_a = p.op("gpsimd",
                           lambda e: e.collective_compute(
                               "ReduceScatter", ALU.add,
                               replica_groups=[[0, 1], [2, 3], [4, 5], [6, 7]],
                               ins=[POs.ap()[0:2048, :].opt()],
                               outs=[POhA.ap().opt()]),
                           half_a, inc="cc")

    s25_all = p.snap()
    s25.close()
    ev_cc_b = p.op("gpsimd",
                   lambda e: e.collective_compute(
                       "ReduceScatter", ALU.add,
                       replica_groups=[[0, 1], [2, 3], [4, 5], [6, 7]],
                       ins=[POs.ap()[2048:4096, :].opt()],
                       outs=[POhB.ap().opt()]),
                   s25_all, inc="cc")
    s2.close()

    s3 = ExitStack()
    potile = [SB(s3, f"pot{i}", [128, D], BF16) for i in range(2)]
    xres = [SB(s3, f"xres{i}", [128, D], F32) for i in range(2)]
    xpbo_sb = SB(s3, "xpbo", [128, D], F32)
    r_sb = SB(s3, "r_sb", [128, D], F32)
    c_sb = SB(s3, "c_sb", [128, D], F32)
    m1_sb = SB(s3, "m1_sb", [128, 1], F32)
    m_sb = SB(s3, "m_sb", [128, 1], F32)
    v1_sb = SB(s3, "v1_sb", [128, 1], F32)
    rstd_sb = SB(s3, "rstd_sb", [128, 1], F32)
    y1st = [SB(s3, f"y1st{i}", [128, D], F32) for i in range(2)]

    ev_bo = p.dma("sync", robuf[0:1, :], bor, [ev_robuf_free] + s2_all)
    ev_bobc = None
    for sl in range(4):
        evm = p.op("tensor", lambda e, o=pb[4][:, :], l=onesr_sb[:, :],
                   r=robuf[0:1, sl * 512:(sl + 1) * 512]:
                   e.matmul(o, lhsT=l, rhs=r, start=True, stop=True),
                   s25_all + [ev_bo, ev_bobc], inc="pe")
        ev_bobc = p.op("scalar",
                       lambda e, o=bobf_bc[:, sl * 512:(sl + 1) * 512],
                       i=pb[4][:, :]: e.activation(o, i, AF.Copy),
                       [evm], inc="ae")
    ev_robuf_free = ev_bobc

    ev_po = [None, None]
    ev_po_used = [None, None]
    ev_xr = [None, None]
    ev_xr_used = [None, None]
    ev_xpbo_used = None
    ev_r_used = None
    ev_c_used = None
    ev_y1_store = [None, None]
    ev_amax3 = [None, None]

    for tt in range(16):
        tp = tt % 2
        tloc = tt * 128
        posrc = (POhA.ap()[tloc:tloc + 128, :] if tt < 8
                 else POhB.ap()[tloc - 1024:tloc - 1024 + 128, :])
        ev_po[tp] = p.dma("sync", potile[tp][:, :], posrc,
                          [ev_cc_a if tt < 8 else ev_cc_b, ev_po_used[tp]])
        ev_xr[tp] = p.dma("scalar", xres[tp][:, :], xr[tloc:tloc + 128, :],
                          (s25_all if tt < 2 else []) + [ev_xr_used[tp]])
        ev_xpbo = p.op("gpsimd",
                       lambda e, o=xpbo_sb[:, :], i=xres[tp][:, :],
                       b_=bobf_bc[:, :]: e.tensor_tensor(o, i, b_, ALU.add),
                       [ev_xr[tp], ev_bobc, ev_xpbo_used], inc="po")
        ev_xr_used[tp] = ev_xpbo
        evr = p.op("vector",
                   lambda e, o=r_sb[:, :], i=potile[tp][:, :],
                   b_=xpbo_sb[:, :]: e.tensor_tensor(o, i, b_, ALU.add),
                   [ev_po[tp], ev_xpbo, ev_r_used], inc="ve")
        ev_po_used[tp] = evr
        ev_xpbo_used = evr
        evm1 = p.op("vector", lambda e, o=m1_sb[:, :], i=r_sb[:, :]:
                    e.reduce_sum(o, i, axis=AX.X), [evr], inc="ve")
        evm = p.op("vector", lambda e, o=m_sb[:, :], i=m1_sb[:, :]:
                   e.tensor_scalar(o, i, -1.0 / D, None, ALU.mult),
                   [evm1], inc="ve")
        evc = p.op("scalar", lambda e, o=c_sb[:, :], i=r_sb[:, :],
                   m=m_sb[:, 0:1]:
                   e.activation(o, i, AF.Identity, bias=m),
                   [evm, ev_c_used], inc="ae")
        ev_r_used = evc
        evsq = p.op("scalar", lambda e, o=r_sb[:, :], i=c_sb[:, :],
                    a_=v1_sb[:, 0:1]:
                    e.activation(o, i, AF.Square, accum_out=a_),
                    [evc], inc="ae")
        ev_r_used = evsq
        evve = p.op("vector", lambda e, o=v1_sb[:, :]:
                    e.tensor_scalar(o, o, 1.0 / D, LNEPS, ALU.mult, ALU.add),
                    [evsq], inc="ve")
        evln = p.op("scalar", lambda e, o=rstd_sb[:, :], i=v1_sb[:, :]:
                    e.activation(o, i, AF.Ln), [evve], inc="ae")
        evex = p.op("scalar", lambda e, o=rstd_sb[:, :], i=rstd_sb[:, :]:
                    e.activation(o, i, AF.Exp, scale=-0.5), [evln], inc="ae")
        evy1 = p.op("scalar", lambda e, o=y1st[tp][:, :], i=c_sb[:, :],
                    m=rstd_sb[:, 0:1]:
                    e.activation(o, i, AF.Identity, scale=m),
                    [evex, ev_y1_store[tp], ev_amax3[tp]], inc="ae")
        ev_c_used = evy1
        ev_amax3[tp] = p.op(
            "vector",
            lambda e, o=amaxA[:, tt, :],
            i=y1st[tp][:, :].rearrange("p (g w) -> p g w", w=32):
            e.reduce_max(o, i, axis=AX.X, apply_absolute_value=True),
            [evy1], inc="ve")
        ev_y1_store[tp] = p.dma("sync", Y1s.ap()[tloc:tloc + 128, :],
                                y1st[tp][:, :], [evy1])

    s3_end = p.snap()
    s3.close()
    s2keep.close()

    s4 = ExitStack()
    wft_sb = SB(s4, "wft_sb", [128, NCH, D], BF16)
    y1l = [SB(s4, f"y1l{i}", [128, D], F32) for i in range(3)]
    t16A = SB(s4, "t16A", [128, NCH * 64], F32)
    ascr = dict(
        q=SB(s4, "a_q", [128, D], F32)[:, :],
        aq=SB(s4, "a_aq", [128, D], F32)[:, :],
        sgn=SB(s4, "a_sgn", [128, D], BF16)[:, :],
        tC=SB(s4, "a_tC", [128, D], F32)[:, :],
        s2=SB(s4, "a_s2", [128, D], BF16)[:, :],
    )
    aq_tm = [SB(s4, f"aq_tm{i}", [128, D], BF16) for i in range(2)]
    afm_sb = [SB(s4, f"afm{i}", [128, NCH, 128], BF16) for i in range(2)]
    r4_sb = SB(s4, "r4_sb", [128, D], F32)
    ffnout_sb = SB(s4, "ffnout", [128, D], F32)
    c4_sb = SB(s4, "c4_sb", [128, D], F32)
    m14_sb = SB(s4, "m14_sb", [128, 1], F32)
    m4_sb = SB(s4, "m4_sb", [128, 1], F32)
    v14_sb = SB(s4, "v14_sb", [128, 1], F32)
    rstd4_sb = SB(s4, "rstd4_sb", [128, 1], F32)
    outst = SB(s4, "outst", [128, D], F32)

    ev_ladA = _emit_scale_ladder(
        p, amaxA[:, :, :].rearrange("p a b -> p (a b)"),
        sclA[:, :], rsclA[:, :], t16A[:, :],
        s3_end, A_EMIN, A_EMAX)

    ev_y1l = [None, None, None]
    for tt in range(3):
        ev_y1l[tt] = p.dma("scalar", y1l[tt][:, :],
                           Y1s.ap()[tt * 128:tt * 128 + 128, :],
                           s3_end + [ev_y1_store[tt % 2]])

    ev_bf = p.dma("sync", robuf[0:1, :], bfr, [ev_robuf_free] + s3_end)

    ev_wft_r = [None] * 8
    prevq = {"sync": None, "scalar": None}
    for r_ in range(8):
        qn = "sync" if r_ % 2 == 0 else "scalar"
        ev_wft_r[r_] = p.dma(
            qn, wft_sb[:, :, r_ * 256:(r_ + 1) * 256],
            Wfg.ap()[r_ * 2048:(r_ + 1) * 2048, :]
            .rearrange("(c p) n -> p c n", p=128),
            (s3_end if r_ < 2 else []) + [ev_wcc, prevq[qn]])
        prevq[qn] = ev_wft_r[r_]

    afree = {}
    ev_y1l_used = [None, None, None]
    ev_aqtm_used = [None, None]
    ev_afm_evac = [None, None]
    ev_afm_used = [None, None]
    bank_free4 = [None, None, None, None]
    ev_out_store = [None]
    st4 = {"r4_used": None, "c4_used": None, "evmm": None, "ffn_used": None}

    def tail_a(tt):
        evcp = None
        for ng in range(4):
            evcp = p.op("scalar",
                        lambda e, o=ffnout_sb[:, ng * 512:(ng + 1) * 512],
                        i=pb[ng][:, :]: e.activation(o, i, AF.Copy),
                        [st4["evmm"][ng], st4["ffn_used"]], inc="ae")
            bank_free4[ng] = evcp
        st4["evcp"] = evcp

    def tail_b(tt):
        tp = tt % 3
        op_ = tt % 2
        tloc = tt * 128
        evr = p.op("gpsimd",
                   lambda e, o=r4_sb[:, :], i0=ffnout_sb[:, :],
                   i1=y1l[tp][:, :]: e.tensor_tensor(o, i0, i1, ALU.add),
                   [st4["evcp"], st4["r4_used"]], inc="po")
        st4["ffn_used"] = evr
        ev_y1l_used[tp] = evr
        st4["r4_used"] = evr
        evmean = p.op("scalar",
                      lambda e, o=c4_sb[:, :], i=r4_sb[:, :],
                      a_=m14_sb[:, 0:1]:
                      e.activation(o, i, AF.Copy, accum_out=a_),
                      [evr, st4["c4_used"]], inc="ae")
        evm = p.op("vector", lambda e, o=m4_sb[:, :], i=m14_sb[:, :]:
                   e.tensor_scalar(o, i, -1.0 / D, None, ALU.mult),
                   [evmean], inc="ve")
        evc = p.op("scalar", lambda e, o=c4_sb[:, :], i=r4_sb[:, :],
                   m=m4_sb[:, 0:1]:
                   e.activation(o, i, AF.Identity, bias=m),
                   [evm], inc="ae")
        evsq = p.op("scalar", lambda e, o=r4_sb[:, :], i=c4_sb[:, :],
                    a_=v14_sb[:, 0:1]:
                    e.activation(o, i, AF.Square, accum_out=a_),
                    [evc], inc="ae")
        st4["r4_used"] = evsq
        evve4 = p.op("vector", lambda e, o=v14_sb[:, :]:
                     e.tensor_scalar(o, o, 1.0 / D, LNEPS, ALU.mult, ALU.add),
                     [evsq], inc="ve")
        evln = p.op("scalar", lambda e, o=rstd4_sb[:, :], i=v14_sb[:, :]:
                    e.activation(o, i, AF.Ln), [evve4], inc="ae")
        evex = p.op("scalar", lambda e, o=rstd4_sb[:, :], i=rstd4_sb[:, :]:
                    e.activation(o, i, AF.Exp, scale=-0.5), [evln], inc="ae")
        evout = p.op("scalar", lambda e, o=outst[:, :], i=c4_sb[:, :],
                     m=rstd4_sb[:, 0:1]:
                     e.activation(o, i, AF.Identity, scale=m),
                     [evex, ev_out_store[0]], inc="ae")
        st4["c4_used"] = evout
        ev_out_store[0] = p.dma("sync", out[tloc:tloc + 128, :],
                                outst[:, :], [evout])

    for tt in range(16):
        tp = tt % 3
        ap_ = tt % 2
        evq, afree = _emit_round(
            p, y1l[tp][:, :], aq_tm[ap_][:, :],
            sclA[:, tt * 64:(tt + 1) * 64], rsclA[:, tt * 64:(tt + 1) * 64],
            ascr, dict(afree, out=ev_aqtm_used[ap_]),
            [ev_y1l[tp]] + list(ev_ladA), pool=True)
        if tt >= 1:
            tail_a(tt - 1)
        evt = None
        for kq in range(4):
            tb = kq % 2
            for kk in range(4):
                kc = kq * 4 + kk
                evt = p.op(
                    "tensor",
                    lambda e, o=pbbf(4 + tb)[:, kk * 128:(kk + 1) * 128],
                    i=aq_tm[ap_][:, kc * 128:(kc + 1) * 128],
                    idn=eyeb_sb[:, :]: e.transpose(o, i, idn),
                    [evq, ev_eye, ev_afm_evac[tb]],
                    inc="pe" if kk == 3 else None)
            ev_afm_evac[tb] = p.op(
                "scalar",
                lambda e, o=afm_sb[ap_][:, kq * 4:(kq + 1) * 4, :]
                .rearrange("p a b -> p (a b)"),
                i=pbbf(4 + tb)[:, :]: e.activation(o, i, AF.Copy),
                [evt, ev_afm_used[ap_] if kq < 2 else None], inc="ae")
        ev_aqtm_used[ap_] = evt
        afree["out"] = evt
        ev_afm = [ev_afm_evac[0], ev_afm_evac[1]]
        for kc in range(NCH):
            for ng in range(4):
                w0 = ([ev_wft_r[2 * ng], ev_wft_r[2 * ng + 1],
                       bank_free4[ng]] if kc == 0 else [])
                p.op("tensor",
                     lambda e, o=pb[ng][:, :], l=afm_sb[ap_][:, kc, :],
                     r=wft_sb[:, kc, ng * 512:(ng + 1) * 512],
                     st=(kc == 0):
                     e.matmul(o, lhsT=l, rhs=r, start=st, stop=False),
                     ev_afm + w0, inc=None)
        evmm = [None] * 4
        for ng in range(4):
            evmm[ng] = p.op(
                "tensor",
                lambda e, o=pb[ng][:, :], l=onesr_sb[:, :],
                r=robuf[0:1, ng * 512:(ng + 1) * 512]:
                e.matmul(o, lhsT=l, rhs=r, start=False, stop=True),
                [ev_bf], inc="pe")
        ev_afm_used[ap_] = evmm[3]
        st4["evmm"] = evmm
        if tt >= 1:
            tail_b(tt - 1)
            if tt + 2 < 16:
                ev_y1l[(tt + 2) % 3] = p.dma(
                    "scalar", y1l[(tt + 2) % 3][:, :],
                    Y1s.ap()[(tt + 2) * 128:(tt + 2) * 128 + 128, :],
                    [ev_y1_store[tt % 2], ev_y1l_used[(tt + 2) % 3]])
    tail_a(15)
    tail_b(15)

    dbg = []
    if debug:
        for name, th, shape, dt in (
                ("dbg_qs", Qs, [8, 128, T], BF16),
                ("dbg_ks", Ks, [8, 128, T], BF16),
                ("dbg_vs", Vs, [8, T, 128], BF16),
                ("dbg_ysin", Ysin, [1024, T], BF16),
                ("dbg_poha", POhA, [1024, D], BF16),
                ("dbg_pohb", POhB, [1024, D], BF16),
                ("dbg_pos", POs, [T, D], BF16),
                ("dbg_wfg", Wfg, [8 * NCH * 128, 256], BF16),
                ("dbg_y1", Y1s, [TPC, D], F32)):
            dap = nc.dram_tensor(name, shape, dt, kind="ExternalOutput").ap()
            p.dma("sync", dap, th.ap(), p.snap())
            dbg.append(name)

    fin = p.snap()
    p.op("sync", lambda e: e.nop(), fin)

    with ExitStack() as ectx:
        sems = {name: ectx.enter_context(nc.semaphore(name))
                for name in ("qd", "ad", "gd", "pe", "ve", "ae", "po", "cc")}
        block = ectx.enter_context(nc.Block())

        def emitter(engname):
            def run(eng):
                lastw = {}
                for (en, w, fn, inc, n) in p.ops:
                    if en != engname:
                        continue
                    for s, v in w.items():
                        if lastw.get(s, -1) >= v:
                            continue
                        eng.wait_ge(sems[s], v)
                        lastw[s] = v
                    inst = fn(eng)
                    if inc is not None:
                        inst.then_inc(sems[inc], n)
            return run

        block.sync(emitter("sync"))
        block.gpsimd(emitter("gpsimd"))
        block.tensor(emitter("tensor"))
        block.vector(emitter("vector"))
        block.scalar(emitter("scalar"))

    s4.close()
    ctx.close()
    return nc, dbg


def _host_inputs(inputs, c):
    b, hh = c >> 1, c & 1
    heads = SET0 if hh == 0 else SET1
    cols = np.concatenate([np.arange(h * 128, (h + 1) * 128) for h in heads])
    f32 = np.float32
    xb = np.ascontiguousarray(np.asarray(inputs["L0_attn_x"][b], dtype=f32))
    d = {
        "x": xb,
        "xr": np.ascontiguousarray(xb[hh * TPC:(hh + 1) * TPC, :]),
        "wq": np.ascontiguousarray(np.asarray(inputs["L0_attn_Wq"], f32)[:, cols]),
        "wk": np.ascontiguousarray(np.asarray(inputs["L0_attn_Wk"], f32)[:, cols]),
        "wv": np.ascontiguousarray(np.asarray(inputs["L0_attn_Wv"], f32)[:, cols]),
        "bqr": np.ascontiguousarray(
            np.asarray(inputs["L0_attn_bq"], f32)[cols].reshape(8, 128).T),
        "bkr": np.ascontiguousarray(
            np.asarray(inputs["L0_attn_bk"], f32)[cols].reshape(8, 128).T),
        "bvr": np.ascontiguousarray(
            np.asarray(inputs["L0_attn_bv"], f32)[cols].reshape(1, 1024)),
        "wo": np.ascontiguousarray(np.asarray(inputs["L0_attn_Wo"], f32)[cols, :]),
        "bor": np.ascontiguousarray(
            np.asarray(inputs["L0_attn_bo"], f32).reshape(1, D)),
        "wf": np.ascontiguousarray(
            np.asarray(inputs["L0_ffn_weight"], f32)[c * 256:(c + 1) * 256, :]),
        "bfr": np.ascontiguousarray(
            np.asarray(inputs["L0_ffn_bias"], f32).reshape(1, D)),
        "eyeb": np.eye(128, dtype=ml_dtypes.bfloat16),
        "onesr": np.ones((1, 128), f32),
        "onesc": np.ones((128, 1), ml_dtypes.bfloat16),
        "alib": _alibi_table(hh),
        "maskt": _mask_table(),
    }
    return d


def kernel(**inputs):
    nc, _ = _build(debug=False)
    in_maps = [_host_inputs(inputs, c) for c in range(8)]
    res = run_bass_kernel_spmd(nc, in_maps, core_ids=list(range(8)))
    out = np.empty((B, T, D), np.float32)
    for c in range(8):
        b, hh = c >> 1, c & 1
        out[b, hh * TPC:(hh + 1) * TPC, :] = res.results[c]["out"]
    return out


# version 14
# speedup vs baseline: 1.0536x; 1.0263x over previous
import math
from collections import defaultdict
from contextlib import ExitStack

import numpy as np
import ml_dtypes

import concourse.bass as bass
import concourse.mybir as mybir
from concourse.bass_utils import run_bass_kernel_spmd

F32 = mybir.dt.float32
BF16 = mybir.dt.bfloat16
AF = mybir.ActivationFunctionType
ALU = mybir.AluOpType
AX = mybir.AxisListType

B, T, D = 4, 4096, 2048
NCH = D // 128
TPC = T // 2
SET0 = [0, 3, 4, 7, 8, 11, 12, 15]
SET1 = [1, 2, 5, 6, 9, 10, 13, 14]
SLOPES = [2.0 ** (-(g + 1) / 2) for g in range(16)]
QSCALE = 1.0 / math.sqrt(128.0)
NEGBIG = -1.0e30
LNEPS = 1e-5
W_EMIN, W_EMAX = -16, 4
A_EMIN, A_EMAX = -16, 7


def _attn_plan():
    colkey = {}
    plans = []
    for j in range(8):
        a0, a1 = SLOPES[SET0[j]], SLOPES[SET1[j]]
        amax, amin = max(a0, a1), min(a0, a1)
        G = 128 if amax > 0.4 else (256 if amax > 0.2 else 512)
        reach = 115.0 / amin + 128.0
        qgs = []
        for q0 in range(0, T, G):
            ref = q0 + G // 2
            kmin = max(0, int(math.floor((ref - reach) / 128.0)) * 128)
            kts = []
            for k0 in range(kmin, q0 + G, 128):
                key = (j, k0 - ref)
                if key not in colkey:
                    colkey[key] = len(colkey)
                delta = k0 - q0
                mi = delta // 128 if delta >= 0 else None
                kts.append((k0, colkey[key], mi))
            qgs.append((q0, ref, kts))
        plans.append((G, qgs))
    return plans, colkey


PLANS, COLKEY = _attn_plan()
NCOL = len(COLKEY)
assert NCOL <= 512, NCOL


def _alibi_table(hh):
    arr = np.zeros((128, NCOL), np.float32)
    heads = SET0 if hh == 0 else SET1
    pp = np.arange(128, dtype=np.float64)
    for (j, off), idx in COLKEY.items():
        arr[:, idx] = (SLOPES[heads[j]] * (off + pp)).astype(np.float32)
    return np.ascontiguousarray(arr)


def _mask_table():
    m = np.zeros((128, 4, 512), np.float32)
    pp = np.arange(128)[:, None]
    ff = np.arange(512)[None, :]
    for d in range(4):
        m[:, d, :] = np.where(ff >= pp + 128 * d, 0.0, NEGBIG)
    return np.ascontiguousarray(m.reshape(128, 4 * 512))


class Plan:
    def __init__(self):
        self.ops = []
        self.cnt = defaultdict(int)

    def op(self, eng, fn, waits=(), inc=None, n=1):
        w = {}
        for ev in waits:
            if ev is None:
                continue
            s, v = ev
            if w.get(s, -1) < v:
                w[s] = v
        evt = None
        if inc is not None:
            self.cnt[inc] += n
            evt = (inc, self.cnt[inc])
        self.ops.append((eng, w, fn, inc, n))
        return evt

    def dma(self, queue, out_ap, in_ap, waits=()):
        sem = {"sync": "qd", "scalar": "ad", "gpsimd": "gd"}[queue]
        fn = lambda e, o=out_ap, i=in_ap: e.dma_start(out=o, in_=i)
        return self.op(queue, fn, waits, inc=sem, n=16)

    def snap(self):
        return [(s, v) for s, v in self.cnt.items()]


def _emit_scale_ladder(p, amax, scl, rscl, t16, waits, emin, emax):
    e1 = p.op("vector", lambda e, o=amax: e.tensor_scalar(
        o, o, float(2.0 ** emin), None, ALU.max), list(waits), inc="ve")
    es = p.op("vector", lambda e, o=scl: e.memset(o, float(2.0 ** (emin - 2))),
              list(waits), inc="ve")
    for ee in range(emin + 1, emax + 1):
        et = p.op("vector", lambda e, o=t16, i=amax, th=float(2.0 ** ee):
                  e.tensor_scalar(o, i, th, 1.0, ALU.is_ge, ALU.add),
                  [e1, es], inc="ve")
        es = p.op("vector", lambda e, o=scl, i=t16: e.tensor_mul(o, o, i),
                  [et], inc="ve")
    er = p.op("vector", lambda e, o=rscl, i=scl: e.reciprocal(o, i),
              [es], inc="ve")
    es = p.op("vector", lambda e, o=scl: e.tensor_scalar(
        o, o, 0.5, None, ALU.mult), [es, er], inc="ve")
    return es, er


def _emit_round(p, src, out_bf, scl, rscl, scr, frees, waits, pool=False):
    W = src.shape[-1]
    nb = W // 32
    q, aq, sgn, tC, s2 = (scr["q"], scr["aq"], scr["sgn"], scr["tC"],
                          scr["s2"])
    win = src.rearrange("p (g w) -> p g w", w=32)
    rb = rscl.unsqueeze(2).broadcast_to([128, nb, 32])
    qv = q.rearrange("p (g w) -> p g w", w=32)
    eq = p.op("vector", lambda e, o=qv, i=win, r=rb: e.tensor_tensor(
        o, i, r, ALU.mult), list(waits) + [frees.get("q")], inc="ve")
    ea = p.op("scalar", lambda e, o=aq, i=q: e.activation(o, i, AF.Abs),
              [eq, frees.get("aq")], inc="ae")
    eg = p.op("scalar", lambda e, o=sgn, i=q: e.activation(o, i, AF.Sign),
              [eq, frees.get("sgn")], inc="ae")
    eA = p.op("vector", lambda e, o=q, i=aq: e.tensor_scalar(
        o, i, 0.25, None, ALU.is_gt), [ea, eg, eq], inc="ve")
    eA = p.op("vector", lambda e, o=q, i=aq: e.scalar_tensor_tensor(
        o, i, 0.75, o, ALU.is_ge, ALU.add), [eA], inc="ve")
    eA = p.op("vector", lambda e, o=q, i=aq: e.scalar_tensor_tensor(
        o, i, 1.25, o, ALU.is_gt, ALU.add), [eA], inc="ve")
    eA = p.op("vector", lambda e, o=q, i=aq: e.scalar_tensor_tensor(
        o, i, 1.75, o, ALU.is_ge, ALU.add), [eA], inc="ve")
    eU = p.op("vector", lambda e, o=tC, i=aq: e.tensor_scalar(
        o, i, 5.0, 2.0, ALU.is_gt, ALU.mult), [ea, frees.get("tC")], inc="ve")
    eU = p.op("vector", lambda e, o=tC, i=aq: e.scalar_tensor_tensor(
        o, i, 2.5, o, ALU.is_gt, ALU.add), [eU], inc="ve")
    eU = p.op("vector", lambda e, o=tC, i=aq: e.scalar_tensor_tensor(
        o, i, 3.5, o, ALU.is_ge, ALU.add), [eU], inc="ve")
    eA = p.op("vector", lambda e, o=q, i=tC: e.scalar_tensor_tensor(
        o, i, 2.0, o, ALU.mult, ALU.add), [eA, eU], inc="ve")
    sb_ = scl.unsqueeze(2).broadcast_to([128, nb, 32])
    s2v = s2.rearrange("p (g w) -> p g w", w=32)
    sgv = sgn.rearrange("p (g w) -> p g w", w=32)
    e2 = p.op("gpsimd", lambda e, o=s2v, i=sgv, s_=sb_: e.tensor_tensor(
        o, i, s_, ALU.mult), [eg, frees.get("s2")], inc="po")
    eo = p.op("gpsimd", lambda e, o=out_bf, i=q, s_=s2: e.tensor_tensor(
        o, i, s_, ALU.mult), [eA, e2, frees.get("out")], inc="po")
    newfrees = {"q": eo, "aq": eA, "sgn": e2, "tC": eA, "s2": eo}
    return eo, newfrees


def _build(debug=False):
    nc = bass.Bass("TRN2", target_bir_lowering=False, debug=False)

    x = nc.dram_tensor("x", [T, D], F32, kind="ExternalInput").ap()
    xr = nc.dram_tensor("xr", [TPC, D], F32, kind="ExternalInput").ap()
    wq = nc.dram_tensor("wq", [D, 1024], F32, kind="ExternalInput").ap()
    wk = nc.dram_tensor("wk", [D, 1024], F32, kind="ExternalInput").ap()
    wv = nc.dram_tensor("wv", [D, 1024], F32, kind="ExternalInput").ap()
    bqr = nc.dram_tensor("bqr", [128, 8], F32, kind="ExternalInput").ap()
    bkr = nc.dram_tensor("bkr", [128, 8], F32, kind="ExternalInput").ap()
    bvr = nc.dram_tensor("bvr", [1, 1024], F32, kind="ExternalInput").ap()
    wo = nc.dram_tensor("wo", [1024, D], F32, kind="ExternalInput").ap()
    bor = nc.dram_tensor("bor", [1, D], F32, kind="ExternalInput").ap()
    wf = nc.dram_tensor("wf", [256, D], F32, kind="ExternalInput").ap()
    bfr = nc.dram_tensor("bfr", [1, D], F32, kind="ExternalInput").ap()
    eyeb = nc.dram_tensor("eyeb", [128, 128], BF16, kind="ExternalInput").ap()
    onesr = nc.dram_tensor("onesr", [1, 128], F32, kind="ExternalInput").ap()
    onesc = nc.dram_tensor("onesc", [128, 1], BF16, kind="ExternalInput").ap()
    alib = nc.dram_tensor("alib", [128, NCOL], F32, kind="ExternalInput").ap()
    maskt = nc.dram_tensor("maskt", [128, 4 * 512], F32, kind="ExternalInput").ap()
    out = nc.dram_tensor("out", [TPC, D], F32, kind="ExternalOutput").ap()

    Qs = nc.dram_tensor("Qs", [8, 128, T], BF16, kind="Internal")
    Ks = nc.dram_tensor("Ks", [8, 128, T], BF16, kind="Internal")
    Vs = nc.dram_tensor("Vs", [8, T, 128], BF16, kind="Internal")
    Ysin = nc.dram_tensor("Ysin", [1024, T], BF16, kind="Internal")
    POs = nc.dram_tensor("POs", [T, D], BF16, kind="Internal")
    POhA = nc.dram_tensor("POhA", [1024, D], BF16, kind="Internal")
    POhB = nc.dram_tensor("POhB", [1024, D], BF16, kind="Internal")
    Wfc = nc.dram_tensor("Wfc", [NCH * 128, 256], BF16, kind="Internal")
    Wfg = nc.dram_tensor("Wfg", [8 * NCH * 128, 256], BF16, kind="Internal",
                         addr_space="Shared")
    Y1s = nc.dram_tensor("Y1s", [TPC, D], F32, kind="Internal")

    p = Plan()
    ctx = ExitStack()

    def SB(scope, name, shape, dt):
        return scope.enter_context(nc.sbuf_tensor(name, shape, dt))

    eyeb_sb = SB(ctx, "eyeb_sb", [128, 128], BF16)
    onesr_sb = SB(ctx, "onesr_sb", [1, 128], F32)
    onesc_sb = SB(ctx, "onesc_sb", [128, 1], BF16)
    alib_sb = SB(ctx, "alib_sb", [128, NCOL], F32)
    bq_sb = SB(ctx, "bq_sb", [128, 8], F32)
    bk_sb = SB(ctx, "bk_sb", [128, 8], F32)
    robuf = SB(ctx, "robuf", [1, D], F32)
    bobf_bc = SB(ctx, "bobf_bc", [128, D], F32)
    amaxA = SB(ctx, "amaxA", [128, NCH, 64], F32)
    sclA = SB(ctx, "sclA", [128, NCH * 64], F32)
    rsclA = SB(ctx, "rsclA", [128, NCH * 64], F32)

    pb = [ctx.enter_context(nc.psum_tensor(f"pb{i}", [128, 512], F32))
          for i in range(8)]

    def pbbf(i):
        return pb[i][:, :].bitcast(BF16)[:, 0:512]

    ev_eye = p.dma("sync", eyeb_sb[:, :], eyeb)
    ev_onesr = p.dma("sync", onesr_sb[:, :], onesr)
    ev_onesc = p.dma("sync", onesc_sb[:, :], onesc)
    ev_alib = p.dma("sync", alib_sb[:, :], alib)
    ev_bq = p.dma("sync", bq_sb[:, :], bqr)
    ev_bk = p.dma("sync", bk_sb[:, :], bkr)
    ev_bv = p.dma("sync", robuf[0:1, 0:1024], bvr)

    s1 = ExitStack()
    wq_sb = SB(s1, "wq_sb", [128, NCH, 1024], BF16)
    wk_sb = SB(s1, "wk_sb", [128, NCH, 1024], BF16)
    wv_sb = SB(s1, "wv_sb", [128, NCH, 1024], BF16)
    xb_sb = [SB(s1, f"xb{i}", [128, 4, D], BF16) for i in range(2)]
    xfm_sb = SB(s1, "xfm", [128, NCH, 512], BF16)
    qkst = [SB(s1, f"qkst{i}", [128, 512], BF16) for i in range(2)]
    vst = [SB(s1, f"vst{i}", [128, 512], BF16) for i in range(2)]
    bv_bc = SB(s1, "bv_bc", [128, 1024], F32)
    wt_sb = [SB(s1, f"wt{i}", [128, 512], F32) for i in range(2)]
    amaxW = SB(s1, "amaxW", [128, 8, 16], F32)
    sclW = SB(s1, "sclW", [128, 128], F32)
    rsclW = SB(s1, "rsclW", [128, 128], F32)
    t16W = SB(s1, "t16W", [128, 128], F32)
    wscr = dict(
        q=SB(s1, "w_q", [128, 512], F32)[:, :],
        aq=SB(s1, "w_aq", [128, 512], F32)[:, :],
        sgn=SB(s1, "w_sgn", [128, 512], BF16)[:, :],
        tC=SB(s1, "w_tC", [128, 512], F32)[:, :],
        s2=SB(s1, "w_s2", [128, 512], BF16)[:, :],
    )
    wq_tm = [SB(s1, f"wq_tm{i}", [128, 512], BF16) for i in range(2)]
    wftst = [SB(s1, f"wftst{i}", [128, 4, 128], BF16) for i in range(2)]

    wstate = {"wt": [None, None], "wt_used": [None, None], "ladder": None,
              "frees": {}, "evft": None, "evft_store": [None, None],
              "amax_done": [], "stores": [], "evq": {},
              "out_free": [None, None]}

    def w_unit_src(u):
        r, cc = u % 2, u // 2
        return wf[r * 128:(r + 1) * 128, cc * 512:(cc + 1) * 512]

    def w_pass1_pair(g):
        for u in (2 * g, 2 * g + 1):
            sl = u % 2
            wstate["wt"][sl] = p.dma(
                "scalar", wt_sb[sl][:, :], w_unit_src(u),
                [wstate["wt"][sl], wstate["wt_used"][sl]])
            wstate["amax_done"].append(p.op(
                "vector",
                lambda e, o=amaxW[:, u, :],
                i=wt_sb[sl][:, :].rearrange("p (g w) -> p g w", w=32):
                e.reduce_max(o, i, axis=AX.X, apply_absolute_value=True),
                [wstate["wt"][sl]], inc="ve"))
            wstate["wt_used"][sl] = wstate["amax_done"][-1]

    def w_round(u):
        sl = u % 2
        wstate["wt"][sl] = p.dma(
            "scalar", wt_sb[sl][:, :], w_unit_src(u),
            [wstate["wt"][sl], wstate["wt_used"][sl]])
        evq, wstate["frees"] = _emit_round(
            p, wt_sb[sl][:, :], wq_tm[sl][:, :],
            sclW[:, u * 16:(u + 1) * 16], rsclW[:, u * 16:(u + 1) * 16],
            wscr, dict(wstate["frees"], out=wstate["out_free"][sl]),
            [wstate["wt"][sl]] + list(wstate["ladder"]))
        wstate["wt_used"][sl] = wstate["frees"]["q"]
        wstate["evq"][u] = evq

    def w_store(u):
        sl = u % 2
        evq = wstate["evq"][u]
        r, cc = u % 2, u // 2
        tb = u % 2
        evt = None
        for kk in range(4):
            evt = p.op(
                "tensor",
                lambda e, o=pbbf(7)[:, kk * 128:(kk + 1) * 128],
                i=wq_tm[sl][:, kk * 128:(kk + 1) * 128],
                idn=eyeb_sb[:, :]: e.transpose(o, i, idn),
                [evq, ev_eye, wstate["evft"]],
                inc="pe" if kk == 3 else None)
        wstate["evft"] = p.op(
            "scalar",
            lambda e, o=wftst[tb][:, :, :].rearrange("p a b -> p (a b)"),
            i=pbbf(7)[:, :]: e.activation(o, i, AF.Copy),
            [evt, wstate["evft_store"][tb]], inc="ae")
        wstate["evft_store"][tb] = p.dma(
            "scalar",
            Wfc.ap().rearrange("(c p) n -> c p n", p=128)
            [cc * 4:(cc + 1) * 4, :, r * 128:(r + 1) * 128]
            .rearrange("c k n -> k c n"),
            wftst[tb][:, :, :],
            [wstate["evft"]])
        wstate["stores"].append(wstate["evft_store"][tb])
        wstate["out_free"][sl] = evt


    ev_bvbc = None
    for half in range(2):
        evm = p.op("tensor", lambda e, o=pb[2][:, :], l=onesr_sb[:, :],
                   r=robuf[0:1, half * 512:(half + 1) * 512]:
                   e.matmul(o, lhsT=l, rhs=r, start=True, stop=True),
                   [ev_onesr, ev_bv, ev_bvbc], inc="pe")
        ev_bvbc = p.op("scalar",
                       lambda e, o=bv_bc[:, half * 512:(half + 1) * 512],
                       i=pb[2][:, :]: e.activation(o, i, AF.Copy),
                       [evm], inc="ae")
    ev_robuf_free = ev_bvbc

    ev_wq = p.dma("gpsimd", wq_sb[:, :, :],
                  wq.rearrange("(c p) f -> p c f", p=128))
    ev_wk = p.dma("gpsimd", wk_sb[:, :, :],
                  wk.rearrange("(c p) f -> p c f", p=128))
    ev_wv = p.dma("gpsimd", wv_sb[:, :, :],
                  wv.rearrange("(c p) f -> p c f", p=128))

    ev_xb = [None, None]
    ev_xb_used = [None, None]
    ev_tr_evac = [None, None]
    ev_qk_evac = [ev_bvbc, None]
    ev_v_evac = [None, None]
    ev_qkst_store = [None, None]
    ev_vst_store = [None, None]
    ev_qs_store = {}
    ev_ks_store = {}

    for grp in range(8):
        t0 = grp * 512
        par = grp % 2
        ev_xb[par] = p.dma(
            "gpsimd", xb_sb[par][:, :, :],
            x[t0:t0 + 512, :].rearrange("(n p) f -> p n f", p=128),
            [ev_xb_used[par]])
        evtr = None
        for c in range(NCH):
            tb = c % 2
            for tt in range(4):
                evtr = p.op(
                    "tensor",
                    lambda e, o=pbbf(tb)[:, tt * 128:(tt + 1) * 128],
                    i=xb_sb[par][:, tt, c * 128:(c + 1) * 128],
                    idn=eyeb_sb[:, :]: e.transpose(o, i, idn),
                    [ev_xb[par], ev_eye, ev_tr_evac[tb]],
                    inc="pe" if tt == 3 else None)
            ev_tr_evac[tb] = p.op(
                "scalar", lambda e, o=xfm_sb[:, c, :], i=pbbf(tb)[:, :]:
                e.activation(o, i, AF.Copy), [evtr], inc="ae")
        ev_xb_used[par] = evtr
        ev_xfm = ev_tr_evac[(NCH - 1) % 2]
        ev_xfm2 = ev_tr_evac[NCH % 2]

        for kind in ("k", "q"):
            wsb = wq_sb if kind == "q" else wk_sb
            ev_w = ev_wq if kind == "q" else ev_wk
            bias_sb = bq_sb if kind == "q" else bk_sb
            dst = Qs if kind == "q" else Ks
            for fc in range(8):
                bank = 2 + (fc % 2)
                evm = None
                for kc in range(NCH):
                    evm = p.op(
                        "tensor",
                        lambda e, o=pb[bank][:, :],
                        l=wsb[:, kc, fc * 128:(fc + 1) * 128],
                        r=xfm_sb[:, kc, :], st=(kc == 0), sp=(kc == NCH - 1):
                        e.matmul(o, lhsT=l, rhs=r, start=st, stop=sp),
                        [ev_w, ev_xfm, ev_xfm2, ev_qk_evac[fc % 2]],
                        inc="pe" if kc == NCH - 1 else None)
                sp_ = fc % 2
                ev_qk_evac[sp_] = p.op(
                    "scalar",
                    lambda e, o=qkst[sp_][:, :], i=pb[bank][:, :],
                    bi=bias_sb[:, fc:fc + 1]:
                    e.activation(o, i, AF.Identity, bias=bi),
                    [evm, ev_qkst_store[sp_]], inc="ae")
                ev_qkst_store[sp_] = p.dma(
                    "sync", dst.ap()[fc, :, t0:t0 + 512], qkst[sp_][:, :],
                    [ev_qk_evac[sp_]])
                (ev_qs_store if kind == "q" else ev_ks_store)[fc] = \
                    ev_qkst_store[sp_]

        for ts4 in range(4):
            for fg in range(2):
                sp_ = (ts4 * 2 + fg) % 2
                bank = 4 + sp_
                evm = None
                for kc in range(NCH):
                    evm = p.op(
                        "tensor",
                        lambda e, o=pb[bank][:, :],
                        l=xfm_sb[:, kc, ts4 * 128:(ts4 + 1) * 128],
                        r=wv_sb[:, kc, fg * 512:(fg + 1) * 512],
                        st=(kc == 0), sp=(kc == NCH - 1):
                        e.matmul(o, lhsT=l, rhs=r, start=st, stop=sp),
                        [ev_wv, ev_xfm, ev_xfm2, ev_v_evac[sp_], ev_bvbc],
                        inc="pe" if kc == NCH - 1 else None)
                ev_v_evac[sp_] = p.op(
                    "vector",
                    lambda e, o=vst[sp_][:, :], i=pb[bank][:, :],
                    b_=bv_bc[:, fg * 512:(fg + 1) * 512]:
                    e.tensor_tensor(o, i, b_, ALU.add),
                    [evm, ev_vst_store[sp_]], inc="ve")
                ev_vst_store[sp_] = p.dma(
                    "sync",
                    Vs.ap()[fg * 4:(fg + 1) * 4,
                            t0 + ts4 * 128:t0 + (ts4 + 1) * 128, :]
                    .rearrange("j t d -> t j d"),
                    vst[sp_][:, :].rearrange("p (j d) -> p j d", d=128),
                    [ev_v_evac[sp_]])

        if grp < 4:
            w_pass1_pair(grp)
            if grp == 3:
                wstate["ladder"] = _emit_scale_ladder(
                    p, amaxW[:, :, :].rearrange("p a b -> p (a b)"),
                    sclW[:, :], rsclW[:, :], t16W[:, :],
                    wstate["amax_done"], W_EMIN, W_EMAX)
        else:
            for u in (2 * (grp - 5), 2 * (grp - 5) + 1):
                if u >= 0:
                    w_store(u)
            for u in (2 * (grp - 4), 2 * (grp - 4) + 1):
                w_round(u)
    for u in (6, 7):
        w_store(u)

    s1_end = p.snap()
    s1.close()

    s2keep = ExitStack()
    wo_sb = SB(s2keep, "wo_sb", [128, 8, D], BF16)
    s2 = ExitStack()
    maskt_sb = SB(s2, "maskt_sb", [128, 4, 512], F32)
    khead = [SB(s2, f"khead{i}", [128, T], BF16) for i in range(2)]
    vhead = [SB(s2, f"vhead{i}", [128, 32, 128], BF16) for i in range(2)]
    qhead = [SB(s2, f"qhead{i}", [128, T], BF16) for i in range(2)]
    expb = [SB(s2, f"expb{i}", [128, 512], BF16) for i in range(6)]
    dn_sb = [SB(s2, f"dn_sb{i}", [1, 512], F32) for i in range(2)]
    rec_sb = [SB(s2, f"rec_sb{i}", [1, 512], F32) for i in range(2)]
    bcm_sb = [SB(s2, f"bcm_sb{i}", [128, 512], F32) for i in range(2)]
    yst = [SB(s2, f"yst{i}", [128, 512], BF16) for i in range(2)]
    ev_mask = p.dma("sync", maskt_sb[:, :, :].rearrange("p a b -> p (a b)"),
                    maskt, s1_end)
    ev_wo = None
    for sl in range(2):
        waits = list(s1_end) if sl == 0 else []
        ev_wo = p.dma(
            "gpsimd", wo_sb[:, sl * 4:(sl + 1) * 4, :],
            wo[sl * 512:(sl + 1) * 512, :].rearrange("(c p) f -> p c f", p=128),
            waits + [ev_wo])
    ev_wo_done = ev_wo

    ev_kh = [None, None]
    ev_vh = [None, None]
    ev_qh = [None, None]
    last_pe_head = [None, None]
    exp_free = [None] * 6
    S_free = {0: None, 1: None, 5: None}
    pv_free = [None, None]
    dn_free = [None, None]
    rec_free = [None, None]
    dnsb_free = [None, None]
    bcm_free = [None, None]
    yst_free = [None, None]
    ygl = 0
    gti = [0]
    bcst = {"bank": None, "yp": 0}
    SBANKS = (0, 1, 5)

    for j in range(8):
        hp = j % 2
        w0 = s1_end if j < 2 else []
        ev_qh[hp] = p.dma("sync", qhead[hp][:, :], Qs.ap()[j, :, :],
                          w0 + [ev_qs_store[j], last_pe_head[hp], ev_qh[hp]])
        ev_kh[hp] = p.dma("sync", khead[hp][:, :], Ks.ap()[j, :, :],
                          w0 + [ev_ks_store[j], last_pe_head[hp], ev_kh[hp]])
        ev_vh[hp] = p.dma("sync", vhead[hp][:, :, :],
                          Vs.ap()[j, :, :].rearrange("(a p) d -> p a d", p=128),
                          w0 + [ev_vst_store[0], ev_vst_store[1],
                                last_pe_head[hp], ev_vh[hp]])
        G, qgs = PLANS[j]
        pend = []

        def pop_one():
            fn_, i_, xp_, ev_, epi_ = pend.pop(0)
            pv_ = fn_(i_, xp_, ev_)
            exp_free[xp_] = pv_
            if epi_ is not None:
                epi_(pv_)

        for (q0, ref, kts) in qgs:
            qp = ygl % 2
            pvbank = 2 + qp
            dnrow = 32 * qp
            nt = len(kts)

            def emit_dnpv(i, xp, evexp, qp=qp, dnrow=dnrow, pvbank=pvbank,
                          G=G, kts=kts, nt=nt, hp=hp):
                first, last = (i == 0), (i == nt - 1)
                p.op("tensor",
                     lambda e, o=pb[4][dnrow:dnrow + 1, 0:G], l=onesc_sb[:, :],
                     r=expb[xp][:, 0:G], st=first, sp=last:
                     e.matmul(o, lhsT=l, rhs=r, start=st, stop=sp),
                     [evexp, ev_onesc] + ([dn_free[qp]] if first else []),
                     inc=None)
                kloc = kts[i][0] // 128
                return p.op(
                    "tensor",
                    lambda e, o=pb[pvbank][:, 0:G],
                    l=vhead[hp][:, kloc, :], r=expb[xp][:, 0:G],
                    st=first, sp=last:
                    e.matmul(o, lhsT=l, rhs=r, start=st, stop=sp),
                    [evexp, ev_vh[hp]] + ([pv_free[qp]] if first else []),
                    inc="pe")

            def epilogue(ev_pvlast, qp=qp, dnrow=dnrow, pvbank=pvbank,
                         G=G, q0=q0, j=j, hp=hp):
                ev_dnev = p.op("scalar",
                               lambda e, o=dn_sb[qp][0:1, 0:G],
                               i=pb[4][dnrow:dnrow + 1, 0:G]:
                               e.activation(o, i, AF.Copy),
                               [ev_pvlast, dnsb_free[qp]], inc="ae")
                dn_free[qp] = ev_dnev
                ev_rec = p.op("vector",
                              lambda e, o=rec_sb[qp][0:1, 0:G],
                              i=dn_sb[qp][0:1, 0:G]:
                              e.reciprocal(o, i), [ev_dnev, rec_free[qp]],
                              inc="ve")
                dnsb_free[qp] = ev_rec
                ev_bcmm = p.op("tensor",
                               lambda e, o=pb[6][:, 0:G], l=onesr_sb[:, :],
                               r=rec_sb[qp][0:1, 0:G]:
                               e.matmul(o, lhsT=l, rhs=r, start=True,
                                        stop=True),
                               [ev_rec, ev_onesr, bcst["bank"]], inc="pe")
                rec_free[qp] = ev_bcmm
                ev_bcev = p.op("scalar",
                               lambda e, o=bcm_sb[qp][:, 0:G], i=pb[6][:, 0:G]:
                               e.activation(o, i, AF.Copy),
                               [ev_bcmm, bcm_free[qp]], inc="ae")
                bcst["bank"] = ev_bcev
                yp = bcst["yp"]
                bcst["yp"] = 1 - yp
                ev_ym = p.op("vector",
                             lambda e, o=yst[yp][:, 0:G], i=pb[pvbank][:, 0:G],
                             b_=bcm_sb[qp][:, 0:G]:
                             e.tensor_tensor(o, i, b_, ALU.mult),
                             [ev_bcev, ev_pvlast, yst_free[yp]], inc="ve")
                pv_free[qp] = ev_ym
                bcm_free[qp] = ev_ym
                yst_free[yp] = p.dma(
                    "scalar",
                    Ysin.ap()[j * 128:(j + 1) * 128, q0:q0 + G],
                    yst[yp][:, 0:G], [ev_ym])
                last_pe_head[hp] = ev_ym

            for i, (k0, colidx, mi) in enumerate(kts):
                sp_ = SBANKS[gti[0] % 3]
                xp = gti[0] % 6
                gti[0] += 1
                evS = p.op(
                    "tensor",
                    lambda e, o=pb[sp_][:, 0:G],
                    l=khead[hp][:, k0:k0 + 128],
                    r=qhead[hp][:, q0:q0 + G]:
                    e.matmul(o, lhsT=l, rhs=r, start=True, stop=True),
                    [ev_kh[hp], ev_qh[hp], S_free[sp_]], inc="pe")
                if mi is not None:
                    evS = p.op(
                        "vector",
                        lambda e, o=pb[sp_][:, 0:G], m=maskt_sb[:, mi, 0:G]:
                        e.tensor_tensor(o, o, m, ALU.add),
                        [evS, ev_mask], inc="ve")
                evexp = p.op(
                    "scalar",
                    lambda e, o=expb[xp][:, 0:G], i=pb[sp_][:, 0:G],
                    bi=alib_sb[:, colidx:colidx + 1]:
                    e.activation(o, i, AF.Exp, bias=bi, scale=QSCALE),
                    [evS, ev_alib, exp_free[xp]], inc="ae")
                S_free[sp_] = evexp
                pend.append((emit_dnpv, i, xp, evexp,
                             epilogue if i == nt - 1 else None))
                if len(pend) > 4:
                    pop_one()
            ygl += 1
        while pend:
            pop_one()
        if j == 1:
            ev_wcc = p.op("gpsimd",
                          lambda e: e.collective_compute(
                              "AllGather", ALU.bypass,
                              replica_groups=[list(range(8))],
                              ins=[Wfc.ap().opt()], outs=[Wfg.ap().opt()]),
                          wstate["stores"], inc="cc")

    s2_all = p.snap()

    s25 = ExitStack()
    ylhs = [SB(s25, f"ylhs{i}", [128, 8, 128], BF16) for i in range(4)]
    post = [SB(s25, f"post{i}", [128, D], BF16) for i in range(2)]

    ev_yl = [None] * 4
    ev_yl_used = [None] * 4
    ev_post_store = [None, None]
    bank_free3 = [None, None, None, None]
    TTORD = list(range(0, 8)) + list(range(16, 24)) +         list(range(8, 16)) + list(range(24, 32))
    GORD = [0, 2, 1, 3]
    ev_cc_a = None
    for it, tt in enumerate(TTORD):
        tp = it % 4
        pp_ = it % 2
        tg = tt * 128
        rowoff = GORD.index(tt // 8) * 1024 + (tt % 8) * 128
        ev_yl[tp] = p.dma(
            "sync" if it % 2 == 0 else "scalar", ylhs[tp][:, :, :],
            Ysin.ap()[:, tg:tg + 128].rearrange("(c p) t -> p c t", p=128),
            (s2_all if it < 4 else []) + [ev_yl_used[tp]])
        evmm = [None] * 4
        for dc in range(8):
            for ng in range(4):
                evmm[ng] = p.op(
                    "tensor",
                    lambda e, o=pb[ng][:, :], l=ylhs[tp][:, dc, :],
                    r=wo_sb[:, dc, ng * 512:(ng + 1) * 512],
                    st=(dc == 0), sp=(dc == 7):
                    e.matmul(o, lhsT=l, rhs=r, start=st, stop=sp),
                    (s2_all if it < 2 else []) +
                    [ev_yl[tp], ev_wo_done, bank_free3[ng]],
                    inc="pe" if dc == 7 else None)
        ev_yl_used[tp] = evmm[3]
        evev = [None] * 4
        for ng in range(4):
            evev[ng] = p.op(
                "scalar",
                lambda e, o=post[pp_][:, ng * 512:(ng + 1) * 512],
                i=pb[ng][:, :]: e.activation(o, i, AF.Copy),
                [evmm[ng], ev_post_store[pp_]], inc="ae")
            bank_free3[ng] = evev[ng]
        ev_post_store[pp_] = p.dma(
            "sync", POs.ap()[rowoff:rowoff + 128, :], post[pp_][:, :],
            [evev[0], evev[1], evev[2], evev[3]])
        if it == 15:
            half_a = p.snap()
            ev_cc_a = p.op("gpsimd",
                           lambda e: e.collective_compute(
                               "ReduceScatter", ALU.add,
                               replica_groups=[[0, 1], [2, 3], [4, 5], [6, 7]],
                               ins=[POs.ap()[0:2048, :].opt()],
                               outs=[POhA.ap().opt()]),
                           half_a, inc="cc")

    s25_all = p.snap()
    s25.close()
    ev_cc_b = p.op("gpsimd",
                   lambda e: e.collective_compute(
                       "ReduceScatter", ALU.add,
                       replica_groups=[[0, 1], [2, 3], [4, 5], [6, 7]],
                       ins=[POs.ap()[2048:4096, :].opt()],
                       outs=[POhB.ap().opt()]),
                   s25_all, inc="cc")
    s2.close()

    s3 = ExitStack()
    potile = [SB(s3, f"pot{i}", [128, D], BF16) for i in range(2)]
    xres = [SB(s3, f"xres{i}", [128, D], F32) for i in range(2)]
    xpbo_sb = SB(s3, "xpbo", [128, D], F32)
    r_sb = SB(s3, "r_sb", [128, D], F32)
    c_sb = SB(s3, "c_sb", [128, D], F32)
    m1_sb = SB(s3, "m1_sb", [128, 1], F32)
    m_sb = SB(s3, "m_sb", [128, 1], F32)
    v1_sb = SB(s3, "v1_sb", [128, 1], F32)
    rstd_sb = SB(s3, "rstd_sb", [128, 1], F32)
    y1st = [SB(s3, f"y1st{i}", [128, D], F32) for i in range(2)]

    ev_bo = p.dma("sync", robuf[0:1, :], bor, [ev_robuf_free] + s2_all)
    ev_bobc = None
    for sl in range(4):
        evm = p.op("tensor", lambda e, o=pb[4][:, :], l=onesr_sb[:, :],
                   r=robuf[0:1, sl * 512:(sl + 1) * 512]:
                   e.matmul(o, lhsT=l, rhs=r, start=True, stop=True),
                   s25_all + [ev_bo, ev_bobc], inc="pe")
        ev_bobc = p.op("scalar",
                       lambda e, o=bobf_bc[:, sl * 512:(sl + 1) * 512],
                       i=pb[4][:, :]: e.activation(o, i, AF.Copy),
                       [evm], inc="ae")
    ev_robuf_free = ev_bobc

    ev_po = [None, None]
    ev_po_used = [None, None]
    ev_xr = [None, None]
    ev_xr_used = [None, None]
    ev_xpbo_used = None
    ev_r_used = None
    ev_c_used = None
    ev_y1_store = [None, None]
    ev_amax3 = [None, None]

    for tt in range(16):
        tp = tt % 2
        tloc = tt * 128
        posrc = (POhA.ap()[tloc:tloc + 128, :] if tt < 8
                 else POhB.ap()[tloc - 1024:tloc - 1024 + 128, :])
        ev_po[tp] = p.dma("sync", potile[tp][:, :], posrc,
                          [ev_cc_a if tt < 8 else ev_cc_b, ev_po_used[tp]])
        ev_xr[tp] = p.dma("scalar", xres[tp][:, :], xr[tloc:tloc + 128, :],
                          (s25_all if tt < 2 else []) + [ev_xr_used[tp]])
        ev_xpbo = p.op("gpsimd",
                       lambda e, o=xpbo_sb[:, :], i=xres[tp][:, :],
                       b_=bobf_bc[:, :]: e.tensor_tensor(o, i, b_, ALU.add),
                       [ev_xr[tp], ev_bobc, ev_xpbo_used], inc="po")
        ev_xr_used[tp] = ev_xpbo
        evr = p.op("vector",
                   lambda e, o=r_sb[:, :], i=potile[tp][:, :],
                   b_=xpbo_sb[:, :]: e.tensor_tensor(o, i, b_, ALU.add),
                   [ev_po[tp], ev_xpbo, ev_r_used], inc="ve")
        ev_po_used[tp] = evr
        ev_xpbo_used = evr
        evm1 = p.op("vector", lambda e, o=m1_sb[:, :], i=r_sb[:, :]:
                    e.reduce_sum(o, i, axis=AX.X), [evr], inc="ve")
        evm = p.op("vector", lambda e, o=m_sb[:, :], i=m1_sb[:, :]:
                   e.tensor_scalar(o, i, -1.0 / D, None, ALU.mult),
                   [evm1], inc="ve")
        evc = p.op("scalar", lambda e, o=c_sb[:, :], i=r_sb[:, :],
                   m=m_sb[:, 0:1]:
                   e.activation(o, i, AF.Identity, bias=m),
                   [evm, ev_c_used], inc="ae")
        ev_r_used = evc
        evsq = p.op("scalar", lambda e, o=r_sb[:, :], i=c_sb[:, :],
                    a_=v1_sb[:, 0:1]:
                    e.activation(o, i, AF.Square, accum_out=a_),
                    [evc], inc="ae")
        ev_r_used = evsq
        evve = p.op("vector", lambda e, o=v1_sb[:, :]:
                    e.tensor_scalar(o, o, 1.0 / D, LNEPS, ALU.mult, ALU.add),
                    [evsq], inc="ve")
        evln = p.op("scalar", lambda e, o=rstd_sb[:, :], i=v1_sb[:, :]:
                    e.activation(o, i, AF.Ln), [evve], inc="ae")
        evex = p.op("scalar", lambda e, o=rstd_sb[:, :], i=rstd_sb[:, :]:
                    e.activation(o, i, AF.Exp, scale=-0.5), [evln], inc="ae")
        evy1 = p.op("scalar", lambda e, o=y1st[tp][:, :], i=c_sb[:, :],
                    m=rstd_sb[:, 0:1]:
                    e.activation(o, i, AF.Identity, scale=m),
                    [evex, ev_y1_store[tp], ev_amax3[tp]], inc="ae")
        ev_c_used = evy1
        ev_amax3[tp] = p.op(
            "vector",
            lambda e, o=amaxA[:, tt, :],
            i=y1st[tp][:, :].rearrange("p (g w) -> p g w", w=32):
            e.reduce_max(o, i, axis=AX.X, apply_absolute_value=True),
            [evy1], inc="ve")
        ev_y1_store[tp] = p.dma("sync", Y1s.ap()[tloc:tloc + 128, :],
                                y1st[tp][:, :], [evy1])

    s3_end = p.snap()
    s3.close()
    s2keep.close()

    s4 = ExitStack()
    wft_sb = SB(s4, "wft_sb", [128, NCH, D], BF16)
    y1l = [SB(s4, f"y1l{i}", [128, D], F32) for i in range(3)]
    t16A = SB(s4, "t16A", [128, NCH * 64], F32)
    ascr = dict(
        q=SB(s4, "a_q", [128, D], F32)[:, :],
        aq=SB(s4, "a_aq", [128, D], F32)[:, :],
        sgn=SB(s4, "a_sgn", [128, D], BF16)[:, :],
        tC=SB(s4, "a_tC", [128, D], F32)[:, :],
        s2=SB(s4, "a_s2", [128, D], BF16)[:, :],
    )
    aq_tm = [SB(s4, f"aq_tm{i}", [128, D], BF16) for i in range(2)]
    afm_sb = [SB(s4, f"afm{i}", [128, NCH, 128], BF16) for i in range(2)]
    r4_sb = SB(s4, "r4_sb", [128, D], F32)
    ffnout_sb = SB(s4, "ffnout", [128, D], F32)
    c4_sb = SB(s4, "c4_sb", [128, D], F32)
    m14_sb = SB(s4, "m14_sb", [128, 1], F32)
    m4_sb = SB(s4, "m4_sb", [128, 1], F32)
    v14_sb = SB(s4, "v14_sb", [128, 1], F32)
    rstd4_sb = SB(s4, "rstd4_sb", [128, 1], F32)
    outst = SB(s4, "outst", [128, D], F32)

    ev_ladA = _emit_scale_ladder(
        p, amaxA[:, :, :].rearrange("p a b -> p (a b)"),
        sclA[:, :], rsclA[:, :], t16A[:, :],
        s3_end, A_EMIN, A_EMAX)

    ev_y1l = [None, None, None]
    for tt in range(3):
        ev_y1l[tt] = p.dma("scalar", y1l[tt][:, :],
                           Y1s.ap()[tt * 128:tt * 128 + 128, :],
                           s3_end + [ev_y1_store[tt % 2]])

    ev_bf = p.dma("sync", robuf[0:1, :], bfr, [ev_robuf_free] + s3_end)
    ev_bfbc = None
    for sl in range(4):
        evm_ = p.op("tensor", lambda e, o=pb[6][:, :], l=onesr_sb[:, :],
                    r=robuf[0:1, sl * 512:(sl + 1) * 512]:
                    e.matmul(o, lhsT=l, rhs=r, start=True, stop=True),
                    s3_end + [ev_bf, ev_bfbc], inc="pe")
        ev_bfbc = p.op("scalar",
                       lambda e, o=bobf_bc[:, sl * 512:(sl + 1) * 512],
                       i=pb[6][:, :]: e.activation(o, i, AF.Copy),
                       [evm_], inc="ae")

    ev_wft_r = [None] * 8
    prevq = {"sync": None, "scalar": None}
    for r_ in range(8):
        qn = "sync" if r_ % 2 == 0 else "scalar"
        ev_wft_r[r_] = p.dma(
            qn, wft_sb[:, :, r_ * 256:(r_ + 1) * 256],
            Wfg.ap()[r_ * 2048:(r_ + 1) * 2048, :]
            .rearrange("(c p) n -> p c n", p=128),
            (s3_end if r_ < 2 else []) + [ev_wcc, prevq[qn]])
        prevq[qn] = ev_wft_r[r_]

    afree = {}
    ev_y1l_used = [None, None, None]
    ev_aqtm_used = [None, None]
    ev_afm_evac = [None, None]
    ev_afm_used = [None, None]
    bank_free4 = [None, None, None, None]
    ev_out_store = [None]
    st4 = {"r4_used": None, "c4_used": None, "evmm": None, "ffn_used": None}

    def tail_a(tt):
        evcp = None
        for ng in range(4):
            evcp = p.op("scalar",
                        lambda e, o=ffnout_sb[:, ng * 512:(ng + 1) * 512],
                        i=pb[ng][:, :]: e.activation(o, i, AF.Copy),
                        [st4["evmm"][ng], st4["ffn_used"]], inc="ae")
            bank_free4[ng] = evcp
        st4["evcp"] = evcp

    def tail_b(tt):
        tp = tt % 3
        op_ = tt % 2
        tloc = tt * 128
        evr = p.op("gpsimd",
                   lambda e, o=r4_sb[:, :], i0=ffnout_sb[:, :],
                   i1=bobf_bc[:, :]: e.tensor_tensor(o, i0, i1, ALU.add),
                   [st4["evcp"], st4["r4_used"], ev_bfbc], inc="po")
        st4["ffn_used"] = evr
        evr = p.op("gpsimd",
                   lambda e, o=r4_sb[:, :], i1=y1l[tp][:, :]:
                   e.tensor_tensor(o, o, i1, ALU.add),
                   [evr], inc="po")
        ev_y1l_used[tp] = evr
        st4["r4_used"] = evr
        evmean = p.op("scalar",
                      lambda e, o=c4_sb[:, :], i=r4_sb[:, :],
                      a_=m14_sb[:, 0:1]:
                      e.activation(o, i, AF.Copy, accum_out=a_),
                      [evr, st4["c4_used"]], inc="ae")
        evm = p.op("vector", lambda e, o=m4_sb[:, :], i=m14_sb[:, :]:
                   e.tensor_scalar(o, i, -1.0 / D, None, ALU.mult),
                   [evmean], inc="ve")
        evc = p.op("scalar", lambda e, o=c4_sb[:, :], i=r4_sb[:, :],
                   m=m4_sb[:, 0:1]:
                   e.activation(o, i, AF.Identity, bias=m),
                   [evm], inc="ae")
        evsq = p.op("scalar", lambda e, o=r4_sb[:, :], i=c4_sb[:, :],
                    a_=v14_sb[:, 0:1]:
                    e.activation(o, i, AF.Square, accum_out=a_),
                    [evc], inc="ae")
        st4["r4_used"] = evsq
        evve4 = p.op("vector", lambda e, o=v14_sb[:, :]:
                     e.tensor_scalar(o, o, 1.0 / D, LNEPS, ALU.mult, ALU.add),
                     [evsq], inc="ve")
        evln = p.op("scalar", lambda e, o=rstd4_sb[:, :], i=v14_sb[:, :]:
                    e.activation(o, i, AF.Ln), [evve4], inc="ae")
        evex = p.op("scalar", lambda e, o=rstd4_sb[:, :], i=rstd4_sb[:, :]:
                    e.activation(o, i, AF.Exp, scale=-0.5), [evln], inc="ae")
        evout = p.op("scalar", lambda e, o=outst[:, :], i=c4_sb[:, :],
                     m=rstd4_sb[:, 0:1]:
                     e.activation(o, i, AF.Identity, scale=m),
                     [evex, ev_out_store[0]], inc="ae")
        st4["c4_used"] = evout
        ev_out_store[0] = p.dma("sync", out[tloc:tloc + 128, :],
                                outst[:, :], [evout])

    for tt in range(16):
        tp = tt % 3
        ap_ = tt % 2
        evq, afree = _emit_round(
            p, y1l[tp][:, :], aq_tm[ap_][:, :],
            sclA[:, tt * 64:(tt + 1) * 64], rsclA[:, tt * 64:(tt + 1) * 64],
            ascr, dict(afree, out=ev_aqtm_used[ap_]),
            [ev_y1l[tp]] + list(ev_ladA), pool=True)
        if tt >= 1:
            tail_a(tt - 1)
        evt = None
        for kq in range(4):
            tb = kq % 2
            for kk in range(4):
                kc = kq * 4 + kk
                evt = p.op(
                    "tensor",
                    lambda e, o=pbbf(4 + tb)[:, kk * 128:(kk + 1) * 128],
                    i=aq_tm[ap_][:, kc * 128:(kc + 1) * 128],
                    idn=eyeb_sb[:, :]: e.transpose(o, i, idn),
                    [evq, ev_eye, ev_afm_evac[tb]],
                    inc="pe" if kk == 3 else None)
            ev_afm_evac[tb] = p.op(
                "scalar",
                lambda e, o=afm_sb[ap_][:, kq * 4:(kq + 1) * 4, :]
                .rearrange("p a b -> p (a b)"),
                i=pbbf(4 + tb)[:, :]: e.activation(o, i, AF.Copy),
                [evt, ev_afm_used[ap_] if kq < 2 else None], inc="ae")
        ev_aqtm_used[ap_] = evt
        afree["out"] = evt
        ev_afm = [ev_afm_evac[0], ev_afm_evac[1]]
        evmm = [None] * 4
        for kc in range(NCH):
            for ng in range(4):
                w0 = ([ev_wft_r[2 * ng], ev_wft_r[2 * ng + 1],
                       bank_free4[ng]] if kc == 0 else [])
                evmm[ng] = p.op(
                    "tensor",
                    lambda e, o=pb[ng][:, :], l=afm_sb[ap_][:, kc, :],
                    r=wft_sb[:, kc, ng * 512:(ng + 1) * 512],
                    st=(kc == 0), sp=(kc == NCH - 1):
                    e.matmul(o, lhsT=l, rhs=r, start=st, stop=sp),
                    ev_afm + w0, inc="pe" if kc == NCH - 1 else None)
        ev_afm_used[ap_] = evmm[3]
        st4["evmm"] = evmm
        if tt >= 1:
            tail_b(tt - 1)
            if tt + 2 < 16:
                ev_y1l[(tt + 2) % 3] = p.dma(
                    "scalar", y1l[(tt + 2) % 3][:, :],
                    Y1s.ap()[(tt + 2) * 128:(tt + 2) * 128 + 128, :],
                    [ev_y1_store[tt % 2], ev_y1l_used[(tt + 2) % 3]])
    tail_a(15)
    tail_b(15)

    dbg = []
    if debug:
        for name, th, shape, dt in (
                ("dbg_qs", Qs, [8, 128, T], BF16),
                ("dbg_ks", Ks, [8, 128, T], BF16),
                ("dbg_vs", Vs, [8, T, 128], BF16),
                ("dbg_ysin", Ysin, [1024, T], BF16),
                ("dbg_poha", POhA, [1024, D], BF16),
                ("dbg_pohb", POhB, [1024, D], BF16),
                ("dbg_pos", POs, [T, D], BF16),
                ("dbg_wfg", Wfg, [8 * NCH * 128, 256], BF16),
                ("dbg_y1", Y1s, [TPC, D], F32)):
            dap = nc.dram_tensor(name, shape, dt, kind="ExternalOutput").ap()
            p.dma("sync", dap, th.ap(), p.snap())
            dbg.append(name)

    fin = p.snap()
    p.op("sync", lambda e: e.nop(), fin)

    with ExitStack() as ectx:
        sems = {name: ectx.enter_context(nc.semaphore(name))
                for name in ("qd", "ad", "gd", "pe", "ve", "ae", "po", "cc")}
        block = ectx.enter_context(nc.Block())

        def emitter(engname):
            def run(eng):
                lastw = {}
                for (en, w, fn, inc, n) in p.ops:
                    if en != engname:
                        continue
                    for s, v in w.items():
                        if lastw.get(s, -1) >= v:
                            continue
                        eng.wait_ge(sems[s], v)
                        lastw[s] = v
                    inst = fn(eng)
                    if inc is not None:
                        inst.then_inc(sems[inc], n)
            return run

        block.sync(emitter("sync"))
        block.gpsimd(emitter("gpsimd"))
        block.tensor(emitter("tensor"))
        block.vector(emitter("vector"))
        block.scalar(emitter("scalar"))

    s4.close()
    ctx.close()
    return nc, dbg


def _host_inputs(inputs, c):
    b, hh = c >> 1, c & 1
    heads = SET0 if hh == 0 else SET1
    cols = np.concatenate([np.arange(h * 128, (h + 1) * 128) for h in heads])
    f32 = np.float32
    xb = np.ascontiguousarray(np.asarray(inputs["L0_attn_x"][b], dtype=f32))
    d = {
        "x": xb,
        "xr": np.ascontiguousarray(xb[hh * TPC:(hh + 1) * TPC, :]),
        "wq": np.ascontiguousarray(np.asarray(inputs["L0_attn_Wq"], f32)[:, cols]),
        "wk": np.ascontiguousarray(np.asarray(inputs["L0_attn_Wk"], f32)[:, cols]),
        "wv": np.ascontiguousarray(np.asarray(inputs["L0_attn_Wv"], f32)[:, cols]),
        "bqr": np.ascontiguousarray(
            np.asarray(inputs["L0_attn_bq"], f32)[cols].reshape(8, 128).T),
        "bkr": np.ascontiguousarray(
            np.asarray(inputs["L0_attn_bk"], f32)[cols].reshape(8, 128).T),
        "bvr": np.ascontiguousarray(
            np.asarray(inputs["L0_attn_bv"], f32)[cols].reshape(1, 1024)),
        "wo": np.ascontiguousarray(np.asarray(inputs["L0_attn_Wo"], f32)[cols, :]),
        "bor": np.ascontiguousarray(
            np.asarray(inputs["L0_attn_bo"], f32).reshape(1, D)),
        "wf": np.ascontiguousarray(
            np.asarray(inputs["L0_ffn_weight"], f32)[c * 256:(c + 1) * 256, :]),
        "bfr": np.ascontiguousarray(
            np.asarray(inputs["L0_ffn_bias"], f32).reshape(1, D)),
        "eyeb": np.eye(128, dtype=ml_dtypes.bfloat16),
        "onesr": np.ones((1, 128), f32),
        "onesc": np.ones((128, 1), ml_dtypes.bfloat16),
        "alib": _alibi_table(hh),
        "maskt": _mask_table(),
    }
    return d


def kernel(**inputs):
    nc, _ = _build(debug=False)
    in_maps = [_host_inputs(inputs, c) for c in range(8)]
    res = run_bass_kernel_spmd(nc, in_maps, core_ids=list(range(8)))
    out = np.empty((B, T, D), np.float32)
    for c in range(8):
        b, hh = c >> 1, c & 1
        out[b, hh * TPC:(hh + 1) * TPC, :] = res.results[c]["out"]
    return out
